# version 05
# baseline (speedup 1.0000x reference)
import numpy as np
import ml_dtypes
import concourse.bass as bass
import concourse.mybir as mybir
from concourse.bass_utils import run_bass_kernel_spmd
from concourse.alu_op_type import AluOpType as OP
from contextlib import ExitStack

F32 = mybir.dt.float32
BF16 = mybir.dt.bfloat16
U8 = mybir.dt.uint8
I32 = mybir.dt.int32
AF = mybir.ActivationFunctionType
BF = ml_dtypes.bfloat16

DM = 1024
SEQ = 8192
DI = 2048
NST = 16
DTR = 64
H4 = 4096
H3 = 3072
TT = 512
NTILE = SEQ // TT
EPS_LN = 1e-5
EPS_L1 = 1e-8

FUSED = True

def _prep_shared(inp):
    s = {}
    for k in ["L0_attn_in_proj_w", "L0_attn_conv_w", "L0_attn_conv_b",
              "L0_attn_x_proj_w", "L0_attn_dt_proj_w", "L0_attn_dt_proj_b",
              "L0_attn_A_log", "L0_attn_D", "L0_attn_out_proj_w",
              "L0_attn_ln_g", "L0_attn_ln_b", "L0_attn_ff_w1", "L0_attn_ff_b1",
              "L0_attn_ff_w2", "L0_attn_ff_b2", "L0_ffn_mu_r", "L0_ffn_mu_k"]:
        s[k.split("L0_attn_")[-1] if k.startswith("L0_attn_") else k] = np.ascontiguousarray(inp[k])
    s["WrT"] = np.ascontiguousarray(inp["L0_ffn_W_r"].T)
    s["WkT"] = np.ascontiguousarray(inp["L0_ffn_W_k"].T)
    s["WvT"] = np.ascontiguousarray(inp["L0_ffn_W_v"].T)
    s["L1WT"] = np.ascontiguousarray(inp["L1_attn_W"].T)
    s["wgT"] = np.ascontiguousarray(inp["L1_ffn_w_gate"].T)
    s["wuT"] = np.ascontiguousarray(inp["L1_ffn_w_up"].T)
    s["wdT"] = np.ascontiguousarray(inp["L1_ffn_w_down"].T)
    for k in ["conv_b", "dt_proj_b", "D", "ln_b", "ff_b2", "ln_g", "ff_b1",
              "L0_ffn_mu_r", "L0_ffn_mu_k"]:
        s[k] = np.ascontiguousarray(s[k].reshape(-1, 128).T)
    s["conv_w"] = np.ascontiguousarray(s["conv_w"].reshape(16, 128, 4).transpose(1, 0, 2))
    s["A_log"] = np.ascontiguousarray(s["A_log"].reshape(16, 128, 16).transpose(1, 0, 2))
    s["oh16"] = np.repeat(np.eye(16, dtype=np.float32), 128, axis=1).astype(BF)
    s["id128"] = np.eye(128, dtype=np.float32).astype(BF)
    return s

SHARED_SPECS = {
    "in_proj_w": ((DM, 2 * DI), F32), "conv_w": ((128, 16, 4), F32), "conv_b": ((128, 16), F32),
    "x_proj_w": ((DI, 96), F32), "dt_proj_w": ((DTR, DI), F32), "dt_proj_b": ((128, 16), F32),
    "A_log": ((128, 16, 16), F32), "D": ((128, 16), F32), "out_proj_w": ((DI, DM), F32),
    "ln_g": ((128, 8), F32), "ln_b": ((128, 8), F32), "ff_w1": ((DM, H4), F32), "ff_b1": ((128, 32), F32),
    "ff_w2": ((H4, DM), F32), "ff_b2": ((128, 8), F32), "L0_ffn_mu_r": ((128, 8), F32),
    "L0_ffn_mu_k": ((128, 8), F32), "WrT": ((DM, DM), F32), "WkT": ((DM, DM), F32),
    "WvT": ((DM, DM), F32), "L1WT": ((DM, H3), F32), "wgT": ((DM, H3), I32),
    "wuT": ((DM, H3), I32), "wdT": ((H3, DM), I32),
    "oh16": ((16, 2048), BF16), "id128": ((128, 128), BF16),
}

SCRATCH_SPECS = {
    "xc": ((DI, SEQ), F32), "z": ((DI, SEQ), F32),
    "xcs_f": ((DI, SEQ), F32), "xcs_b": ((DI, SEQ), F32),
    "delta_f": ((DI, SEQ), F32), "delta_b": ((DI, SEQ), F32),
    "u_f": ((DI, SEQ), BF16), "u_b": ((DI, SEQ), BF16),
    "Bt_f": ((16, SEQ), BF16), "Bt_b": ((16, SEQ), BF16),
    "Ct_f": ((16, SEQ), BF16), "Ct_b": ((16, SEQ), BF16),
    "ys_f": ((DI, SEQ), BF16), "ys_b": ((DI, SEQ), BF16),
    "mf": ((DM, SEQ), F32), "x1": ((DM, SEQ), F32), "x2": ((DM, SEQ), F32),
    "lf": ((DM, SEQ), F32), "lv": ((DM, SEQ), F32), "ht": ((DM, SEQ), F32),
    "mo": ((DM, SEQ), F32), "g1": ((H4, SEQ), BF16), "w3": ((DM, SEQ), BF16),
    "x3": ((DM, SEQ), F32), "hh": ((H3, SEQ), BF16),
}

PHASE_IO = {
    "p1":  (["in_proj_w"], ["x"], ["xc", "z"]),
    "p2f": (["conv_w", "conv_b", "x_proj_w", "dt_proj_w", "dt_proj_b"], ["xc"],
            ["xcs_f", "delta_f", "u_f", "Bt_f", "Ct_f"]),
    "p2b": (["conv_w", "conv_b", "x_proj_w", "dt_proj_w", "dt_proj_b"], ["xc"],
            ["xcs_b", "delta_b", "u_b", "Bt_b", "Ct_b"]),
    "p3f": (["A_log", "oh16", "id128"], ["delta_f", "u_f", "Bt_f", "Ct_f"], ["ys_f"]),
    "p3b": (["A_log", "oh16", "id128"], ["delta_b", "u_b", "Bt_b", "Ct_b"], ["ys_b"]),
    "p4f": (["out_proj_w", "D"], ["ys_f", "xcs_f", "z"], ["mf"]),
    "p4b": (["out_proj_w", "D"], ["ys_b", "xcs_b", "z", "mf"], ["mo"]),
    "p45a": (["ff_w1", "ff_b1", "ln_g", "ln_b"], ["mo"], ["g1"]),
    "p45b": (["ff_w2", "ff_b2"], ["g1", "x"], ["x1"]),
    "p5": (["WrT", "WkT", "WvT", "L0_ffn_mu_r", "L0_ffn_mu_k"], ["x1"], ["x2"]),
    "p6": (["L1WT"], ["x2"], ["lf", "lv"]),
    "p7": ([], ["lf", "lv"], ["ht"]),
    "p8a": ([], ["x2", "ht"], ["x3", "w3"]),
    "p8b1": (["wgT", "wuT"], ["w3"], ["hh"]),
    "p8b2": (["wdT"], ["hh", "x3"], ["y"]),
}

NP_DT = {F32: np.float32, BF16: BF, I32: np.int32, U8: np.uint8}


class Ctx:
    def __init__(self, nc, block, sems, io):
        self.nc = nc
        self.block = block
        self.sems = sems
        self.cnt = {k: 0 for k in sems}
        self.io = io
        self.sb = {}

    def inc(self, instr, sem, n=1):
        instr.then_inc(self.sems[sem], n)
        self.cnt[sem] += n

    def wait(self, eng, *sems):
        for s in sems:
            if self.cnt[s] > 0:
                eng.wait_ge(self.sems[s], self.cnt[s])

    def wait_val(self, eng, sem, val):
        if val > 0:
            eng.wait_ge(self.sems[sem], val)

    def mark(self, sem):
        return self.cnt[sem]

    def barrier(self):
        cnts = dict(self.cnt)
        for eng_name in ["sync", "vector", "scalar", "tensor", "gpsimd"]:
            def bar(eng, cnts=cnts):
                for s, c in cnts.items():
                    if c > 0:
                        eng.wait_ge(self.sems[s], c)
            getattr(self.block, eng_name)(bar)


SEM_NAMES = ["d", "o", "v", "a", "p", "g"]


def load_weights_bf16(ctx, dst, src_ap, K, M, stg, name, mcap=4096):
    nc = ctx.nc
    nk = K // 128
    for ki in range(nk):
        for c0 in range(0, M, mcap):
            mc = min(mcap, M - c0)
            def dma(sync, ki=ki, c0=c0, mc=mc):
                ctx.wait(sync, "a")
                ctx.inc(sync.dma_start(out=stg[:, :mc], in_=src_ap[ki * 128:(ki + 1) * 128, c0:c0 + mc]), "d", 16)
            ctx.block.sync(dma)
            def cast(sc, ki=ki, c0=c0, mc=mc):
                ctx.wait(sc, "d")
                ctx.inc(sc.copy(out=dst[:, ki, c0:c0 + mc], in_=stg[:, :mc]), "a")
            ctx.block.scalar(cast)


def emit_ln_stats(ctx, xa_view, nch, T, xbf_view, sq, psA, psB, st, ones_bf, ones_f32, eps_ap):
    nc = ctx.nc
    def cast(v):
        ctx.wait(v, "d", "p", "a")
        ctx.inc(v.tensor_copy(xbf_view[:], xa_view[:]), "v")
    ctx.block.vector(cast)
    def square(sc):
        ctx.wait(sc, "d", "p", "v", "o")
        ctx.inc(sc.activation(out=sq[:], in_=xa_view[:], func=AF.Square), "a")
    ctx.block.scalar(square)
    def mms(t):
        ctx.wait(t, "v", "a")
        for c in range(nch):
            ins = t.matmul(psA[:], ones_bf[:, 0:1], xbf_view[:, c, :], start=(c == 0), stop=(c == nch - 1))
        ctx.inc(ins, "p")
        for c in range(nch):
            ins = t.matmul(psB[:], ones_f32[:, 0:1], sq[:, c, :], start=(c == 0), stop=(c == nch - 1))
        ctx.inc(ins, "p")
    ctx.block.tensor(mms)
    inv = 1.0 / (nch * 128)
    mu, ms, mu2, var, sd = (st[:, 2 * T:3 * T], st[:, 3 * T:4 * T], st[:, 4 * T:5 * T],
                            st[:, 5 * T:6 * T], st[:, 6 * T:7 * T])
    def stats_a(sc):
        ctx.wait(sc, "p")
        sc.activation(out=mu, in_=psA[:], func=AF.Copy, scale=inv)
        ctx.inc(sc.activation(out=ms, in_=psB[:], func=AF.Copy, scale=inv), "a")
    ctx.block.scalar(stats_a)
    def stats_v(v):
        ctx.wait(v, "a")
        v.tensor_tensor(mu2, mu, mu, OP.mult)
        ctx.inc(v.tensor_tensor(var, ms, mu2, OP.subtract), "v")
    ctx.block.vector(stats_v)
    rstd, m2 = st[:, 0:T], st[:, T:2 * T]
    def stats_a2(sc):
        ctx.wait(sc, "v")
        sc.activation(out=sd, in_=var, func=AF.Ln, bias=eps_ap)
        ctx.inc(sc.activation(out=rstd, in_=sd, func=AF.Exp, scale=-0.5), "a")
    ctx.block.scalar(stats_a2)
    def stats_v2(v):
        ctx.wait(v, "a")
        ctx.inc(v.tensor_tensor(m2, mu, rstd, OP.mult), "v")
    ctx.block.vector(stats_v2)
    return rstd, m2


def emit_ln_normalize(ctx, xa_view, nch, T, out_view, rstd, m2, psR, psM, ones1_f32, tmp,
                      gain=None, bias_=None):
    def bcast(t):
        ctx.wait(t, "v")
        t.matmul(psR[:], ones1_f32[:, :], rstd, start=True, stop=True)
        ins = t.matmul(psM[:], ones1_f32[:, :], m2, start=True, stop=True)
        ctx.inc(ins, "p")
    ctx.block.tensor(bcast)
    def norm(v):
        ctx.wait(v, "p")
        for c in range(nch):
            v.tensor_tensor(tmp[:], xa_view[:, c, :], psR[:], OP.mult)
            if gain is None:
                ins = v.tensor_tensor(out_view[:, c, :], tmp[:], psM[:], OP.subtract)
            else:
                v.tensor_tensor(tmp[:], tmp[:], psM[:], OP.subtract)
                ins = v.tensor_scalar(out_view[:, c, :], tmp[:], gain[c], bias_[c], OP.mult, OP.add)
        ctx.inc(ins, "v")
    ctx.block.vector(norm)


def emit_p1(ctx):
    nc = ctx.nc
    x = ctx.io["x"]
    xcd, zd = ctx.io["xc"], ctx.io["z"]
    wsrc = ctx.io["in_proj_w"]
    with ExitStack() as es:
        def sb(name, shape, dt):
            return es.enter_context(nc.sbuf_tensor(name, shape, dt))
        def psum(name, shape):
            return es.enter_context(nc.psum_tensor(name, shape, F32))
        xa = [sb("p1_xa%d" % i, [128, 8, TT], F32) for i in range(2)]
        sq = [sb("p1_sq%d" % i, [128, 8, TT], F32) for i in range(2)]
        xbf = [sb("p1_xbf%d" % i, [128, 8, TT], BF16) for i in range(2)]
        xln = [sb("p1_xln%d" % i, [128, 8, TT], BF16) for i in range(2)]
        st = sb("p1_st", [1, 7 * TT], F32)
        tmp = sb("p1_tmp", [128, TT], F32)
        ones_bf = sb("p1_onesb", [128, 1], BF16)
        ones_f32 = sb("p1_onesf", [128, 1], F32)
        ones1 = sb("p1_ones1", [1, 128], F32)
        cns = sb("p1_cns", [128, 4], F32)
        wip = sb("p1_w", [128, 8, 32 * 128], BF16)
        stg = sb("p1_stg", [128, 2048], F32)
        ev = sb("p1_ev", [128, 4, TT], F32)
        psA = psum("p1_psA", [1, TT])
        psB = psum("p1_psB", [1, TT])
        psR = psum("p1_psR", [128, TT])
        psM = psum("p1_psM", [128, TT])
        psK = [psum("p1_ps%d" % i, [128, TT]) for i in range(4)]
        def init(v):
            v.memset(ones_bf[:], 1.0)
            v.memset(ones_f32[:], 1.0)
            v.memset(cns[:, 0:1], EPS_LN)
            ctx.inc(v.memset(ones1[:], 1.0), "v")
        ctx.block.vector(init)
        load_weights_bf16(ctx, wip.ap(), wsrc, DM, 4096, stg, "wip", mcap=2048)
        wip_v = wip.ap().rearrange("p k (f j) -> p k f j", j=128)
        tile_vmark = {}
        tile_amark = {}
        evk, dmk, mmk = {}, {}, {}
        for it in range(NTILE):
            t0 = it * TT
            tb = it % 2
            def dma_in(sync, t0=t0, it=it, tb=tb):
                if it >= 2:
                    ctx.wait_val(sync, "v", tile_vmark[it - 2])
                    ctx.wait_val(sync, "a", tile_amark[it - 2])
                ctx.inc(sync.dma_start(
                    out=xa[tb][:], in_=x.rearrange("(c p) t -> p c t", p=128)[:, :, t0:t0 + TT]), "d", 16)
            ctx.block.sync(dma_in)
            rstd, m2 = emit_ln_stats(ctx, xa[tb].ap(), 8, TT, xbf[tb].ap(), sq[tb].ap(), psA, psB, st.ap(),
                                     ones_bf, ones_f32, cns[0:1, 0:1])
            emit_ln_normalize(ctx, xa[tb].ap(), 8, TT, xln[tb].ap(), rstd, m2, psR, psM, ones1, tmp.ap())
            tile_vmark[it] = ctx.mark("v")
            tile_amark[it] = ctx.mark("a")
            nvm = tile_vmark[it]
            for fo in range(32):
                j = fo % 4
                gfo = it * 32 + fo
                def mm(t, fo=fo, j=j, gfo=gfo, nvm=nvm, tb=tb):
                    ctx.wait_val(t, "v", nvm)
                    if (gfo - 4) in evk:
                        ctx.wait_val(t, "a", evk[gfo - 4])
                    for ki in range(8):
                        ins = t.matmul(psK[j][:], wip_v[:, ki, fo, :], xln[tb][:, ki, :],
                                       start=(ki == 0), stop=(ki == 7))
                    ctx.inc(ins, "p")
                    mmk[gfo] = ctx.mark("p")
                ctx.block.tensor(mm)
                def evac(sc, fo=fo, j=j, gfo=gfo):
                    ctx.wait_val(sc, "p", mmk[gfo])
                    if (gfo - 4) in dmk:
                        ctx.wait_val(sc, "o", dmk[gfo - 4])
                    ctx.inc(sc.copy(out=ev[:, fo % 4, :], in_=psK[j][:]), "a")
                    evk[gfo] = ctx.mark("a")
                ctx.block.scalar(evac)
                def dma_out(g, fo=fo, t0=t0, gfo=gfo):
                    ctx.wait_val(g, "a", evk[gfo])
                    dst = xcd if fo < 16 else zd
                    r0 = (fo % 16) * 128
                    ctx.inc(g.dma_start(out=dst[r0:r0 + 128, t0:t0 + TT], in_=ev[:, fo % 4, :]), "o", 16)
                    dmk[gfo] = ctx.mark("o")
                ctx.block.gpsimd(dma_out)
        def fin(g):
            ctx.wait(g, "o")
            g.nop()
        ctx.block.gpsimd(fin)


def emit_p2(ctx, dirn):
    nc = ctx.nc
    xcd = ctx.io["xc"]
    xcs_d, delta_d, u_d = ctx.io["xcs_" + dirn], ctx.io["delta_" + dirn], ctx.io["u_" + dirn]
    Bt_d, Ct_d = ctx.io["Bt_" + dirn], ctx.io["Ct_" + dirn]
    with ExitStack() as es:
        def sb(name, shape, dt):
            return es.enter_context(nc.sbuf_tensor("p2%s_%s" % (dirn, name), shape, dt))
        def psum(name, shape):
            return es.enter_context(nc.psum_tensor("p2%s_%s" % (dirn, name), shape, F32))
        xe = [sb("xe%d" % i, [128, 16, TT + 3], F32) for i in range(2)]
        pb = [sb("pb%d" % i, [128, 4, TT], F32) for i in range(2)]
        xcs = sb("xcs", [128, 16, TT], F32)
        xcsb = sb("xcsb", [128, 16, TT], BF16)
        acc = sb("acc", [128, TT], F32)
        acc2 = sb("acc2", [128, TT], F32)
        accR = [sb("accR%d" % i, [128, TT], F32) for i in range(2)]
        delta = sb("delta", [128, 16, TT], F32)
        ub = sb("ub", [128, 16, TT], BF16)
        dtb = sb("dtb", [64, TT], BF16)
        bct = sb("bct", [16, 2, TT], BF16)
        cw = sb("cw", [128, 16, 4], F32)
        cb = sb("cb", [128, 16], F32)
        dpb = sb("dpb", [128, 16], F32)
        onep = sb("onep", [128, 1], F32)
        xpw = sb("xpw", [128, 16, 96], BF16)
        dtw = sb("dtw", [64, 16, 128], BF16)
        stg = sb("stg", [128, 2048], F32)
        psDT = psum("psDT", [64, TT])
        psB16 = psum("psB16", [16, TT])
        psC16 = psum("psC16", [16, TT])
        psK = [psum("ps%d" % i, [128, TT]) for i in range(4)]
        def dma_par(sync):
            ctx.inc(sync.dma_start(out=cw[:], in_=ctx.io["conv_w"]), "d", 16)
            ctx.inc(sync.dma_start(out=cb[:], in_=ctx.io["conv_b"]), "d", 16)
            ctx.inc(sync.dma_start(out=dpb[:], in_=ctx.io["dt_proj_b"]), "d", 16)
        ctx.block.sync(dma_par)
        def init_v(v):
            ctx.inc(v.memset(onep[:], 1.0), "v")
        ctx.block.vector(init_v)
        load_weights_bf16(ctx, xpw.ap(), ctx.io["x_proj_w"], DI, 96, stg, "xpw")
        def dma_dtw(sync):
            ctx.wait(sync, "a")
            ctx.inc(sync.dma_start(out=stg[0:64, :2048], in_=ctx.io["dt_proj_w"]), "d", 16)
        ctx.block.sync(dma_dtw)
        def cast_dtw(sc):
            ctx.wait(sc, "d")
            ctx.inc(sc.copy(out=dtw.ap().rearrange("p a b -> p (a b)"), in_=stg[0:64, :2048]), "a")
        ctx.block.scalar(cast_dtw)

        dmark = {}
        silu_mark = {}
        silu_cmark = {}
        conv_mark = {}
        cast_mark = {}
        mmxp_mark = {}
        evxp_mark = {}
        bc_mark = {}
        u_mark = {}
        st_mark = {}
        sp_mark = {}
        for it in range(NTILE):
            t0 = it * TT
            tb = it % 2
            if dirn == "f":
                lo, hi = t0 - 3, t0 + TT
                pad_lo = max(0, -lo)
                pad_hi = 0
                shifts = [0, 1, 2, 3]
            else:
                lo, hi = t0, t0 + TT + 3
                pad_lo = 0
                pad_hi = max(0, hi - SEQ)
                shifts = [3, 2, 1, 0]
            def dma_in(sync, it=it, tb=tb, lo=lo, hi=hi, pad_lo=pad_lo, pad_hi=pad_hi):
                if (it - 2) in sp_mark:
                    ctx.wait_val(sync, "v", sp_mark[it - 2])
                if dirn == "f":
                    ctx.inc(sync.dma_start(
                        out=xe[tb][:, :, pad_lo:TT + 3],
                        in_=xcd.rearrange("(c p) t -> p c t", p=128)[:, :, lo + pad_lo:hi]), "d", 16)
                else:
                    ctx.inc(sync.dma_start(
                        out=xe[tb][:, :, 0:TT + 3 - pad_hi],
                        in_=xcd.rearrange("(c p) t -> p c t", p=128)[:, :, lo:hi - pad_hi]), "d", 16)
                dmark[it] = ctx.mark("d")
            ctx.block.sync(dma_in)
            if pad_lo or pad_hi:
                def pad(g, it=it, tb=tb, pad_lo=pad_lo, pad_hi=pad_hi):
                    ctx.wait(g, "v", "a", "p")
                    if dirn == "f":
                        ctx.inc(g.memset(xe[tb][:, :, 0:pad_lo], 0.0), "g")
                    else:
                        ctx.inc(g.memset(xe[tb][:, :, TT + 3 - pad_hi:TT + 3], 0.0), "g")
                ctx.block.gpsimd(pad)
            for c in range(16):
                def conv(v, c=c, it=it, tb=tb, shifts=shifts):
                    ctx.wait_val(v, "d", dmark[it])
                    ctx.wait(v, "g")
                    key = (it, c - 2) if c >= 2 else (it - 1, 14 + c)
                    if key in silu_cmark:
                        ctx.wait_val(v, "a", silu_cmark[key])
                    if c == 0 and (it - 1) in u_mark:
                        ctx.wait_val(v, "v", u_mark[it - 1])
                    if c == 0 and (it - 1) in st_mark:
                        ctx.wait_val(v, "o", st_mark[it - 1])
                    s = shifts
                    for k in range(4):
                        v.tensor_scalar(pb[tb][:, k, :], xe[tb][:, c, s[k]:s[k] + TT], cw[:, c, k:k + 1], None, OP.mult)
                    v.tensor_tensor(acc[:], pb[tb][:, 0, :], pb[tb][:, 1, :], OP.add)
                    v.tensor_tensor(acc2[:], pb[tb][:, 2, :], pb[tb][:, 3, :], OP.add)
                    ctx.inc(v.tensor_tensor(accR[c % 2][:], acc[:], acc2[:], OP.add), "v")
                    conv_mark[(it, c)] = ctx.mark("v")
                ctx.block.vector(conv)
                def silu(sc, c=c, it=it, tb=tb):
                    ctx.wait_val(sc, "v", conv_mark[(it, c)])
                    ctx.inc(sc.activation(out=xcs[:, c, :], in_=accR[c % 2][:], func=AF.Silu, bias=cb[:, c:c + 1]), "a")
                    silu_cmark[(it, c)] = ctx.mark("a")
                    if c == 15:
                        silu_mark[it] = ctx.mark("a")
                ctx.block.scalar(silu)
            def cast_xcs(v, it=it):
                ctx.wait_val(v, "a", silu_mark[it])
                if (it - 1) in mmxp_mark:
                    ctx.wait_val(v, "p", mmxp_mark[it - 1])
                ctx.inc(v.tensor_copy(xcsb[:], xcs[:]), "v")
                cast_mark[it] = ctx.mark("v")
            ctx.block.vector(cast_xcs)
            def mm_xp(t, it=it):
                ctx.wait_val(t, "v", cast_mark[it])
                if (it - 1) in evxp_mark:
                    ctx.wait_val(t, "a", evxp_mark[it - 1])
                for ki in range(16):
                    ins = t.matmul(psDT[:], xpw[:, ki, 0:64], xcsb[:, ki, :], start=(ki == 0), stop=(ki == 15))
                for ki in range(16):
                    ins2 = t.matmul(psB16[:], xpw[:, ki, 64:80], xcsb[:, ki, :], start=(ki == 0), stop=(ki == 15))
                for ki in range(16):
                    ins3 = t.matmul(psC16[:], xpw[:, ki, 80:96], xcsb[:, ki, :], start=(ki == 0), stop=(ki == 15))
                ctx.inc(ins3, "p")
                mmxp_mark[it] = ctx.mark("p")
            ctx.block.tensor(mm_xp)
            def evac_xp(sc, it=it):
                ctx.wait_val(sc, "p", mmxp_mark[it])
                if (it - 1) in bc_mark:
                    ctx.wait_val(sc, "o", bc_mark[it - 1])
                sc.copy(out=dtb[:], in_=psDT[:])
                sc.copy(out=bct[:, 0, :], in_=psB16[:])
                ctx.inc(sc.copy(out=bct[:, 1, :], in_=psC16[:]), "a")
                evxp_mark[it] = ctx.mark("a")
            ctx.block.scalar(evac_xp)
            def dma_bc(g, t0=t0, it=it):
                ctx.wait_val(g, "a", evxp_mark[it])
                ctx.inc(g.dma_start(out=Bt_d[:, t0:t0 + TT], in_=bct[:, 0, :]), "o", 16)
                ctx.inc(g.dma_start(out=Ct_d[:, t0:t0 + TT], in_=bct[:, 1, :]), "o", 16)
                bc_mark[it] = ctx.mark("o")
            ctx.block.gpsimd(dma_bc)
            mmk2, evk2 = {}, {}
            for mo in range(16):
                j = mo % 4
                def mm_dt(t, mo=mo, j=j, it=it):
                    ctx.wait_val(t, "a", evxp_mark[it] if mo < 4 else max(evxp_mark[it], evk2[mo - 4]))
                    if mo == 0 and (it - 1) in sp_mark:
                        ctx.wait_val(t, "v", sp_mark[it - 1])
                    ctx.inc(t.matmul(psK[j][:], dtw[:, mo, :], dtb[:], start=True, stop=True), "p")
                    mmk2[mo] = ctx.mark("p")
                ctx.block.tensor(mm_dt)
                def dt_ev(sc, mo=mo, j=j, it=it):
                    ctx.wait_val(sc, "p", mmk2[mo])
                    if mo == 0 and (it - 1) in st_mark:
                        ctx.wait_val(sc, "o", st_mark[it - 1])
                    if mo == 0 and (it - 1) in u_mark:
                        ctx.wait_val(sc, "v", u_mark[it - 1])
                    ctx.inc(sc.activation(out=delta[:, mo, :], in_=psK[j][:], func=AF.Identity,
                                          bias=dpb[:, mo:mo + 1]), "a")
                    evk2[mo] = ctx.mark("a")
                ctx.block.scalar(dt_ev)
            for hh in range(2):
                if hh == 0:
                    A_, B_ = xe[tb][:, 0:8, 0:TT], xe[tb][:, 8:16, 0:TT]
                else:
                    A_, B_ = xe[tb][:, 8:16, 0:TT], xe[tb][:, 0:8, 0:TT]
                dlt = delta[:, hh * 8:hh * 8 + 8, :]
                def soft_a(sc, A_=A_, B_=B_, dlt=dlt, it=it, hh=hh):
                    ctx.wait_val(sc, "a", evk2[15])
                    ctx.wait_val(sc, "v", conv_mark[(it, 15)])
                    sc.activation(out=A_, in_=dlt, func=AF.Abs)
                    sc.activation(out=B_, in_=A_, func=AF.Exp, scale=-1.0)
                    sc.activation(out=A_, in_=B_, func=AF.Ln, bias=onep[:, 0:1])
                    ctx.inc(sc.activation(out=B_, in_=dlt, func=AF.Relu), "a")
                ctx.block.scalar(soft_a)
                def soft_v(v, A_=A_, B_=B_, dlt=dlt, it=it, hh=hh):
                    ctx.wait(v, "a")
                    ctx.inc(v.tensor_tensor(dlt, A_, B_, OP.add), "v")
                    if hh == 1:
                        sp_mark[it] = ctx.mark("v")
                ctx.block.vector(soft_v)
            def mk_u(v, it=it):
                ctx.wait_val(v, "v", sp_mark[it])
                ctx.inc(v.tensor_tensor(ub[:], delta[:], xcs[:], OP.mult), "v")
                u_mark[it] = ctx.mark("v")
            ctx.block.vector(mk_u)
            def dma_out(sc, t0=t0, it=it):
                ctx.wait_val(sc, "v", u_mark[it])
                ctx.inc(sc.dma_start(out=xcs_d.rearrange("(c p) t -> p c t", p=128)[:, :, t0:t0 + TT], in_=xcs[:]), "o", 16)
                ctx.inc(sc.dma_start(out=delta_d.rearrange("(c p) t -> p c t", p=128)[:, :, t0:t0 + TT], in_=delta[:]), "o", 16)
                ctx.inc(sc.dma_start(out=u_d.rearrange("(c p) t -> p c t", p=128)[:, :, t0:t0 + TT], in_=ub[:]), "o", 16)
                st_mark[it] = ctx.mark("o")
            ctx.block.scalar(dma_out)
        def fin(g):
            ctx.wait(g, "o")
            g.nop()
        ctx.block.gpsimd(fin)


TB = 1024
NTB = SEQ // TB

def emit_p3(ctx, dirn):
    nc = ctx.nc
    delta_d, u_d = ctx.io["delta_" + dirn], ctx.io["u_" + dirn]
    Bt_d, Ct_d, ys_d = ctx.io["Bt_" + dirn], ctx.io["Ct_" + dirn], ctx.io["ys_" + dirn]
    rev = (dirn == "b")
    with ExitStack() as es:
        def sb(name, shape, dt):
            return es.enter_context(nc.sbuf_tensor("p3%s_%s" % (dirn, name), shape, dt))
        def psum(name, shape):
            return es.enter_context(nc.psum_tensor("p3%s_%s" % (dirn, name), shape, F32))
        dall = [sb("dall%d" % i, [128, 8, TB], F32) for i in range(2)]
        uall = [sb("uall%d" % i, [128, 8, TB], BF16) for i in range(2)]
        Bbc = sb("Bbc", [128, 16, TB], BF16)
        Cbc = sb("Cbc", [128, 16, TB], BF16)
        at = [sb("at%d" % i, [128, TB], BF16) for i in range(8)]
        bt0 = [sb("bt0%d" % i, [128, TB], BF16) for i in range(2)]
        ht = [sb("ht%d" % i, [128, TB], BF16) for i in range(2)]
        hct = [sb("hct%d" % i, [128, TB], BF16) for i in range(4)]
        ysb = [sb("ysb%d" % i, [128, TB], BF16) for i in range(2)]
        carry = sb("carry", [128, 256], F32)
        zcol = sb("zcol", [128, 1], F32)
        asc = sb("asc", [128, 16, 16], F32)
        ascr = sb("ascr", [128, 16, 16], F32)
        bc16 = sb("bc16", [16, 2, TB], BF16)
        oh = sb("oh", [16, 2048], BF16)
        idt = sb("idt", [128, 128], BF16)
        psBC = [psum("psBC%d" % i, [128, TB]) for i in range(2)]
        psY = [psum("psY%d" % i, [128, TB]) for i in range(2)]
        def dma_par(sync):
            ctx.inc(sync.dma_start(out=ascr[:], in_=ctx.io["A_log"]), "d", 16)
            ctx.inc(sync.dma_start(out=oh[:], in_=ctx.io["oh16"]), "d", 16)
            ctx.inc(sync.dma_start(out=idt[:], in_=ctx.io["id128"]), "d", 16)
        ctx.block.sync(dma_par)
        ascE = at[0].ap()[:, 0:256].rearrange("p (a b) -> p a b", b=16)
        def init_a(sc):
            ctx.wait(sc, "d")
            ctx.inc(sc.activation(out=ascE, in_=ascr[:], func=AF.Exp), "a")
        ctx.block.scalar(init_a)
        def init_v(v):
            ctx.wait(v, "a")
            v.tensor_scalar(asc[:], ascE, -1.0, None, OP.mult)
            ctx.inc(v.memset(zcol[:], 0.0), "v")
        ctx.block.vector(init_v)
        init_vmark = ctx.mark("v")

        tbs = list(range(NTB))
        if rev:
            tbs = tbs[::-1]
        pending_yev = []
        carry_q = []
        half_amark = {}
        half_gmark = {}
        half_vmark = {}
        bc_pmark = {}
        bc_amark = 0
        ev_mark = {}
        dmao_mark = {}
        exp_mark = {}
        b_mark = {}
        bsem = {}
        scan_mark = {}
        y_mark = {}
        bcev_am = {}
        for tbi, tb in enumerate(tbs):
            s0 = tb * TB
            def dma_bc(sync, s0=s0, tbi=tbi):
                if tbi > 0:
                    ctx.wait_val(sync, "p", bc_pmark[tbi - 1])
                ctx.inc(sync.dma_start(out=bc16[:, 0, :], in_=Bt_d[:, s0:s0 + TB]), "d", 16)
                ctx.inc(sync.dma_start(out=bc16[:, 1, :], in_=Ct_d[:, s0:s0 + TB]), "d", 16)
            ctx.block.sync(dma_bc)
            dmark = ctx.mark("d")
            for n in range(16):
                def bc_mm(t, n=n, tbi=tbi):
                    ctx.wait_val(t, "d", dmark)
                    if tbi > 0:
                        i_last = (tbi * 2 - 1) * 128 + 7 * 16 + n
                        if i_last in scan_mark:
                            ctx.wait_val(t, "v", scan_mark[i_last])
                    ctx.wait_val(t, "a", bcev_am.get(n - 1, bc_amark))
                    for seg in range(2):
                        t.matmul(psBC[0][:, seg * 512:(seg + 1) * 512], oh[:, n * 128:(n + 1) * 128],
                                 bc16[:, 0, seg * 512:(seg + 1) * 512], start=True, stop=True)
                        ins = t.matmul(psBC[1][:, seg * 512:(seg + 1) * 512], oh[:, n * 128:(n + 1) * 128],
                                       bc16[:, 1, seg * 512:(seg + 1) * 512], start=True, stop=True)
                    ctx.inc(ins, "p")
                ctx.block.tensor(bc_mm)
                pm = ctx.mark("p")
                def bc_ev(sc, n=n, pm=pm):
                    ctx.wait_val(sc, "p", pm)
                    sc.copy(out=Bbc[:, n, :], in_=psBC[0][:])
                    ctx.inc(sc.copy(out=Cbc[:, n, :], in_=psBC[1][:]), "a")
                ctx.block.scalar(bc_ev)
                bcev_am[n] = ctx.mark("a")
            bc_pmark[tbi] = ctx.mark("p")
            bc_amark = ctx.mark("a")
            stage2_a = ctx.mark("a")

            for half in range(2):
                q = tbi * 2 + half
                hb = q % 2
                def dma_du(sync, s0=s0, half=half, q=q, hb=hb):
                    if q == 0:
                        ctx.wait_val(sync, "v", init_vmark)
                    if q >= 2:
                        ctx.wait_val(sync, "a", half_amark[q - 2])
                        ctx.wait_val(sync, "g", half_gmark[q - 2])
                        ctx.wait_val(sync, "v", half_vmark[q - 2])
                    c0 = half * 8
                    ctx.inc(sync.dma_start(out=dall[hb][:], in_=delta_d.rearrange("(c p) t -> p c t", p=128)[:, c0:c0 + 8, s0:s0 + TB]), "d", 16)
                    ctx.inc(sync.dma_start(out=uall[hb][:], in_=u_d.rearrange("(c p) t -> p c t", p=128)[:, c0:c0 + 8, s0:s0 + TB]), "d", 16)
                ctx.block.sync(dma_du)
                dumark = ctx.mark("d")
                for dci in range(8):
                    dc = half * 8 + dci
                    for gi in range(4):
                        base = q * 128 + dci * 16 + gi * 4
                        if gi == 1 and pending_yev:
                            for fn_ in pending_yev:
                                fn_()
                            pending_yev = []
                        if gi % 2 == 0:
                            def exp_grp(sc, dci=dci, dc=dc, gi=gi, base=base, hb=hb, dumark=dumark):
                                ctx.wait_val(sc, "d", dumark)
                                for k in range(8):
                                    i = base + k
                                    n = gi * 4 + k
                                    if (i - 8) in scan_mark:
                                        ctx.wait_val(sc, "v", scan_mark[i - 8])
                                    ctx.inc(sc.activation(out=at[i % 8][:], in_=dall[hb][:, dci, :], func=AF.Exp,
                                                          scale=asc[:, dc, n:n + 1]), "a")
                                    exp_mark[i] = ctx.mark("a")
                            ctx.block.scalar(exp_grp)
                        def dve_grp(v, dci=dci, dc=dc, gi=gi, base=base, hb=hb, tbi=tbi, dumark=dumark):
                            ctx.wait_val(v, "d", dumark)
                            ctx.wait_val(v, "a", max(bcev_am[gi * 4 + 3], exp_mark[base + 3]))
                            v.tensor_tensor(bt0[0][:], uall[hb][:, dci, :], Bbc[:, gi * 4, :], OP.mult)
                            for k in range(4):
                                i = base + k
                                n = gi * 4 + k
                                slot = dc * 16 + n
                                init = zcol[:, 0:1] if tbi == 0 else carry[:, slot:slot + 1]
                                a_in = at[i % 8][:, ::-1] if rev else at[i % 8][:]
                                b_in = bt0[k % 2][:, ::-1] if rev else bt0[k % 2][:]
                                if (i - 4) in y_mark:
                                    ctx.wait_val(v, "p", y_mark[i - 4])
                                ctx.inc(v.tensor_tensor_scan(ht[i % 2][:], a_in, b_in, init, OP.mult, OP.add), "v")
                                if k < 3:
                                    v.tensor_tensor(bt0[(k + 1) % 2][:], uall[hb][:, dci, :], Bbc[:, n + 1, :], OP.mult)
                                ctx.wait(v, "v")
                                v.tensor_copy(carry[:, slot:slot + 1], ht[i % 2][:, TB - 1:TB])
                                h_in = ht[i % 2][:, ::-1] if rev else ht[i % 2][:]
                                ctx.inc(v.tensor_tensor(hct[i % 4][:], h_in, Cbc[:, n, :], OP.mult), "v")
                                scan_mark[i] = ctx.mark("v")
                        ctx.block.vector(dve_grp)
                        for hk in range(2):
                            def pe_grp(t, dci=dci, dc=dc, gi=gi, base=base, tbi=tbi, hk=hk):
                                b2 = base + hk * 2
                                ctx.wait_val(t, "v", scan_mark[b2 + 1])
                                gdc = tbi * 16 + dc
                                if gi == 0 and hk == 0 and (gdc - 2) in ev_mark:
                                    ctx.wait_val(t, "a", ev_mark[gdc - 2])
                                for k in range(2):
                                    i = b2 + k
                                    n = gi * 4 + hk * 2 + k
                                    for seg in range(2):
                                        ins = t.matmul(psY[dc % 2][:, seg * 512:(seg + 1) * 512], idt[:],
                                                       hct[i % 4][:, seg * 512:(seg + 1) * 512],
                                                       start=(n == 0), stop=(n == 15))
                                ctx.inc(ins, "p")
                                y_mark[b2 + 1] = ctx.mark("p")
                            ctx.block.tensor(pe_grp)
                    gdc = tbi * 16 + dc
                    ymark_last = y_mark[q * 128 + (dc % 8) * 16 + 12 + 3]
                    def mk_yev(dc=dc, s0=s0, gdc=gdc, ymark_last=ymark_last):
                        def y_ev(sc):
                            ctx.wait_val(sc, "p", ymark_last)
                            if (gdc - 2) in dmao_mark:
                                ctx.wait_val(sc, "o", dmao_mark[gdc - 2])
                            ctx.inc(sc.copy(out=ysb[dc % 2][:], in_=psY[dc % 2][:]), "a")
                            ev_mark[gdc] = ctx.mark("a")
                        ctx.block.scalar(y_ev)
                        def y_dma(g):
                            ctx.wait_val(g, "a", ev_mark[gdc])
                            ctx.inc(g.dma_start(out=ys_d[dc * 128:(dc + 1) * 128, s0:s0 + TB], in_=ysb[dc % 2][:]), "o", 16)
                            dmao_mark[gdc] = ctx.mark("o")
                        ctx.block.gpsimd(y_dma)
                    pending_yev.append(mk_yev)
                for fn_ in pending_yev:
                    fn_()
                pending_yev = []
                half_amark[q] = ctx.mark("a")
                half_gmark[q] = ctx.mark("g")
                half_vmark[q] = ctx.mark("v")
        def fin(g):
            ctx.wait(g, "o")
            g.nop()
        ctx.block.gpsimd(fin)


def emit_p4(ctx, dirn):
    nc = ctx.nc
    ys_d, xcs_d, zd = ctx.io["ys_" + dirn], ctx.io["xcs_" + dirn], ctx.io["z"]
    with ExitStack() as es:
        def sb(name, shape, dt):
            return es.enter_context(nc.sbuf_tensor("p4%s_%s" % (dirn, name), shape, dt))
        ys = sb("ys", [128, 16, TT], BF16)
        xcs = sb("xcs", [128, 16, TT], F32)
        zt = sb("zt", [128, 16, TT], F32)
        ygb = sb("ygb", [128, 16, TT], BF16)
        t1 = sb("t1", [128, TT], F32)
        t2 = sb("t2", [128, TT], F32)
        Dp = sb("Dp", [128, 16], F32)
        opw = sb("opw", [128, 16, 8 * 128], BF16)
        stg = sb("stg", [128, 1024], F32)
        ev = sb("ev", [128, 4, TT], F32)
        mft = sb("mft", [128, 8, TT], F32)
        psK = [es.enter_context(nc.psum_tensor("p4%s_ps%d" % (dirn, i), [128, TT], F32)) for i in range(4)]
        def dma_par(sync):
            ctx.inc(sync.dma_start(out=Dp[:], in_=ctx.io["D"]), "d", 16)
        ctx.block.sync(dma_par)
        load_weights_bf16(ctx, opw.ap(), ctx.io["out_proj_w"], DI, 1024, stg, "opw")
        opw_v = opw.ap().rearrange("p k (f j) -> p k f j", j=128)
        evk, mmk, dmk = {}, {}, {}
        sem_ev = "a" if dirn == "f" else "v"
        for it in range(NTILE):
            t0 = it * TT
            def dma_in(sync, t0=t0):
                ctx.wait(sync, "v", "a", "p")
                ctx.inc(sync.dma_start(out=ys[:], in_=ys_d.rearrange("(c p) t -> p c t", p=128)[:, :, t0:t0 + TT]), "d", 16)
                ctx.inc(sync.dma_start(out=xcs[:], in_=xcs_d.rearrange("(c p) t -> p c t", p=128)[:, :, t0:t0 + TT]), "d", 16)
                ctx.inc(sync.dma_start(out=zt[:], in_=zd.rearrange("(c p) t -> p c t", p=128)[:, :, t0:t0 + TT]), "d", 16)
                if dirn == "b":
                    ctx.inc(sync.dma_start(out=mft[:], in_=ctx.io["mf"].rearrange("(c p) t -> p c t", p=128)[:, :, t0:t0 + TT]), "d", 16)
            ctx.block.sync(dma_in)
            for c in range(16):
                def gate_a(sc, c=c):
                    ctx.wait(sc, "d", "v")
                    ctx.inc(sc.activation(out=t2[:], in_=zt[:, c, :], func=AF.Silu), "a")
                ctx.block.scalar(gate_a)
                def gate_v(v, c=c):
                    ctx.wait(v, "d", "a")
                    v.scalar_tensor_tensor(t1[:], xcs[:, c, :], Dp[:, c:c + 1], ys[:, c, :], OP.mult, OP.add)
                    ctx.inc(v.tensor_tensor(ygb[:, c, :], t1[:], t2[:], OP.mult), "v")
                ctx.block.vector(gate_v)
            gvm = ctx.mark("v")
            gam = ctx.mark("a")
            ddm = ctx.mark("d")
            for fo in range(8):
                j = fo % 4
                gfo = it * 8 + fo
                def mm(t, fo=fo, j=j, gfo=gfo, gvm=gvm, gam=gam):
                    ctx.wait_val(t, "v", gvm)
                    ctx.wait_val(t, "a", gam)
                    if (gfo - 4) in evk:
                        ctx.wait_val(t, sem_ev, evk[gfo - 4])
                    for ki in range(16):
                        ins = t.matmul(psK[j][:], opw_v[:, ki, fo, :], ygb[:, ki, :],
                                       start=(ki == 0), stop=(ki == 15))
                    ctx.inc(ins, "p")
                    mmk[gfo] = ctx.mark("p")
                ctx.block.tensor(mm)
                if dirn == "f":
                    def evac(sc, fo=fo, j=j, gfo=gfo):
                        ctx.wait_val(sc, "p", mmk[gfo])
                        if (gfo - 4) in dmk:
                            ctx.wait_val(sc, "o", dmk[gfo - 4])
                        ctx.inc(sc.copy(out=ev[:, fo % 4, :], in_=psK[j][:]), "a")
                        evk[gfo] = ctx.mark("a")
                    ctx.block.scalar(evac)
                    def dma_out(g, fo=fo, t0=t0, gfo=gfo):
                        ctx.wait_val(g, "a", evk[gfo])
                        ctx.inc(g.dma_start(out=ctx.io["mf"][fo * 128:(fo + 1) * 128, t0:t0 + TT],
                                            in_=ev[:, fo % 4, :]), "o", 16)
                        dmk[gfo] = ctx.mark("o")
                    ctx.block.gpsimd(dma_out)
                else:
                    def evac(v, fo=fo, j=j, gfo=gfo, ddm=ddm):
                        ctx.wait_val(v, "p", mmk[gfo])
                        ctx.wait_val(v, "d", ddm)
                        if (gfo - 4) in dmk:
                            ctx.wait_val(v, "o", dmk[gfo - 4])
                        ctx.inc(v.tensor_tensor(ev[:, fo % 4, :], psK[j][:], mft[:, fo, :], OP.add), "v")
                        evk[gfo] = ctx.mark("v")
                    ctx.block.vector(evac)
                    def dma_out(g, fo=fo, t0=t0, gfo=gfo):
                        ctx.wait_val(g, "v", evk[gfo])
                        ctx.inc(g.dma_start(out=ctx.io["mo"][fo * 128:(fo + 1) * 128, t0:t0 + TT],
                                            in_=ev[:, fo % 4, :]), "o", 16)
                        dmk[gfo] = ctx.mark("o")
                    ctx.block.gpsimd(dma_out)
        def fin(g):
            ctx.wait(g, "o")
            g.nop()
        ctx.block.gpsimd(fin)


def emit_p45a(ctx):
    nc = ctx.nc
    with ExitStack() as es:
        def sb(name, shape, dt):
            return es.enter_context(nc.sbuf_tensor("p45a_%s" % name, shape, dt))
        def psum(name, shape):
            return es.enter_context(nc.psum_tensor("p45a_%s" % name, shape, F32))
        mo = [sb("mo%d" % i, [128, 8, TT], F32) for i in range(2)]
        sq = sb("sq", [128, 8, TT], F32)
        mbf = sb("mbf", [128, 8, TT], BF16)
        hb = [sb("hb%d" % i, [128, 8, TT], BF16) for i in range(2)]
        st = sb("st", [1, 7 * TT], F32)
        tmp = sb("tmp", [128, TT], F32)
        ones_bf = sb("onesb", [128, 1], BF16)
        ones_f32 = sb("onesf", [128, 1], F32)
        ones1 = sb("ones1", [1, 128], F32)
        cns = sb("cns", [128, 4], F32)
        lng = sb("lng", [128, 8], F32)
        lnb = sb("lnb", [128, 8], F32)
        fb1 = sb("fb1", [128, 32], F32)
        w1 = sb("w1", [128, 8, 32 * 128], BF16)
        stg = sb("stg", [128, 4096], F32)
        ev = sb("ev", [128, 4, TT], BF16)
        psA = psum("psA", [1, TT]); psB = psum("psB", [1, TT])
        psR = psum("psR", [128, TT]); psM = psum("psM", [128, TT])
        psK = [psum("ps%d" % i, [128, TT]) for i in range(4)]
        def dma_par(sync):
            ctx.inc(sync.dma_start(out=lng[:], in_=ctx.io["ln_g"]), "d", 16)
            ctx.inc(sync.dma_start(out=lnb[:], in_=ctx.io["ln_b"]), "d", 16)
            ctx.inc(sync.dma_start(out=fb1[:], in_=ctx.io["ff_b1"]), "d", 16)
        ctx.block.sync(dma_par)
        def init(v):
            v.memset(ones_bf[:], 1.0)
            v.memset(ones_f32[:], 1.0)
            v.memset(cns[:, 0:1], EPS_LN)
            ctx.inc(v.memset(ones1[:], 1.0), "v")
        ctx.block.vector(init)
        load_weights_bf16(ctx, w1.ap(), ctx.io["ff_w1"], DM, 4096, stg, "w1")
        w1_v = w1.ap().rearrange("p k (f j) -> p k f j", j=128)
        gains = [lng[:, c:c + 1] for c in range(8)]
        biases = [lnb[:, c:c + 1] for c in range(8)]
        tile_vmark, tile_amark, evk, dmk, mmk = {}, {}, {}, {}, {}
        for it in range(NTILE):
            t0 = it * TT
            tb = it % 2
            def dma_in(sync, t0=t0, it=it, tb=tb):
                if it >= 2:
                    ctx.wait_val(sync, "v", tile_vmark[it - 2])
                    ctx.wait_val(sync, "a", tile_amark[it - 2])
                ctx.inc(sync.dma_start(out=mo[tb][:], in_=ctx.io["mo"].rearrange("(c p) t -> p c t", p=128)[:, :, t0:t0 + TT]), "d", 16)
            ctx.block.sync(dma_in)
            rstd, m2 = emit_ln_stats(ctx, mo[tb].ap(), 8, TT, mbf.ap(), sq.ap(), psA, psB, st.ap(),
                                     ones_bf, ones_f32, cns[0:1, 0:1])
            emit_ln_normalize(ctx, mo[tb].ap(), 8, TT, hb[tb].ap(), rstd, m2, psR, psM, ones1, tmp.ap(),
                              gain=gains, bias_=biases)
            tile_vmark[it] = ctx.mark("v")
            tile_amark[it] = ctx.mark("a")
            nvm = tile_vmark[it]
            for fo in range(32):
                j = fo % 4
                gfo = it * 32 + fo
                def mm(t, fo=fo, j=j, gfo=gfo, nvm=nvm, tb=tb):
                    ctx.wait_val(t, "v", nvm)
                    if (gfo - 4) in evk:
                        ctx.wait_val(t, "a", evk[gfo - 4])
                    for ki in range(8):
                        ins = t.matmul(psK[j][:], w1_v[:, ki, fo, :], hb[tb][:, ki, :],
                                       start=(ki == 0), stop=(ki == 7))
                    ctx.inc(ins, "p")
                    mmk[gfo] = ctx.mark("p")
                ctx.block.tensor(mm)
                def evac(sc, fo=fo, j=j, gfo=gfo):
                    ctx.wait_val(sc, "p", mmk[gfo])
                    if (gfo - 4) in dmk:
                        ctx.wait_val(sc, "o", dmk[gfo - 4])
                    ctx.inc(sc.activation(out=ev[:, fo % 4, :], in_=psK[j][:], func=AF.Gelu,
                                          bias=fb1[:, fo:fo + 1]), "a")
                    evk[gfo] = ctx.mark("a")
                ctx.block.scalar(evac)
                def dma_out(g, fo=fo, t0=t0, gfo=gfo):
                    ctx.wait_val(g, "a", evk[gfo])
                    ctx.inc(g.dma_start(out=ctx.io["g1"][fo * 128:(fo + 1) * 128, t0:t0 + TT],
                                        in_=ev[:, fo % 4, :]), "o", 16)
                    dmk[gfo] = ctx.mark("o")
                ctx.block.gpsimd(dma_out)
        def fin(g):
            ctx.wait(g, "o")
            g.nop()
        ctx.block.gpsimd(fin)


def emit_p45b(ctx):
    nc = ctx.nc
    with ExitStack() as es:
        def sb(name, shape, dt):
            return es.enter_context(nc.sbuf_tensor("p45b_%s" % name, shape, dt))
        def psum(name, shape):
            return es.enter_context(nc.psum_tensor("p45b_%s" % name, shape, F32))
        g1 = sb("g1", [128, 32, TT], BF16)
        xt = sb("xt", [128, 8, TT], F32)
        f = sb("f", [128, 8, TT], F32)
        fbf = sb("fbf", [128, 8, TT], BF16)
        sq = sb("sq", [128, 8, TT], F32)
        fn = sb("fn", [128, 8, TT], F32)
        st = sb("st", [1, 7 * TT], F32)
        tmp = sb("tmp", [128, TT], F32)
        ones_bf = sb("onesb", [128, 1], BF16)
        ones_f32 = sb("onesf", [128, 1], F32)
        ones1 = sb("ones1", [1, 128], F32)
        cns = sb("cns", [128, 4], F32)
        fb2 = sb("fb2", [128, 8], F32)
        w2 = sb("w2", [128, 32, 8 * 128], BF16)
        stg = sb("stg", [128, 1024], F32)
        psA = psum("psA", [1, TT]); psB = psum("psB", [1, TT])
        psR = psum("psR", [128, TT]); psM = psum("psM", [128, TT])
        psK = [psum("ps%d" % i, [128, TT]) for i in range(4)]
        def dma_par(sync):
            ctx.inc(sync.dma_start(out=fb2[:], in_=ctx.io["ff_b2"]), "d", 16)
        ctx.block.sync(dma_par)
        def init(v):
            v.memset(ones_bf[:], 1.0)
            v.memset(ones_f32[:], 1.0)
            v.memset(cns[:, 0:1], EPS_LN)
            ctx.inc(v.memset(ones1[:], 1.0), "v")
        ctx.block.vector(init)
        load_weights_bf16(ctx, w2.ap(), ctx.io["ff_w2"], H4, 1024, stg, "w2")
        w2_v = w2.ap().rearrange("p k (f j) -> p k f j", j=128)
        evk, mmk = {}, {}
        for it in range(NTILE):
            t0 = it * TT
            def dma_in(sync, t0=t0):
                ctx.wait(sync, "v", "a", "p")
                ctx.inc(sync.dma_start(out=g1[:], in_=ctx.io["g1"].rearrange("(c p) t -> p c t", p=128)[:, :, t0:t0 + TT]), "d", 16)
                ctx.inc(sync.dma_start(out=xt[:], in_=ctx.io["x"].rearrange("(c p) t -> p c t", p=128)[:, :, t0:t0 + TT]), "d", 16)
            ctx.block.sync(dma_in)
            dmm = ctx.mark("d")
            avm = ctx.mark("a")
            for fo in range(8):
                j = fo % 4
                gfo = it * 8 + fo
                def mm(t, fo=fo, j=j, gfo=gfo, dmm=dmm, avm=avm):
                    ctx.wait_val(t, "d", dmm)
                    ctx.wait_val(t, "a", avm if (gfo - 4) not in evk else max(avm, evk[gfo - 4]))
                    for ki in range(32):
                        ins = t.matmul(psK[j][:], w2_v[:, ki, fo, :], g1[:, ki, :],
                                       start=(ki == 0), stop=(ki == 31))
                    ctx.inc(ins, "p")
                    mmk[gfo] = ctx.mark("p")
                ctx.block.tensor(mm)
                def evac(sc, fo=fo, j=j, gfo=gfo):
                    ctx.wait_val(sc, "p", mmk[gfo])
                    ctx.wait(sc, "v", "o")
                    ctx.inc(sc.activation(out=f[:, fo, :], in_=psK[j][:], func=AF.Identity,
                                          bias=fb2[:, fo:fo + 1]), "a")
                    evk[gfo] = ctx.mark("a")
                ctx.block.scalar(evac)
            rstd, m2 = emit_ln_stats(ctx, f.ap(), 8, TT, fbf.ap(), sq.ap(), psA, psB, st.ap(),
                                     ones_bf, ones_f32, cns[0:1, 0:1])
            emit_ln_normalize(ctx, f.ap(), 8, TT, fn.ap(), rstd, m2, psR, psM, ones1, tmp.ap())
            def resid(v):
                ctx.wait(v, "d", "o")
                ctx.inc(v.tensor_tensor(sq[:], fn[:], xt[:], OP.add), "v")
            ctx.block.vector(resid)
            def dma_out(g, t0=t0):
                ctx.wait(g, "v")
                ctx.inc(g.dma_start(out=ctx.io["x1"].rearrange("(c p) t -> p c t", p=128)[:, :, t0:t0 + TT], in_=sq[:]), "o", 16)
            ctx.block.gpsimd(dma_out)
        def fin(g):
            ctx.wait(g, "o")
            g.nop()
        ctx.block.gpsimd(fin)


def emit_p5(ctx):
    nc = ctx.nc
    with ExitStack() as es:
        def sb(name, shape, dt):
            return es.enter_context(nc.sbuf_tensor("p5_%s" % name, shape, dt))
        def psum(name, shape):
            return es.enter_context(nc.psum_tensor("p5_%s" % name, shape, F32))
        xa = sb("xa", [128, 8, TT], F32)
        sq = sb("sq", [128, 8, TT], F32)
        xbf = sb("xbf", [128, 8, TT], BF16)
        u = sb("u", [128, 8, TT], F32)
        upr = sb("upr", [128, 8, TT], F32)
        ucar = sb("ucar", [128, 8, 1], F32)
        xrb = sb("xrb", [128, 8, TT], BF16)
        xkb = sb("xkb", [128, 8, TT], BF16)
        rb = sb("rb", [128, 8, TT], F32)
        kab = sb("kab", [128, 8, TT], BF16)
        ob = sb("ob", [128, 8, TT], F32)
        obf = sb("obf", [128, 8, TT], BF16)
        st = sb("st", [1, 7 * TT], F32)
        tmp = sb("tmp", [128, TT], F32)
        t2 = sb("t2", [128, TT], F32)
        ones_bf = sb("onesb", [128, 1], BF16)
        ones_f32 = sb("onesf", [128, 1], F32)
        ones1 = sb("ones1", [1, 128], F32)
        cns = sb("cns", [128, 4], F32)
        mur = sb("mur", [128, 8], F32)
        muk = sb("muk", [128, 8], F32)
        wr = sb("wr", [128, 8, 8 * 128], BF16)
        wk = sb("wk", [128, 8, 8 * 128], BF16)
        wv = sb("wv", [128, 8, 8 * 128], BF16)
        stg = sb("stg", [128, 1024], F32)
        psA = psum("psA", [1, TT]); psB = psum("psB", [1, TT])
        psR = psum("psR", [128, TT]); psM = psum("psM", [128, TT])
        psK = [psum("ps%d" % i, [128, TT]) for i in range(4)]
        def dma_par(sync):
            ctx.inc(sync.dma_start(out=mur[:], in_=ctx.io["L0_ffn_mu_r"]), "d", 16)
            ctx.inc(sync.dma_start(out=muk[:], in_=ctx.io["L0_ffn_mu_k"]), "d", 16)
        ctx.block.sync(dma_par)
        def init(v):
            v.memset(ones_bf[:], 1.0)
            v.memset(ones_f32[:], 1.0)
            v.memset(cns[:, 0:1], EPS_LN)
            v.memset(ucar[:], 0.0)
            ctx.inc(v.memset(ones1[:], 1.0), "v")
        ctx.block.vector(init)
        load_weights_bf16(ctx, wr.ap(), ctx.io["WrT"], DM, 1024, stg, "wr")
        load_weights_bf16(ctx, wk.ap(), ctx.io["WkT"], DM, 1024, stg, "wk")
        load_weights_bf16(ctx, wv.ap(), ctx.io["WvT"], DM, 1024, stg, "wv")
        wr_v = wr.ap().rearrange("p k (f j) -> p k f j", j=128)
        wk_v = wk.ap().rearrange("p k (f j) -> p k f j", j=128)
        wv_v = wv.ap().rearrange("p k (f j) -> p k f j", j=128)
        for it in range(NTILE):
            t0 = it * TT
            def dma_in(sync, t0=t0):
                ctx.wait(sync, "v", "a", "p")
                ctx.inc(sync.dma_start(out=xa[:], in_=ctx.io["x1"].rearrange("(c p) t -> p c t", p=128)[:, :, t0:t0 + TT]), "d", 16)
            ctx.block.sync(dma_in)
            rstd, m2 = emit_ln_stats(ctx, xa.ap(), 8, TT, xbf.ap(), sq.ap(), psA, psB, st.ap(),
                                     ones_bf, ones_f32, cns[0:1, 0:1])
            emit_ln_normalize(ctx, xa.ap(), 8, TT, u.ap(), rstd, m2, psR, psM, ones1, tmp.ap())
            def shift_v(v):
                v.tensor_copy(upr[:, :, 1:TT], u[:, :, 0:TT - 1])
                ctx.inc(v.tensor_copy(upr[:, :, 0:1], ucar[:]), "v")
                ctx.wait(v, "v")
                ctx.inc(v.tensor_copy(ucar[:], u[:, :, TT - 1:TT]), "v")
                for c in range(8):
                    v.tensor_tensor(t2[:], u[:, c, :], upr[:, c, :], OP.subtract)
                    v.scalar_tensor_tensor(xrb[:, c, :], t2[:], mur[:, c:c + 1], upr[:, c, :], OP.mult, OP.add)
                    ins = v.scalar_tensor_tensor(xkb[:, c, :], t2[:], muk[:, c:c + 1], upr[:, c, :], OP.mult, OP.add)
                ctx.inc(ins, "v")
            ctx.block.vector(shift_v)
            xvm = ctx.mark("v")
            mmk, evk = {}, {}
            for fo in range(24):
                j = fo % 4
                grp = fo // 8
                fi = fo % 8
                def mm3(t, fo=fo, j=j, grp=grp, fi=fi, xvm=xvm):
                    ctx.wait_val(t, "v", xvm)
                    if (fo - 4) in evk:
                        s_, v_ = evk[fo - 4]
                        ctx.wait_val(t, s_, v_)
                    if grp == 2:
                        ctx.wait_val(t, "a", evk[15][1])
                    wsel = (wr_v, wk_v, wv_v)[grp]
                    xsel = (xrb, xkb, kab)[grp]
                    for ki in range(8):
                        ins = t.matmul(psK[j][:], wsel[:, ki, fi, :], xsel[:, ki, :],
                                       start=(ki == 0), stop=(ki == 7))
                    ctx.inc(ins, "p")
                    mmk[fo] = ctx.mark("p")
                ctx.block.tensor(mm3)
                if grp == 0:
                    def ev_r(sc, fo=fo, j=j, fi=fi):
                        ctx.wait_val(sc, "p", mmk[fo])
                        ctx.wait(sc, "v", "o")
                        ctx.inc(sc.activation(out=rb[:, fi, :], in_=psK[j][:], func=AF.Sigmoid), "a")
                        evk[fo] = ("a", ctx.mark("a"))
                    ctx.block.scalar(ev_r)
                elif grp == 1:
                    def ev_k(sc, fo=fo, j=j, fi=fi):
                        ctx.wait_val(sc, "p", mmk[fo])
                        sc.activation(out=t2[:], in_=psK[j][:], func=AF.Relu)
                        ctx.inc(sc.activation(out=kab[:, fi, :], in_=t2[:], func=AF.Square), "a")
                        evk[fo] = ("a", ctx.mark("a"))
                    ctx.block.scalar(ev_k)
                else:
                    def ev_v(v, fo=fo, j=j, fi=fi):
                        ctx.wait_val(v, "p", mmk[fo])
                        ctx.wait_val(v, "a", evk[fi][1])
                        ctx.inc(v.tensor_tensor(ob[:, fi, :], rb[:, fi, :], psK[j][:], OP.mult), "v")
                        evk[fo] = ("v", ctx.mark("v"))
                    ctx.block.vector(ev_v)
            rstd, m2 = emit_ln_stats(ctx, ob.ap(), 8, TT, obf.ap(), sq.ap(), psA, psB, st.ap(),
                                     ones_bf, ones_f32, cns[0:1, 0:1])
            emit_ln_normalize(ctx, ob.ap(), 8, TT, rb.ap(), rstd, m2, psR, psM, ones1, tmp.ap())
            def resid(v):
                ctx.wait(v, "o")
                ctx.inc(v.tensor_tensor(sq[:], rb[:], xa[:], OP.add), "v")
            ctx.block.vector(resid)
            def dma_out(g, t0=t0):
                ctx.wait(g, "v")
                ctx.inc(g.dma_start(out=ctx.io["x2"].rearrange("(c p) t -> p c t", p=128)[:, :, t0:t0 + TT], in_=sq[:]), "o", 16)
            ctx.block.gpsimd(dma_out)
        def fin(g):
            ctx.wait(g, "o")
            g.nop()
        ctx.block.gpsimd(fin)


def emit_p6(ctx):
    nc = ctx.nc
    with ExitStack() as es:
        def sb(name, shape, dt):
            return es.enter_context(nc.sbuf_tensor("p6_%s" % name, shape, dt))
        def psum(name, shape):
            return es.enter_context(nc.psum_tensor("p6_%s" % name, shape, F32))
        xa = sb("xa", [128, 8, TT], F32)
        xbf = sb("xbf", [128, 8, TT], BF16)
        vb = sb("vb", [128, 8, TT], BF16)
        gtf = sb("gtf", [128, 8, TT], F32)
        gti = sb("gti", [128, 8, TT], F32)
        gth = sb("gth", [128, 8, TT], F32)
        tmpB = sb("tmpB", [128, 8, TT], F32)
        st = sb("st", [1, 7 * TT], F32)
        tmp = sb("tmp", [128, TT], F32)
        msk = sb("msk", [128, 8, TT], U8)
        lft = sb("lft", [128, 8, TT], F32)
        lvt = sb("lvt", [128, 8, TT], F32)
        sq = lft
        ones_bf = sb("onesb", [128, 1], BF16)
        ones_f32 = sb("onesf", [128, 1], F32)
        ones1 = sb("ones1", [1, 128], F32)
        cns = sb("cns", [128, 4], F32)
        wt = sb("wt", [128, 8, 24 * 128], BF16)
        stg = sb("stg", [128, 1024], F32)
        psA = psum("psA", [1, TT]); psB = psum("psB", [1, TT])
        psR = psum("psR", [128, TT]); psM = psum("psM", [128, TT])
        psK = [psum("ps%d" % i, [128, TT]) for i in range(4)]
        def init(v):
            v.memset(ones_bf[:], 1.0)
            v.memset(ones_f32[:], 1.0)
            v.memset(cns[:, 0:1], EPS_LN)
            v.memset(cns[:, 1:2], 1.0)
            v.memset(cns[:, 2:3], 0.5 + EPS_L1)
            ctx.inc(v.memset(ones1[:], 1.0), "v")
        ctx.block.vector(init)
        load_weights_bf16(ctx, wt.ap(), ctx.io["L1WT"], DM, 3072, stg, "wt", mcap=1024)
        wt_v = wt.ap().rearrange("p k (f j) -> p k f j", j=128)
        one_b = cns[:, 1:2]
        half_b = cns[:, 2:3]

        def sp_chain(src_ap, out_ap, sign, s1, s2):
            def sa(sc):
                ctx.wait(sc, "v")
                sc.activation(out=s1[:], in_=src_ap, func=AF.Abs)
                sc.activation(out=s2[:], in_=s1[:], func=AF.Exp, scale=-1.0)
                sc.activation(out=s1[:], in_=s2[:], func=AF.Ln, bias=one_b)
                ctx.inc(sc.activation(out=s2[:], in_=src_ap, func=AF.Relu, scale=float(sign)), "a")
            ctx.block.scalar(sa)
            def sv(v):
                ctx.wait(v, "a")
                ctx.inc(v.tensor_tensor(out_ap, s1[:], s2[:], OP.add), "v")
            ctx.block.vector(sv)

        for it in range(NTILE):
            t0 = it * TT
            def dma_in(sync, t0=t0):
                ctx.wait(sync, "v", "a", "p")
                ctx.inc(sync.dma_start(out=xa[:], in_=ctx.io["x2"].rearrange("(c p) t -> p c t", p=128)[:, :, t0:t0 + TT]), "d", 16)
            ctx.block.sync(dma_in)
            rstd, m2 = emit_ln_stats(ctx, xa.ap(), 8, TT, xbf.ap(), sq.ap(), psA, psB, st.ap(),
                                     ones_bf, ones_f32, cns[0:1, 0:1])
            emit_ln_normalize(ctx, xa.ap(), 8, TT, vb.ap(), rstd, m2, psR, psM, ones1, tmp.ap())
            mmk, evk = {}, {}
            for fo in range(24):
                j = fo % 4
                def mm(t, fo=fo, j=j):
                    ctx.wait(t, "v")
                    if fo - 4 in evk:
                        ctx.wait_val(t, "a", evk[fo - 4])
                    for ki in range(8):
                        ins = t.matmul(psK[j][:], wt_v[:, ki, fo, :], vb[:, ki, :],
                                       start=(ki == 0), stop=(ki == 7))
                    ctx.inc(ins, "p")
                    mmk[fo] = ctx.mark("p")
                ctx.block.tensor(mm)
                def evac(sc, fo=fo, j=j):
                    ctx.wait_val(sc, "p", mmk[fo])
                    ctx.wait(sc, "v")
                    dst = (gtf, gti, gth)[fo // 8]
                    ctx.inc(sc.copy(out=dst[:, fo % 8, :], in_=psK[j][:]), "a")
                    evk[fo] = ctx.mark("a")
                ctx.block.scalar(evac)
            gvm = ctx.mark("v")
            def sp_fi(sc, gvm=gvm):
                ctx.wait_val(sc, "v", gvm)
                sc.activation(out=xa[:], in_=gtf[:], func=AF.Abs)
                sc.activation(out=tmpB[:], in_=xa[:], func=AF.Exp, scale=-1.0)
                sc.activation(out=xa[:], in_=tmpB[:], func=AF.Ln, bias=one_b)
                sc.activation(out=tmpB[:], in_=gtf[:], func=AF.Relu, scale=-1.0)
                sc.activation(out=lvt[:], in_=gti[:], func=AF.Abs)
                sc.activation(out=lft[:], in_=lvt[:], func=AF.Exp, scale=-1.0)
                sc.activation(out=lvt[:], in_=lft[:], func=AF.Ln, bias=one_b)
                ctx.inc(sc.activation(out=lft[:], in_=gti[:], func=AF.Relu, scale=-1.0), "a")
            ctx.block.scalar(sp_fi)
            def mk_diff(v):
                ctx.wait(v, "a")
                v.tensor_tensor(gtf[:], xa[:], tmpB[:], OP.add)
                v.tensor_tensor(gti[:], lvt[:], lft[:], OP.add)
                ctx.inc(v.tensor_tensor(xa[:], gtf[:], gti[:], OP.subtract), "v")
            ctx.block.vector(mk_diff)
            dm_ = ctx.mark("v")
            def sp_d(sc, dm_=dm_):
                ctx.wait_val(sc, "v", dm_)
                sc.activation(out=gtf[:], in_=xa[:], func=AF.Abs)
                sc.activation(out=tmpB[:], in_=gtf[:], func=AF.Exp, scale=-1.0)
                sc.activation(out=gtf[:], in_=tmpB[:], func=AF.Ln, bias=one_b)
                sc.activation(out=tmpB[:], in_=xa[:], func=AF.Relu)
                ctx.inc(sc.activation(out=gti[:], in_=xa[:], func=AF.Relu, scale=-1.0), "a")
            ctx.block.scalar(sp_d)
            def mk_lfli(v, t0=t0):
                ctx.wait(v, "a", "o")
                v.tensor_tensor(tmpB[:], gtf[:], tmpB[:], OP.add)
                v.tensor_tensor(gti[:], gtf[:], gti[:], OP.add)
                v.tensor_scalar(lft[:], tmpB[:], -1.0, EPS_L1, OP.mult, OP.add)
                ctx.inc(v.tensor_scalar(xa[:], gti[:], -1.0, None, OP.mult), "v")
            ctx.block.vector(mk_lfli)
            lm_ = ctx.mark("v")
            def sp_h(sc, lm_=lm_):
                ctx.wait_val(sc, "v", lm_)
                sc.activation(out=gtf[:], in_=gth[:], func=AF.Abs)
                sc.activation(out=tmpB[:], in_=gtf[:], func=AF.Exp, scale=-1.0)
                sc.activation(out=gtf[:], in_=tmpB[:], func=AF.Ln, bias=one_b)
                sc.activation(out=tmpB[:], in_=gth[:], func=AF.Relu, scale=-1.0)
                sc.activation(out=gti[:], in_=gth[:], func=AF.Relu)
                ctx.inc(sc.activation(out=lvt[:], in_=gti[:], func=AF.Ln, bias=half_b), "a")
            ctx.block.scalar(sp_h)
            def mksel(v, t0=t0):
                ctx.wait(v, "a")
                v.tensor_tensor(gti[:], gtf[:], tmpB[:], OP.add)
                ctx.inc(v.tensor_scalar(tmpB[:], gti[:], -1.0, EPS_L1, OP.mult, OP.add), "v")
                ctx.inc(v.tensor_scalar(msk[:], gth[:], 0.0, None, OP.is_ge), "v")
                ctx.wait(v, "v")
                v.select(gth[:], msk[:], lvt[:], tmpB[:], add_drain=True)
                ctx.inc(v.tensor_tensor(lvt[:], xa[:], gth[:], OP.add), "v")
            ctx.block.vector(mksel)
            def dma_out(g, t0=t0):
                ctx.wait(g, "v")
                ctx.inc(g.dma_start(out=ctx.io["lf"].rearrange("(c p) t -> p c t", p=128)[:, :, t0:t0 + TT], in_=lft[:]), "o", 16)
                ctx.inc(g.dma_start(out=ctx.io["lv"].rearrange("(c p) t -> p c t", p=128)[:, :, t0:t0 + TT], in_=lvt[:]), "o", 16)
            ctx.block.gpsimd(dma_out)
        def fin(g):
            ctx.wait(g, "o")
            g.nop()
        ctx.block.gpsimd(fin)


def emit_p7(ctx):
    nc = ctx.nc
    with ExitStack() as es:
        def sb(name, shape, dt):
            return es.enter_context(nc.sbuf_tensor("p7_%s" % name, shape, dt))
        A = sb("A", [128, SEQ], F32)
        B = sb("B", [128, SEQ], F32)
        C = sb("C", [128, SEQ], F32)
        D = sb("D", [128, SEQ], F32)
        E = sb("E", [128, SEQ], F32)
        ones = sb("ones", [128, SEQ], F32)
        zc = sb("zc", [128, 1], F32)
        nc_ = sb("ncol", [128, 1], F32)
        def init(v):
            v.memset(ones[:], 1.0)
            v.memset(zc[:], 0.0)
            ctx.inc(v.memset(nc_[:], -1e30), "v")
        ctx.block.vector(init)
        for fc in range(8):
            def dma_in(sync, fc=fc):
                ctx.wait(sync, "v", "a", "o")
                ctx.inc(sync.dma_start(out=A[:], in_=ctx.io["lf"][fc * 128:(fc + 1) * 128, :]), "d", 16)
                ctx.inc(sync.dma_start(out=B[:], in_=ctx.io["lv"][fc * 128:(fc + 1) * 128, :]), "d", 16)
            ctx.block.sync(dma_in)
            def scans(v, fc=fc):
                ctx.wait(v, "d")
                ctx.inc(v.tensor_tensor_scan(C[:], ones[:], A[:], zc[:, 0:1], OP.mult, OP.add), "v")
                ctx.wait(v, "v")
                ctx.inc(v.tensor_tensor(D[:], B[:], C[:], OP.subtract), "v")
                ctx.inc(v.tensor_tensor_scan(E[:], D[:], D[:], nc_[:, 0:1], OP.max, OP.max), "v")
                ctx.wait(v, "v")
                v.tensor_copy(B[:, 0:1], nc_[:])
                ctx.inc(v.tensor_tensor(B[:, 1:SEQ], E[:, 0:SEQ - 1], E[:, 1:SEQ], OP.subtract), "v")
                ctx.inc(v.tensor_tensor(A[:], D[:], E[:], OP.subtract), "v")
            ctx.block.vector(scans)
            def exps(sc, fc=fc):
                ctx.wait(sc, "v")
                sc.activation(out=D[:], in_=B[:], func=AF.Exp)
                ctx.inc(sc.activation(out=B[:, 0:SEQ], in_=A[:], func=AF.Exp), "a")
            ctx.block.scalar(exps)
            def scan2(v, fc=fc):
                ctx.wait(v, "a")
                ctx.inc(v.tensor_tensor_scan(A[:], D[:], B[:], zc[:, 0:1], OP.mult, OP.add), "v")
                ctx.wait(v, "v")
                ctx.inc(v.tensor_tensor(B[:], C[:], E[:], OP.add), "v")
            ctx.block.vector(scan2)
            def fin_a(sc, fc=fc):
                ctx.wait(sc, "v")
                ctx.inc(sc.activation(out=D[:], in_=A[:], func=AF.Ln), "a")
            ctx.block.scalar(fin_a)
            def fin_v(v, fc=fc):
                ctx.wait(v, "a")
                ctx.inc(v.tensor_tensor(A[:], B[:], D[:], OP.add), "v")
            ctx.block.vector(fin_v)
            def fin_a2(sc, fc=fc):
                ctx.wait(sc, "v")
                ctx.inc(sc.activation(out=E[:], in_=A[:], func=AF.Exp), "a")
            ctx.block.scalar(fin_a2)
            def dma_out(g, fc=fc):
                ctx.wait(g, "a")
                ctx.inc(g.dma_start(out=ctx.io["ht"][fc * 128:(fc + 1) * 128, :], in_=E[:]), "o", 16)
            ctx.block.gpsimd(dma_out)
        def fin(g):
            ctx.wait(g, "o")
            g.nop()
        ctx.block.gpsimd(fin)


def emit_p8a(ctx):
    nc = ctx.nc
    with ExitStack() as es:
        def sb(name, shape, dt):
            return es.enter_context(nc.sbuf_tensor("p8a_%s" % name, shape, dt))
        def psum(name, shape):
            return es.enter_context(nc.psum_tensor("p8a_%s" % name, shape, F32))
        xa = sb("xa", [128, 8, TT], F32)
        hta = sb("hta", [128, 8, TT], F32)
        sq = sb("sq", [128, 8, TT], F32)
        hbf = sb("hbf", [128, 8, TT], BF16)
        hn = sb("hn", [128, 8, TT], F32)
        x3b = sb("x3b", [128, 8, TT], F32)
        w3t = sb("w3t", [128, 8, TT], BF16)
        st = sb("st", [1, 7 * TT], F32)
        tmp = sb("tmp", [128, TT], F32)
        ones_bf = sb("onesb", [128, 1], BF16)
        ones_f32 = sb("onesf", [128, 1], F32)
        ones1 = sb("ones1", [1, 128], F32)
        cns = sb("cns", [128, 4], F32)
        psA = psum("psA", [1, TT]); psB = psum("psB", [1, TT])
        psR = psum("psR", [128, TT]); psM = psum("psM", [128, TT])
        def init(v):
            v.memset(ones_bf[:], 1.0)
            v.memset(ones_f32[:], 1.0)
            v.memset(cns[:, 0:1], EPS_LN)
            ctx.inc(v.memset(ones1[:], 1.0), "v")
        ctx.block.vector(init)
        for it in range(NTILE):
            t0 = it * TT
            def dma_in(sync, t0=t0):
                ctx.wait(sync, "v", "a", "p")
                ctx.inc(sync.dma_start(out=xa[:], in_=ctx.io["x2"].rearrange("(c p) t -> p c t", p=128)[:, :, t0:t0 + TT]), "d", 16)
                ctx.inc(sync.dma_start(out=hta[:], in_=ctx.io["ht"].rearrange("(c p) t -> p c t", p=128)[:, :, t0:t0 + TT]), "d", 16)
            ctx.block.sync(dma_in)
            rstd, m2 = emit_ln_stats(ctx, hta.ap(), 8, TT, hbf.ap(), sq.ap(), psA, psB, st.ap(),
                                     ones_bf, ones_f32, cns[0:1, 0:1])
            emit_ln_normalize(ctx, hta.ap(), 8, TT, hn.ap(), rstd, m2, psR, psM, ones1, tmp.ap())
            def resid(v):
                ctx.wait(v, "o")
                ctx.inc(v.tensor_tensor(x3b[:], hn[:], xa[:], OP.add), "v")
            ctx.block.vector(resid)
            def dma_x3(g, t0=t0):
                ctx.wait(g, "v")
                ctx.inc(g.dma_start(out=ctx.io["x3"].rearrange("(c p) t -> p c t", p=128)[:, :, t0:t0 + TT], in_=x3b[:]), "o", 16)
            ctx.block.gpsimd(dma_x3)
            rstd, m2 = emit_ln_stats(ctx, x3b.ap(), 8, TT, hbf.ap(), sq.ap(), psA, psB, st.ap(),
                                     ones_bf, ones_f32, cns[0:1, 0:1])
            emit_ln_normalize(ctx, x3b.ap(), 8, TT, w3t.ap(), rstd, m2, psR, psM, ones1, tmp.ap())
            def dma_w3(g, t0=t0):
                ctx.wait(g, "v")
                ctx.inc(g.dma_start(out=ctx.io["w3"].rearrange("(c p) t -> p c t", p=128)[:, :, t0:t0 + TT], in_=w3t[:]), "o", 16)
            ctx.block.gpsimd(dma_w3)
        def fin(g):
            ctx.wait(g, "o")
            g.nop()
        ctx.block.gpsimd(fin)


def emit_p8b1(ctx):
    nc = ctx.nc
    with ExitStack() as es:
        def sb(name, shape, dt):
            return es.enter_context(nc.sbuf_tensor("p8b1_%s" % name, shape, dt))
        def psum(name, shape):
            return es.enter_context(nc.psum_tensor("p8b1_%s" % name, shape, F32))
        w3t = sb("w3t", [128, 8, TT], BF16)
        tg = sb("tg", [128, TT], F32)
        hht = sb("hht", [128, 24, TT], BF16)
        wg = sb("wg", [128, 8, 24 * 128], BF16)
        wu = sb("wu", [128, 8, 24 * 128], BF16)
        stg = sb("stg", [128, 1024], I32)
        psK = [psum("ps%d" % i, [128, TT]) for i in range(4)]
        load_weights_bf16(ctx, wg.ap(), ctx.io["wgT"], DM, 3072, stg, "wg", mcap=1024)
        load_weights_bf16(ctx, wu.ap(), ctx.io["wuT"], DM, 3072, stg, "wu", mcap=1024)
        wg_v = wg.ap().rearrange("p k (f j) -> p k f j", j=128)
        wu_v = wu.ap().rearrange("p k (f j) -> p k f j", j=128)
        SC = 1.0 / 127.0
        evk, mmk, muk = {}, {}, {}
        for it in range(NTILE):
            t0 = it * TT
            def dma_in(sync, t0=t0):
                ctx.wait(sync, "v", "a", "p")
                ctx.inc(sync.dma_start(out=w3t[:], in_=ctx.io["w3"].rearrange("(c p) t -> p c t", p=128)[:, :, t0:t0 + TT]), "d", 16)
            ctx.block.sync(dma_in)
            dmm = ctx.mark("d")
            for fo in range(24):
                jg, ju = (fo % 2) * 2, (fo % 2) * 2 + 1
                gfo = it * 24 + fo
                def mm(t, fo=fo, jg=jg, ju=ju, gfo=gfo, dmm=dmm):
                    ctx.wait_val(t, "d", dmm)
                    if (gfo - 2) in evk:
                        ctx.wait_val(t, "a", evk[gfo - 2])
                    if (gfo - 2) in muk:
                        ctx.wait_val(t, "v", muk[gfo - 2])
                    for ki in range(8):
                        ins = t.matmul(psK[jg][:], wg_v[:, ki, fo, :], w3t[:, ki, :],
                                       start=(ki == 0), stop=(ki == 7))
                    for ki in range(8):
                        ins = t.matmul(psK[ju][:], wu_v[:, ki, fo, :], w3t[:, ki, :],
                                       start=(ki == 0), stop=(ki == 7))
                    ctx.inc(ins, "p")
                    mmk[gfo] = ctx.mark("p")
                ctx.block.tensor(mm)
                def act_g(sc, fo=fo, jg=jg, gfo=gfo):
                    ctx.wait_val(sc, "p", mmk[gfo])
                    if (gfo - 1) in muk:
                        ctx.wait_val(sc, "v", muk[gfo - 1])
                    ctx.inc(sc.activation(out=tg[:], in_=psK[jg][:], func=AF.Silu, scale=SC), "a")
                    evk[gfo] = ctx.mark("a")
                ctx.block.scalar(act_g)
                def mul_u(v, fo=fo, ju=ju, gfo=gfo):
                    ctx.wait_val(v, "a", evk[gfo])
                    ctx.wait(v, "o")
                    ctx.inc(v.scalar_tensor_tensor(hht[:, fo, :], psK[ju][:], SC, tg[:], OP.mult, OP.mult), "v")
                    muk[gfo] = ctx.mark("v")
                ctx.block.vector(mul_u)
            def dma_out(g, t0=t0):
                ctx.wait(g, "v")
                ctx.inc(g.dma_start(out=ctx.io["hh"].rearrange("(c p) t -> p c t", p=128)[:, :, t0:t0 + TT], in_=hht[:]), "o", 16)
            ctx.block.gpsimd(dma_out)
        def fin(g):
            ctx.wait(g, "o")
            g.nop()
        ctx.block.gpsimd(fin)


def emit_p8b2(ctx):
    nc = ctx.nc
    with ExitStack() as es:
        def sb(name, shape, dt):
            return es.enter_context(nc.sbuf_tensor("p8b2_%s" % name, shape, dt))
        def psum(name, shape):
            return es.enter_context(nc.psum_tensor("p8b2_%s" % name, shape, F32))
        hht = sb("hht", [128, 24, TT], BF16)
        x3t = sb("x3t", [128, 8, TT], F32)
        f = sb("f", [128, 8, TT], F32)
        fbf = sb("fbf", [128, 8, TT], BF16)
        sq = sb("sq", [128, 8, TT], F32)
        fn = sb("fn", [128, 8, TT], F32)
        st = sb("st", [1, 7 * TT], F32)
        tmp = sb("tmp", [128, TT], F32)
        ones_bf = sb("onesb", [128, 1], BF16)
        ones_f32 = sb("onesf", [128, 1], F32)
        ones1 = sb("ones1", [1, 128], F32)
        cns = sb("cns", [128, 4], F32)
        wd = sb("wd", [128, 24, 8 * 128], BF16)
        stg = sb("stg", [128, 1024], I32)
        psA = psum("psA", [1, TT]); psB = psum("psB", [1, TT])
        psR = psum("psR", [128, TT]); psM = psum("psM", [128, TT])
        psK = [psum("ps%d" % i, [128, TT]) for i in range(4)]
        def init(v):
            v.memset(ones_bf[:], 1.0)
            v.memset(ones_f32[:], 1.0)
            v.memset(cns[:, 0:1], EPS_LN)
            ctx.inc(v.memset(ones1[:], 1.0), "v")
        ctx.block.vector(init)
        load_weights_bf16(ctx, wd.ap(), ctx.io["wdT"], H3, 1024, stg, "wd")
        wd_v = wd.ap().rearrange("p k (f j) -> p k f j", j=128)
        evk, mmk = {}, {}
        for it in range(NTILE):
            t0 = it * TT
            def dma_in(sync, t0=t0):
                ctx.wait(sync, "v", "a", "p")
                ctx.inc(sync.dma_start(out=hht[:], in_=ctx.io["hh"].rearrange("(c p) t -> p c t", p=128)[:, :, t0:t0 + TT]), "d", 16)
                ctx.inc(sync.dma_start(out=x3t[:], in_=ctx.io["x3"].rearrange("(c p) t -> p c t", p=128)[:, :, t0:t0 + TT]), "d", 16)
            ctx.block.sync(dma_in)
            dmm = ctx.mark("d")
            avm = ctx.mark("a")
            for fo in range(8):
                j = fo % 4
                gfo = it * 8 + fo
                def mm(t, fo=fo, j=j, gfo=gfo, dmm=dmm, avm=avm):
                    ctx.wait_val(t, "d", dmm)
                    ctx.wait_val(t, "a", avm if (gfo - 4) not in evk else max(avm, evk[gfo - 4]))
                    for ki in range(24):
                        ins = t.matmul(psK[j][:], wd_v[:, ki, fo, :], hht[:, ki, :],
                                       start=(ki == 0), stop=(ki == 23))
                    ctx.inc(ins, "p")
                    mmk[gfo] = ctx.mark("p")
                ctx.block.tensor(mm)
                def evac(sc, fo=fo, j=j, gfo=gfo):
                    ctx.wait_val(sc, "p", mmk[gfo])
                    ctx.wait(sc, "v", "o")
                    ctx.inc(sc.copy(out=f[:, fo, :], in_=psK[j][:]), "a")
                    evk[gfo] = ctx.mark("a")
                ctx.block.scalar(evac)
            rstd, m2 = emit_ln_stats(ctx, f.ap(), 8, TT, fbf.ap(), sq.ap(), psA, psB, st.ap(),
                                     ones_bf, ones_f32, cns[0:1, 0:1])
            emit_ln_normalize(ctx, f.ap(), 8, TT, fn.ap(), rstd, m2, psR, psM, ones1, tmp.ap())
            def resid(v):
                ctx.wait(v, "d", "o")
                ctx.inc(v.tensor_tensor(sq[:], fn[:], x3t[:], OP.add), "v")
            ctx.block.vector(resid)
            def dma_out(g, t0=t0):
                ctx.wait(g, "v")
                ctx.inc(g.dma_start(out=ctx.io["y"].rearrange("(c p) t -> p c t", p=128)[:, :, t0:t0 + TT], in_=sq[:]), "o", 16)
            ctx.block.gpsimd(dma_out)
        def fin(g):
            ctx.wait(g, "o")
            g.nop()
        ctx.block.gpsimd(fin)


PHASE_EMIT = {
    "p1": emit_p1,
    "p2f": lambda ctx: emit_p2(ctx, "f"), "p2b": lambda ctx: emit_p2(ctx, "b"),
    "p3f": lambda ctx: emit_p3(ctx, "f"), "p3b": lambda ctx: emit_p3(ctx, "b"),
    "p4f": lambda ctx: emit_p4(ctx, "f"), "p4b": lambda ctx: emit_p4(ctx, "b"),
    "p45a": emit_p45a, "p45b": emit_p45b,
    "p5": emit_p5, "p6": emit_p6, "p7": emit_p7,
    "p8a": emit_p8a, "p8b1": emit_p8b1, "p8b2": emit_p8b2,
}


def build_phase_nc(phase, core_input_names, fused=False):
    nc = bass.Bass("TRN2", target_bir_lowering=False)
    io = {}
    phases = phase if isinstance(phase, list) else [phase]
    wts, sins, souts = set(), set(), set()
    for ph in phases:
        w, si, so = PHASE_IO[ph]
        wts |= set(w); sins |= set(si) - souts; souts |= set(so)
    for w in sorted(wts):
        shape, dt = SHARED_SPECS[w]
        shape = (shape[0], 1) if len(shape) == 1 else shape
        io[w] = nc.dram_tensor(w, list(shape), dt, kind="ExternalInput").ap()
    if fused:
        io["x"] = nc.dram_tensor("x", [DM, SEQ], F32, kind="ExternalInput").ap()
        io["y"] = nc.dram_tensor("y", [DM, SEQ], F32, kind="ExternalOutput").ap()
        for s in sorted((sins | souts) - {"x", "y"}):
            shape, dt = SCRATCH_SPECS[s]
            io[s] = nc.dram_tensor(s, list(shape), dt, kind="Internal").ap()
    else:
        for s in sorted(sins):
            if s == "x":
                io["x"] = nc.dram_tensor("x", [DM, SEQ], F32, kind="ExternalInput").ap()
            else:
                shape, dt = SCRATCH_SPECS[s]
                io[s] = nc.dram_tensor(s, list(shape), dt, kind="ExternalInput").ap()
        for s in sorted(souts):
            if s == "y":
                io["y"] = nc.dram_tensor("y", [DM, SEQ], F32, kind="ExternalOutput").ap()
            else:
                shape, dt = SCRATCH_SPECS[s]
                io[s] = nc.dram_tensor(s, list(shape), dt, kind="ExternalOutput").ap()
    with ExitStack() as stack:
        sems = {k: stack.enter_context(nc.semaphore("s_" + k)) for k in SEM_NAMES}
        block = stack.enter_context(nc.Block())
        ctx = Ctx(nc, block, sems, io)
        for i, ph in enumerate(phases):
            if i > 0:
                ctx.barrier()
            PHASE_EMIT[ph](ctx)
    return nc, sorted(wts), sorted(sins), sorted(souts)


PHASES = ["p1", "p2f", "p2b", "p3f", "p3b", "p4f", "p4b", "p45a", "p45b", "p5", "p6", "p7", "p8a", "p8b1", "p8b2"]


def run_phases(x_cores, shared):
    ncore = len(x_cores)
    state = [{"x": x_cores[i]} for i in range(ncore)]
    for ph in PHASES:
        nc, wts, sins, souts = build_phase_nc(ph, None)
        in_maps = []
        for i in range(ncore):
            m = {}
            for w in wts:
                a = shared[w]
                m[w] = a.reshape(-1, 1) if a.ndim == 1 else a
            for s in sins:
                m[s] = state[i][s]
            in_maps.append(m)
        res = run_bass_kernel_spmd(nc, in_maps, core_ids=list(range(ncore)))
        for i in range(ncore):
            for s in souts:
                state[i][s] = np.asarray(res.results[i][s])
        later = set()
        for ph2 in PHASES[PHASES.index(ph) + 1:]:
            later |= set(PHASE_IO[ph2][1])
        for i in range(ncore):
            for s in [k for k in state[i] if k not in later and k != "y"]:
                del state[i][s]
    return state


def run_fused(x_cores, shared):
    nc, wts, sins, souts = build_phase_nc(PHASES, None, fused=True)
    in_maps = []
    for i in range(len(x_cores)):
        m = {}
        for w in wts:
            a = shared[w]
            m[w] = np.ascontiguousarray(a.reshape(-1, 1)) if a.ndim == 1 else a
        m["x"] = x_cores[i]
        in_maps.append(m)
    res = run_bass_kernel_spmd(nc, in_maps, core_ids=list(range(len(x_cores))))
    return [np.asarray(r["y"]) for r in res.results]


def kernel(**inputs):
    shared = _prep_shared(inputs)
    x = inputs["L0_attn_x"]
    x_cores = [np.ascontiguousarray(x[b].T) for b in range(8)]
    if FUSED:
        ys = run_fused(x_cores, shared)
    else:
        state = run_phases(x_cores, shared)
        ys = [state[b]["y"] for b in range(8)]
    out = np.stack([y.T for y in ys], axis=0)
    return np.ascontiguousarray(out)


# version 06
# speedup vs baseline: 1.1753x; 1.1753x over previous
import numpy as np
import ml_dtypes
import concourse.bass as bass
import concourse.mybir as mybir
from concourse.bass_utils import run_bass_kernel_spmd
from concourse.alu_op_type import AluOpType as OP
from contextlib import ExitStack

F32 = mybir.dt.float32
BF16 = mybir.dt.bfloat16
U8 = mybir.dt.uint8
I32 = mybir.dt.int32
AF = mybir.ActivationFunctionType
BF = ml_dtypes.bfloat16

DM = 1024
SEQ = 8192
DI = 2048
NST = 16
DTR = 64
H4 = 4096
H3 = 3072
TT = 512
NTILE = SEQ // TT
EPS_LN = 1e-5
EPS_L1 = 1e-8

FUSED = True

def _prep_shared(inp):
    s = {}
    for k in ["L0_attn_in_proj_w", "L0_attn_conv_w", "L0_attn_conv_b",
              "L0_attn_x_proj_w", "L0_attn_dt_proj_w", "L0_attn_dt_proj_b",
              "L0_attn_A_log", "L0_attn_D", "L0_attn_out_proj_w",
              "L0_attn_ln_g", "L0_attn_ln_b", "L0_attn_ff_w1", "L0_attn_ff_b1",
              "L0_attn_ff_w2", "L0_attn_ff_b2", "L0_ffn_mu_r", "L0_ffn_mu_k"]:
        s[k.split("L0_attn_")[-1] if k.startswith("L0_attn_") else k] = np.ascontiguousarray(inp[k])
    s["WrT"] = np.ascontiguousarray(inp["L0_ffn_W_r"].T)
    s["WkT"] = np.ascontiguousarray(inp["L0_ffn_W_k"].T)
    s["WvT"] = np.ascontiguousarray(inp["L0_ffn_W_v"].T)
    s["L1WT"] = np.ascontiguousarray(inp["L1_attn_W"].T)
    s["wgT"] = np.ascontiguousarray(inp["L1_ffn_w_gate"].T)
    s["wuT"] = np.ascontiguousarray(inp["L1_ffn_w_up"].T)
    s["wdT"] = np.ascontiguousarray(inp["L1_ffn_w_down"].T)
    for k in ["conv_b", "dt_proj_b", "D", "ln_b", "ff_b2", "ln_g", "ff_b1",
              "L0_ffn_mu_r", "L0_ffn_mu_k"]:
        s[k] = np.ascontiguousarray(s[k].reshape(-1, 128).T)
    s["conv_w"] = np.ascontiguousarray(s["conv_w"].reshape(16, 128, 4).transpose(1, 0, 2))
    s["A_log"] = np.ascontiguousarray(s["A_log"].reshape(16, 128, 16).transpose(1, 0, 2))
    s["oh16"] = np.repeat(np.eye(16, dtype=np.float32), 128, axis=1).astype(BF)
    s["id128"] = np.eye(128, dtype=np.float32).astype(BF)
    return s

SHARED_SPECS = {
    "in_proj_w": ((DM, 2 * DI), F32), "conv_w": ((128, 16, 4), F32), "conv_b": ((128, 16), F32),
    "x_proj_w": ((DI, 96), F32), "dt_proj_w": ((DTR, DI), F32), "dt_proj_b": ((128, 16), F32),
    "A_log": ((128, 16, 16), F32), "D": ((128, 16), F32), "out_proj_w": ((DI, DM), F32),
    "ln_g": ((128, 8), F32), "ln_b": ((128, 8), F32), "ff_w1": ((DM, H4), F32), "ff_b1": ((128, 32), F32),
    "ff_w2": ((H4, DM), F32), "ff_b2": ((128, 8), F32), "L0_ffn_mu_r": ((128, 8), F32),
    "L0_ffn_mu_k": ((128, 8), F32), "WrT": ((DM, DM), F32), "WkT": ((DM, DM), F32),
    "WvT": ((DM, DM), F32), "L1WT": ((DM, H3), F32), "wgT": ((DM, H3), I32),
    "wuT": ((DM, H3), I32), "wdT": ((H3, DM), I32),
    "oh16": ((16, 2048), BF16), "id128": ((128, 128), BF16),
}

SCRATCH_SPECS = {
    "xc": ((DI, SEQ), F32), "z": ((DI, SEQ), BF16),
    "xcs_f": ((DI, SEQ), F32), "xcs_b": ((DI, SEQ), F32),
    "delta_f": ((DI, SEQ), F32), "delta_b": ((DI, SEQ), F32),
    "u_f": ((DI, SEQ), BF16), "u_b": ((DI, SEQ), BF16),
    "Bt_f": ((16, SEQ), BF16), "Bt_b": ((16, SEQ), BF16),
    "Ct_f": ((16, SEQ), BF16), "Ct_b": ((16, SEQ), BF16),
    "ys_f": ((DI, SEQ), BF16), "ys_b": ((DI, SEQ), BF16),
    "mf": ((DM, SEQ), F32), "x1": ((DM, SEQ), F32), "x2": ((DM, SEQ), F32),
    "lf": ((DM, SEQ), F32), "lv": ((DM, SEQ), F32), "ht": ((DM, SEQ), F32),
    "mo": ((DM, SEQ), F32), "g1": ((H4, SEQ), BF16), "w3": ((DM, SEQ), BF16),
    "x3": ((DM, SEQ), F32), "hh": ((H3, SEQ), BF16),
}

PHASE_IO = {
    "p1":  (["in_proj_w"], ["x"], ["xc", "z"]),
    "p2f": (["conv_w", "conv_b", "x_proj_w", "dt_proj_w", "dt_proj_b"], ["xc"],
            ["xcs_f", "delta_f", "u_f", "Bt_f", "Ct_f"]),
    "p2b": (["conv_w", "conv_b", "x_proj_w", "dt_proj_w", "dt_proj_b"], ["xc"],
            ["xcs_b", "delta_b", "u_b", "Bt_b", "Ct_b"]),
    "p3f": (["A_log", "oh16", "id128"], ["delta_f", "u_f", "Bt_f", "Ct_f"], ["ys_f"]),
    "p3b": (["A_log", "oh16", "id128"], ["delta_b", "u_b", "Bt_b", "Ct_b"], ["ys_b"]),
    "p4f": (["out_proj_w", "D"], ["ys_f", "xcs_f", "z"], ["mf"]),
    "p4b": (["out_proj_w", "D"], ["ys_b", "xcs_b", "z", "mf"], ["mo"]),
    "p45a": (["ff_w1", "ff_b1", "ln_g", "ln_b"], ["mo"], ["g1"]),
    "p45b": (["ff_w2", "ff_b2"], ["g1", "x"], ["x1"]),
    "p5": (["WrT", "WkT", "WvT", "L0_ffn_mu_r", "L0_ffn_mu_k"], ["x1"], ["x2"]),
    "p6": (["L1WT"], ["x2"], ["lf", "lv"]),
    "p7": ([], ["lf", "lv"], ["ht"]),
    "p8a": ([], ["x2", "ht"], ["x3", "w3"]),
    "p8b1": (["wgT", "wuT"], ["w3"], ["hh"]),
    "p8b2": (["wdT"], ["hh", "x3"], ["y"]),
}

NP_DT = {F32: np.float32, BF16: BF, I32: np.int32, U8: np.uint8}


class Ctx:
    def __init__(self, nc, block, sems, io):
        self.nc = nc
        self.block = block
        self.sems = sems
        self.cnt = {k: 0 for k in sems}
        self.io = io
        self.sb = {}

    def inc(self, instr, sem, n=1):
        instr.then_inc(self.sems[sem], n)
        self.cnt[sem] += n

    def wait(self, eng, *sems):
        for s in sems:
            if self.cnt[s] > 0:
                eng.wait_ge(self.sems[s], self.cnt[s])

    def wait_val(self, eng, sem, val):
        if val > 0:
            eng.wait_ge(self.sems[sem], val)

    def mark(self, sem):
        return self.cnt[sem]

    def barrier(self):
        cnts = dict(self.cnt)
        for eng_name in ["sync", "vector", "scalar", "tensor", "gpsimd"]:
            def bar(eng, cnts=cnts):
                for s, c in cnts.items():
                    if c > 0:
                        eng.wait_ge(self.sems[s], c)
            getattr(self.block, eng_name)(bar)


SEM_NAMES = ["d", "o", "v", "a", "p", "g"]


def load_weights_bf16(ctx, dst, src_ap, K, M, stg, name, mcap=4096):
    nc = ctx.nc
    nk = K // 128
    for ki in range(nk):
        for c0 in range(0, M, mcap):
            mc = min(mcap, M - c0)
            def dma(sync, ki=ki, c0=c0, mc=mc):
                ctx.wait(sync, "a")
                ctx.inc(sync.dma_start(out=stg[:, :mc], in_=src_ap[ki * 128:(ki + 1) * 128, c0:c0 + mc]), "d", 16)
            ctx.block.sync(dma)
            def cast(sc, ki=ki, c0=c0, mc=mc):
                ctx.wait(sc, "d")
                ctx.inc(sc.copy(out=dst[:, ki, c0:c0 + mc], in_=stg[:, :mc]), "a")
            ctx.block.scalar(cast)


def emit_ln_stats(ctx, xa_view, nch, T, xbf_view, sq, psA, psB, st, ones_bf, ones_f32, eps_ap):
    nc = ctx.nc
    def cast(v):
        ctx.wait(v, "d", "p", "a")
        ctx.inc(v.tensor_copy(xbf_view[:], xa_view[:]), "v")
    ctx.block.vector(cast)
    def square(sc):
        ctx.wait(sc, "d", "p", "v", "o")
        ctx.inc(sc.activation(out=sq[:], in_=xa_view[:], func=AF.Square), "a")
    ctx.block.scalar(square)
    def mms(t):
        ctx.wait(t, "v", "a")
        for c in range(nch):
            ins = t.matmul(psA[:], ones_bf[:, 0:1], xbf_view[:, c, :], start=(c == 0), stop=(c == nch - 1))
        ctx.inc(ins, "p")
        for c in range(nch):
            ins = t.matmul(psB[:], ones_f32[:, 0:1], sq[:, c, :], start=(c == 0), stop=(c == nch - 1))
        ctx.inc(ins, "p")
    ctx.block.tensor(mms)
    inv = 1.0 / (nch * 128)
    mu, ms, mu2, var, sd = (st[:, 2 * T:3 * T], st[:, 3 * T:4 * T], st[:, 4 * T:5 * T],
                            st[:, 5 * T:6 * T], st[:, 6 * T:7 * T])
    def stats_a(sc):
        ctx.wait(sc, "p")
        sc.activation(out=mu, in_=psA[:], func=AF.Copy, scale=inv)
        ctx.inc(sc.activation(out=ms, in_=psB[:], func=AF.Copy, scale=inv), "a")
    ctx.block.scalar(stats_a)
    def stats_v(v):
        ctx.wait(v, "a")
        v.tensor_tensor(mu2, mu, mu, OP.mult)
        ctx.inc(v.tensor_tensor(var, ms, mu2, OP.subtract), "v")
    ctx.block.vector(stats_v)
    rstd, m2 = st[:, 0:T], st[:, T:2 * T]
    def stats_a2(sc):
        ctx.wait(sc, "v")
        sc.activation(out=sd, in_=var, func=AF.Ln, bias=eps_ap)
        ctx.inc(sc.activation(out=rstd, in_=sd, func=AF.Exp, scale=-0.5), "a")
    ctx.block.scalar(stats_a2)
    def stats_v2(v):
        ctx.wait(v, "a")
        ctx.inc(v.tensor_tensor(m2, mu, rstd, OP.mult), "v")
    ctx.block.vector(stats_v2)
    return rstd, m2


def emit_ln_normalize(ctx, xa_view, nch, T, out_view, rstd, m2, psR, psM, ones1_f32, tmp,
                      gain=None, bias_=None):
    def bcast(t):
        ctx.wait(t, "v")
        t.matmul(psR[:], ones1_f32[:, :], rstd, start=True, stop=True)
        ins = t.matmul(psM[:], ones1_f32[:, :], m2, start=True, stop=True)
        ctx.inc(ins, "p")
    ctx.block.tensor(bcast)
    def norm(v):
        ctx.wait(v, "p")
        for c in range(nch):
            v.tensor_tensor(tmp[:], xa_view[:, c, :], psR[:], OP.mult)
            if gain is None:
                ins = v.tensor_tensor(out_view[:, c, :], tmp[:], psM[:], OP.subtract)
            else:
                v.tensor_tensor(tmp[:], tmp[:], psM[:], OP.subtract)
                ins = v.tensor_scalar(out_view[:, c, :], tmp[:], gain[c], bias_[c], OP.mult, OP.add)
        ctx.inc(ins, "v")
    ctx.block.vector(norm)


def emit_p1(ctx):
    nc = ctx.nc
    x = ctx.io["x"]
    xcd, zd = ctx.io["xc"], ctx.io["z"]
    wsrc = ctx.io["in_proj_w"]
    with ExitStack() as es:
        def sb(name, shape, dt):
            return es.enter_context(nc.sbuf_tensor(name, shape, dt))
        def psum(name, shape):
            return es.enter_context(nc.psum_tensor(name, shape, F32))
        xa = [sb("p1_xa%d" % i, [128, 8, TT], F32) for i in range(2)]
        sq = [sb("p1_sq%d" % i, [128, 8, TT], F32) for i in range(2)]
        xbf = [sb("p1_xbf%d" % i, [128, 8, TT], BF16) for i in range(2)]
        xln = [sb("p1_xln%d" % i, [128, 8, TT], BF16) for i in range(2)]
        st = sb("p1_st", [1, 7 * TT], F32)
        tmp = sb("p1_tmp", [128, TT], F32)
        ones_bf = sb("p1_onesb", [128, 1], BF16)
        ones_f32 = sb("p1_onesf", [128, 1], F32)
        ones1 = sb("p1_ones1", [1, 128], F32)
        cns = sb("p1_cns", [128, 4], F32)
        wip = sb("p1_w", [128, 8, 32 * 128], BF16)
        stg = sb("p1_stg", [128, 2048], F32)
        ev = sb("p1_ev", [128, 4, TT], F32)
        psA = psum("p1_psA", [1, TT])
        psB = psum("p1_psB", [1, TT])
        psR = psum("p1_psR", [128, TT])
        psM = psum("p1_psM", [128, TT])
        psK = [psum("p1_ps%d" % i, [128, TT]) for i in range(4)]
        def init(v):
            v.memset(ones_bf[:], 1.0)
            v.memset(ones_f32[:], 1.0)
            v.memset(cns[:, 0:1], EPS_LN)
            ctx.inc(v.memset(ones1[:], 1.0), "v")
        ctx.block.vector(init)
        load_weights_bf16(ctx, wip.ap(), wsrc, DM, 4096, stg, "wip", mcap=2048)
        wip_v = wip.ap().rearrange("p k (f j) -> p k f j", j=128)
        tile_vmark = {}
        tile_amark = {}
        evk, dmk, mmk = {}, {}, {}
        for it in range(NTILE):
            t0 = it * TT
            tb = it % 2
            def dma_in(sync, t0=t0, it=it, tb=tb):
                if it >= 2:
                    ctx.wait_val(sync, "v", tile_vmark[it - 2])
                    ctx.wait_val(sync, "a", tile_amark[it - 2])
                ctx.inc(sync.dma_start(
                    out=xa[tb][:], in_=x.rearrange("(c p) t -> p c t", p=128)[:, :, t0:t0 + TT]), "d", 16)
            ctx.block.sync(dma_in)
            rstd, m2 = emit_ln_stats(ctx, xa[tb].ap(), 8, TT, xbf[tb].ap(), sq[tb].ap(), psA, psB, st.ap(),
                                     ones_bf, ones_f32, cns[0:1, 0:1])
            emit_ln_normalize(ctx, xa[tb].ap(), 8, TT, xln[tb].ap(), rstd, m2, psR, psM, ones1, tmp.ap())
            tile_vmark[it] = ctx.mark("v")
            tile_amark[it] = ctx.mark("a")
            nvm = tile_vmark[it]
            for fo in range(32):
                j = fo % 4
                gfo = it * 32 + fo
                def mm(t, fo=fo, j=j, gfo=gfo, nvm=nvm, tb=tb):
                    ctx.wait_val(t, "v", nvm)
                    if (gfo - 4) in evk:
                        ctx.wait_val(t, "a", evk[gfo - 4])
                    for ki in range(8):
                        ins = t.matmul(psK[j][:], wip_v[:, ki, fo, :], xln[tb][:, ki, :],
                                       start=(ki == 0), stop=(ki == 7))
                    ctx.inc(ins, "p")
                    mmk[gfo] = ctx.mark("p")
                ctx.block.tensor(mm)
                def evac(sc, fo=fo, j=j, gfo=gfo):
                    ctx.wait_val(sc, "p", mmk[gfo])
                    if (gfo - 4) in dmk:
                        ctx.wait_val(sc, "o", dmk[gfo - 4])
                    ctx.inc(sc.copy(out=ev[:, fo % 4, :], in_=psK[j][:]), "a")
                    evk[gfo] = ctx.mark("a")
                ctx.block.scalar(evac)
                def dma_out(g, fo=fo, t0=t0, gfo=gfo):
                    ctx.wait_val(g, "a", evk[gfo])
                    dst = xcd if fo < 16 else zd
                    r0 = (fo % 16) * 128
                    ctx.inc(g.dma_start(out=dst[r0:r0 + 128, t0:t0 + TT], in_=ev[:, fo % 4, :]), "o", 16)
                    dmk[gfo] = ctx.mark("o")
                ctx.block.gpsimd(dma_out)
        def fin(g):
            ctx.wait(g, "o")
            g.nop()
        ctx.block.gpsimd(fin)


def emit_p2(ctx, dirn):
    nc = ctx.nc
    xcd = ctx.io["xc"]
    xcs_d, delta_d, u_d = ctx.io["xcs_" + dirn], ctx.io["delta_" + dirn], ctx.io["u_" + dirn]
    Bt_d, Ct_d = ctx.io["Bt_" + dirn], ctx.io["Ct_" + dirn]
    with ExitStack() as es:
        def sb(name, shape, dt):
            return es.enter_context(nc.sbuf_tensor("p2%s_%s" % (dirn, name), shape, dt))
        def psum(name, shape):
            return es.enter_context(nc.psum_tensor("p2%s_%s" % (dirn, name), shape, F32))
        xe = [sb("xe%d" % i, [128, 16, TT + 3], F32) for i in range(2)]
        pb = [sb("pb%d" % i, [128, 4, TT], F32) for i in range(2)]
        xcs = sb("xcs", [128, 16, TT], F32)
        xcsb = sb("xcsb", [128, 16, TT], BF16)
        acc = sb("acc", [128, TT], F32)
        acc2 = sb("acc2", [128, TT], F32)
        accR = [sb("accR%d" % i, [128, TT], F32) for i in range(2)]
        delta = sb("delta", [128, 16, TT], F32)
        ub = sb("ub", [128, 16, TT], BF16)
        dtb = sb("dtb", [64, TT], BF16)
        bct = sb("bct", [16, 2, TT], BF16)
        cw = sb("cw", [128, 16, 4], F32)
        cb = sb("cb", [128, 16], F32)
        dpb = sb("dpb", [128, 16], F32)
        onep = sb("onep", [128, 1], F32)
        xpw = sb("xpw", [128, 16, 96], BF16)
        dtw = sb("dtw", [64, 16, 128], BF16)
        stg = sb("stg", [128, 2048], F32)
        psDT = psum("psDT", [64, TT])
        psB16 = psum("psB16", [16, TT])
        psC16 = psum("psC16", [16, TT])
        psK = [psum("ps%d" % i, [128, TT]) for i in range(4)]
        def dma_par(sync):
            ctx.inc(sync.dma_start(out=cw[:], in_=ctx.io["conv_w"]), "d", 16)
            ctx.inc(sync.dma_start(out=cb[:], in_=ctx.io["conv_b"]), "d", 16)
            ctx.inc(sync.dma_start(out=dpb[:], in_=ctx.io["dt_proj_b"]), "d", 16)
        ctx.block.sync(dma_par)
        def init_v(v):
            ctx.inc(v.memset(onep[:], 1.0), "v")
        ctx.block.vector(init_v)
        load_weights_bf16(ctx, xpw.ap(), ctx.io["x_proj_w"], DI, 96, stg, "xpw")
        def dma_dtw(sync):
            ctx.wait(sync, "a")
            ctx.inc(sync.dma_start(out=stg[0:64, :2048], in_=ctx.io["dt_proj_w"]), "d", 16)
        ctx.block.sync(dma_dtw)
        def cast_dtw(sc):
            ctx.wait(sc, "d")
            ctx.inc(sc.copy(out=dtw.ap().rearrange("p a b -> p (a b)"), in_=stg[0:64, :2048]), "a")
        ctx.block.scalar(cast_dtw)

        dmark = {}
        silu_mark = {}
        silu_cmark = {}
        conv_mark = {}
        cast_mark = {}
        mmxp_mark = {}
        evxp_mark = {}
        bc_mark = {}
        u_mark = {}
        st_mark = {}
        sp_mark = {}
        for it in range(NTILE):
            t0 = it * TT
            tb = it % 2
            if dirn == "f":
                lo, hi = t0 - 3, t0 + TT
                pad_lo = max(0, -lo)
                pad_hi = 0
                shifts = [0, 1, 2, 3]
            else:
                lo, hi = t0, t0 + TT + 3
                pad_lo = 0
                pad_hi = max(0, hi - SEQ)
                shifts = [3, 2, 1, 0]
            def dma_in(sync, it=it, tb=tb, lo=lo, hi=hi, pad_lo=pad_lo, pad_hi=pad_hi):
                if (it - 2) in sp_mark:
                    ctx.wait_val(sync, "v", sp_mark[it - 2])
                if dirn == "f":
                    ctx.inc(sync.dma_start(
                        out=xe[tb][:, :, pad_lo:TT + 3],
                        in_=xcd.rearrange("(c p) t -> p c t", p=128)[:, :, lo + pad_lo:hi]), "d", 16)
                else:
                    ctx.inc(sync.dma_start(
                        out=xe[tb][:, :, 0:TT + 3 - pad_hi],
                        in_=xcd.rearrange("(c p) t -> p c t", p=128)[:, :, lo:hi - pad_hi]), "d", 16)
                dmark[it] = ctx.mark("d")
            ctx.block.sync(dma_in)
            if pad_lo or pad_hi:
                def pad(g, it=it, tb=tb, pad_lo=pad_lo, pad_hi=pad_hi):
                    ctx.wait(g, "v", "a", "p")
                    if dirn == "f":
                        ctx.inc(g.memset(xe[tb][:, :, 0:pad_lo], 0.0), "g")
                    else:
                        ctx.inc(g.memset(xe[tb][:, :, TT + 3 - pad_hi:TT + 3], 0.0), "g")
                ctx.block.gpsimd(pad)
            for c in range(16):
                def conv(v, c=c, it=it, tb=tb, shifts=shifts):
                    ctx.wait_val(v, "d", dmark[it])
                    ctx.wait(v, "g")
                    key = (it, c - 2) if c >= 2 else (it - 1, 14 + c)
                    if key in silu_cmark:
                        ctx.wait_val(v, "a", silu_cmark[key])
                    if c == 0 and (it - 1) in u_mark:
                        ctx.wait_val(v, "v", u_mark[it - 1])
                    if c == 0 and (it - 1) in st_mark:
                        ctx.wait_val(v, "o", st_mark[it - 1])
                    s = shifts
                    for k in range(4):
                        v.tensor_scalar(pb[tb][:, k, :], xe[tb][:, c, s[k]:s[k] + TT], cw[:, c, k:k + 1], None, OP.mult)
                    v.tensor_tensor(acc[:], pb[tb][:, 0, :], pb[tb][:, 1, :], OP.add)
                    v.tensor_tensor(acc2[:], pb[tb][:, 2, :], pb[tb][:, 3, :], OP.add)
                    ctx.inc(v.tensor_tensor(accR[c % 2][:], acc[:], acc2[:], OP.add), "v")
                    conv_mark[(it, c)] = ctx.mark("v")
                ctx.block.vector(conv)
                def silu(sc, c=c, it=it, tb=tb):
                    ctx.wait_val(sc, "v", conv_mark[(it, c)])
                    ctx.inc(sc.activation(out=xcs[:, c, :], in_=accR[c % 2][:], func=AF.Silu, bias=cb[:, c:c + 1]), "a")
                    silu_cmark[(it, c)] = ctx.mark("a")
                    if c == 15:
                        silu_mark[it] = ctx.mark("a")
                ctx.block.scalar(silu)
            def cast_xcs(v, it=it):
                ctx.wait_val(v, "a", silu_mark[it])
                if (it - 1) in mmxp_mark:
                    ctx.wait_val(v, "p", mmxp_mark[it - 1])
                ctx.inc(v.tensor_copy(xcsb[:], xcs[:]), "v")
                cast_mark[it] = ctx.mark("v")
            ctx.block.vector(cast_xcs)
            def mm_xp(t, it=it):
                ctx.wait_val(t, "v", cast_mark[it])
                if (it - 1) in evxp_mark:
                    ctx.wait_val(t, "a", evxp_mark[it - 1])
                for ki in range(16):
                    ins = t.matmul(psDT[:], xpw[:, ki, 0:64], xcsb[:, ki, :], start=(ki == 0), stop=(ki == 15))
                for ki in range(16):
                    ins2 = t.matmul(psB16[:], xpw[:, ki, 64:80], xcsb[:, ki, :], start=(ki == 0), stop=(ki == 15))
                for ki in range(16):
                    ins3 = t.matmul(psC16[:], xpw[:, ki, 80:96], xcsb[:, ki, :], start=(ki == 0), stop=(ki == 15))
                ctx.inc(ins3, "p")
                mmxp_mark[it] = ctx.mark("p")
            ctx.block.tensor(mm_xp)
            def evac_xp(sc, it=it):
                ctx.wait_val(sc, "p", mmxp_mark[it])
                if (it - 1) in bc_mark:
                    ctx.wait_val(sc, "o", bc_mark[it - 1])
                sc.copy(out=dtb[:], in_=psDT[:])
                sc.copy(out=bct[:, 0, :], in_=psB16[:])
                ctx.inc(sc.copy(out=bct[:, 1, :], in_=psC16[:]), "a")
                evxp_mark[it] = ctx.mark("a")
            ctx.block.scalar(evac_xp)
            def dma_bc(g, t0=t0, it=it):
                ctx.wait_val(g, "a", evxp_mark[it])
                ctx.inc(g.dma_start(out=Bt_d[:, t0:t0 + TT], in_=bct[:, 0, :]), "o", 16)
                ctx.inc(g.dma_start(out=Ct_d[:, t0:t0 + TT], in_=bct[:, 1, :]), "o", 16)
                bc_mark[it] = ctx.mark("o")
            ctx.block.gpsimd(dma_bc)
            mmk2, evk2 = {}, {}
            for mo in range(16):
                j = mo % 4
                def mm_dt(t, mo=mo, j=j, it=it):
                    ctx.wait_val(t, "a", evxp_mark[it] if mo < 4 else max(evxp_mark[it], evk2[mo - 4]))
                    if mo == 0 and (it - 1) in sp_mark:
                        ctx.wait_val(t, "v", sp_mark[it - 1])
                    ctx.inc(t.matmul(psK[j][:], dtw[:, mo, :], dtb[:], start=True, stop=True), "p")
                    mmk2[mo] = ctx.mark("p")
                ctx.block.tensor(mm_dt)
                def dt_ev(sc, mo=mo, j=j, it=it):
                    ctx.wait_val(sc, "p", mmk2[mo])
                    if mo == 0 and (it - 1) in st_mark:
                        ctx.wait_val(sc, "o", st_mark[it - 1])
                    if mo == 0 and (it - 1) in u_mark:
                        ctx.wait_val(sc, "v", u_mark[it - 1])
                    ctx.inc(sc.activation(out=delta[:, mo, :], in_=psK[j][:], func=AF.Identity,
                                          bias=dpb[:, mo:mo + 1]), "a")
                    evk2[mo] = ctx.mark("a")
                ctx.block.scalar(dt_ev)
            for hh in range(2):
                if hh == 0:
                    A_, B_ = xe[tb][:, 0:8, 0:TT], xe[tb][:, 8:16, 0:TT]
                else:
                    A_, B_ = xe[tb][:, 8:16, 0:TT], xe[tb][:, 0:8, 0:TT]
                dlt = delta[:, hh * 8:hh * 8 + 8, :]
                def soft_a(sc, A_=A_, B_=B_, dlt=dlt, it=it, hh=hh):
                    ctx.wait_val(sc, "a", evk2[15])
                    ctx.wait_val(sc, "v", conv_mark[(it, 15)])
                    sc.activation(out=A_, in_=dlt, func=AF.Abs)
                    sc.activation(out=B_, in_=A_, func=AF.Exp, scale=-1.0)
                    sc.activation(out=A_, in_=B_, func=AF.Ln, bias=onep[:, 0:1])
                    ctx.inc(sc.activation(out=B_, in_=dlt, func=AF.Relu), "a")
                ctx.block.scalar(soft_a)
                def soft_v(v, A_=A_, B_=B_, dlt=dlt, it=it, hh=hh):
                    ctx.wait(v, "a")
                    ctx.inc(v.tensor_tensor(dlt, A_, B_, OP.add), "v")
                    if hh == 1:
                        sp_mark[it] = ctx.mark("v")
                ctx.block.vector(soft_v)
            def mk_u(v, it=it):
                ctx.wait_val(v, "v", sp_mark[it])
                ctx.inc(v.tensor_tensor(ub[:], delta[:], xcs[:], OP.mult), "v")
                u_mark[it] = ctx.mark("v")
            ctx.block.vector(mk_u)
            def dma_out(sc, t0=t0, it=it):
                ctx.wait_val(sc, "v", u_mark[it])
                ctx.inc(sc.dma_start(out=xcs_d.rearrange("(c p) t -> p c t", p=128)[:, :, t0:t0 + TT], in_=xcs[:]), "o", 16)
                ctx.inc(sc.dma_start(out=delta_d.rearrange("(c p) t -> p c t", p=128)[:, :, t0:t0 + TT], in_=delta[:]), "o", 16)
                ctx.inc(sc.dma_start(out=u_d.rearrange("(c p) t -> p c t", p=128)[:, :, t0:t0 + TT], in_=ub[:]), "o", 16)
                st_mark[it] = ctx.mark("o")
            ctx.block.scalar(dma_out)
        def fin(g):
            ctx.wait(g, "o")
            g.nop()
        ctx.block.gpsimd(fin)


TB = 1024
NTB = SEQ // TB

def emit_p3(ctx, dirn):
    nc = ctx.nc
    delta_d, u_d = ctx.io["delta_" + dirn], ctx.io["u_" + dirn]
    Bt_d, Ct_d, ys_d = ctx.io["Bt_" + dirn], ctx.io["Ct_" + dirn], ctx.io["ys_" + dirn]
    rev = (dirn == "b")
    with ExitStack() as es:
        def sb(name, shape, dt):
            return es.enter_context(nc.sbuf_tensor("p3%s_%s" % (dirn, name), shape, dt))
        def psum(name, shape):
            return es.enter_context(nc.psum_tensor("p3%s_%s" % (dirn, name), shape, F32))
        dall = [sb("dall%d" % i, [128, 8, TB], F32) for i in range(2)]
        uall = [sb("uall%d" % i, [128, 8, TB], BF16) for i in range(2)]
        Bbc = sb("Bbc", [128, 16, TB], BF16)
        Cbc = sb("Cbc", [128, 16, TB], BF16)
        at = [sb("at%d" % i, [128, TB], BF16) for i in range(8)]
        bt0 = [sb("bt0%d" % i, [128, TB], BF16) for i in range(2)]
        ht = [sb("ht%d" % i, [128, TB], BF16) for i in range(2)]
        hct = [sb("hct%d" % i, [128, TB], BF16) for i in range(4)]
        ysb = [sb("ysb%d" % i, [128, TB], BF16) for i in range(2)]
        carry = sb("carry", [128, 256], F32)
        zcol = sb("zcol", [128, 1], F32)
        asc = sb("asc", [128, 16, 16], F32)
        ascr = sb("ascr", [128, 16, 16], F32)
        bc16 = sb("bc16", [16, 2, TB], BF16)
        oh = sb("oh", [16, 2048], BF16)
        idt = sb("idt", [128, 128], BF16)
        psBC = [psum("psBC%d" % i, [128, TB]) for i in range(2)]
        psY = [psum("psY%d" % i, [128, TB]) for i in range(2)]
        def dma_par(sync):
            ctx.inc(sync.dma_start(out=ascr[:], in_=ctx.io["A_log"]), "d", 16)
            ctx.inc(sync.dma_start(out=oh[:], in_=ctx.io["oh16"]), "d", 16)
            ctx.inc(sync.dma_start(out=idt[:], in_=ctx.io["id128"]), "d", 16)
        ctx.block.sync(dma_par)
        ascE = at[0].ap()[:, 0:256].rearrange("p (a b) -> p a b", b=16)
        def init_a(sc):
            ctx.wait(sc, "d")
            ctx.inc(sc.activation(out=ascE, in_=ascr[:], func=AF.Exp), "a")
        ctx.block.scalar(init_a)
        def init_v(v):
            ctx.wait(v, "a")
            v.tensor_scalar(asc[:], ascE, -1.0, None, OP.mult)
            ctx.inc(v.memset(zcol[:], 0.0), "v")
        ctx.block.vector(init_v)
        init_vmark = ctx.mark("v")

        tbs = list(range(NTB))
        if rev:
            tbs = tbs[::-1]
        pending_yev = []
        carry_q = []
        half_amark = {}
        half_gmark = {}
        half_vmark = {}
        bc_pmark = {}
        bc_amark = 0
        ev_mark = {}
        dmao_mark = {}
        exp_mark = {}
        b_mark = {}
        bsem = {}
        scan_mark = {}
        y_mark = {}
        bcev_am = {}
        for tbi, tb in enumerate(tbs):
            s0 = tb * TB
            def dma_bc(sync, s0=s0, tbi=tbi):
                if tbi > 0:
                    ctx.wait_val(sync, "p", bc_pmark[tbi - 1])
                ctx.inc(sync.dma_start(out=bc16[:, 0, :], in_=Bt_d[:, s0:s0 + TB]), "d", 16)
                ctx.inc(sync.dma_start(out=bc16[:, 1, :], in_=Ct_d[:, s0:s0 + TB]), "d", 16)
            ctx.block.sync(dma_bc)
            dmark = ctx.mark("d")
            for n in range(16):
                def bc_mm(t, n=n, tbi=tbi):
                    ctx.wait_val(t, "d", dmark)
                    if tbi > 0:
                        i_last = (tbi * 2 - 1) * 128 + 7 * 16 + n
                        if i_last in scan_mark:
                            ctx.wait_val(t, "v", scan_mark[i_last])
                    ctx.wait_val(t, "a", bcev_am.get(n - 1, bc_amark))
                    for seg in range(2):
                        t.matmul(psBC[0][:, seg * 512:(seg + 1) * 512], oh[:, n * 128:(n + 1) * 128],
                                 bc16[:, 0, seg * 512:(seg + 1) * 512], start=True, stop=True)
                        ins = t.matmul(psBC[1][:, seg * 512:(seg + 1) * 512], oh[:, n * 128:(n + 1) * 128],
                                       bc16[:, 1, seg * 512:(seg + 1) * 512], start=True, stop=True)
                    ctx.inc(ins, "p")
                ctx.block.tensor(bc_mm)
                pm = ctx.mark("p")
                def bc_ev(sc, n=n, pm=pm):
                    ctx.wait_val(sc, "p", pm)
                    sc.copy(out=Bbc[:, n, :], in_=psBC[0][:])
                    ctx.inc(sc.copy(out=Cbc[:, n, :], in_=psBC[1][:]), "a")
                ctx.block.scalar(bc_ev)
                bcev_am[n] = ctx.mark("a")
            bc_pmark[tbi] = ctx.mark("p")
            bc_amark = ctx.mark("a")
            stage2_a = ctx.mark("a")

            for half in range(2):
                q = tbi * 2 + half
                hb = q % 2
                def dma_du(sync, s0=s0, half=half, q=q, hb=hb):
                    if q == 0:
                        ctx.wait_val(sync, "v", init_vmark)
                    if q >= 2:
                        ctx.wait_val(sync, "a", half_amark[q - 2])
                        ctx.wait_val(sync, "g", half_gmark[q - 2])
                        ctx.wait_val(sync, "v", half_vmark[q - 2])
                    c0 = half * 8
                    ctx.inc(sync.dma_start(out=dall[hb][:], in_=delta_d.rearrange("(c p) t -> p c t", p=128)[:, c0:c0 + 8, s0:s0 + TB]), "d", 16)
                    ctx.inc(sync.dma_start(out=uall[hb][:], in_=u_d.rearrange("(c p) t -> p c t", p=128)[:, c0:c0 + 8, s0:s0 + TB]), "d", 16)
                ctx.block.sync(dma_du)
                dumark = ctx.mark("d")
                for dci in range(8):
                    dc = half * 8 + dci
                    for gi in range(4):
                        base = q * 128 + dci * 16 + gi * 4
                        if gi == 1 and pending_yev:
                            for fn_ in pending_yev:
                                fn_()
                            pending_yev = []
                        if gi % 2 == 0:
                            def exp_grp(sc, dci=dci, dc=dc, gi=gi, base=base, hb=hb, dumark=dumark):
                                ctx.wait_val(sc, "d", dumark)
                                for k in range(8):
                                    i = base + k
                                    n = gi * 4 + k
                                    if (i - 8) in scan_mark:
                                        ctx.wait_val(sc, "v", scan_mark[i - 8])
                                    ctx.inc(sc.activation(out=at[i % 8][:], in_=dall[hb][:, dci, :], func=AF.Exp,
                                                          scale=asc[:, dc, n:n + 1]), "a")
                                    exp_mark[i] = ctx.mark("a")
                            ctx.block.scalar(exp_grp)
                        def dve_grp(v, dci=dci, dc=dc, gi=gi, base=base, hb=hb, tbi=tbi, dumark=dumark):
                            ctx.wait_val(v, "d", dumark)
                            ctx.wait_val(v, "a", max(bcev_am[gi * 4 + 3], exp_mark[base + 3]))
                            v.tensor_tensor(bt0[0][:], uall[hb][:, dci, :], Bbc[:, gi * 4, :], OP.mult)
                            for k in range(4):
                                i = base + k
                                n = gi * 4 + k
                                slot = dc * 16 + n
                                init = zcol[:, 0:1] if tbi == 0 else carry[:, slot:slot + 1]
                                a_in = at[i % 8][:, ::-1] if rev else at[i % 8][:]
                                b_in = bt0[k % 2][:, ::-1] if rev else bt0[k % 2][:]
                                if (i - 4) in y_mark:
                                    ctx.wait_val(v, "p", y_mark[i - 4])
                                ctx.inc(v.tensor_tensor_scan(ht[i % 2][:], a_in, b_in, init, OP.mult, OP.add), "v")
                                if k < 3:
                                    v.tensor_tensor(bt0[(k + 1) % 2][:], uall[hb][:, dci, :], Bbc[:, n + 1, :], OP.mult)
                                ctx.wait(v, "v")
                                v.tensor_copy(carry[:, slot:slot + 1], ht[i % 2][:, TB - 1:TB])
                                h_in = ht[i % 2][:, ::-1] if rev else ht[i % 2][:]
                                ctx.inc(v.tensor_tensor(hct[i % 4][:], h_in, Cbc[:, n, :], OP.mult), "v")
                                scan_mark[i] = ctx.mark("v")
                        ctx.block.vector(dve_grp)
                        for hk in range(2):
                            def pe_grp(t, dci=dci, dc=dc, gi=gi, base=base, tbi=tbi, hk=hk):
                                b2 = base + hk * 2
                                ctx.wait_val(t, "v", scan_mark[b2 + 1])
                                gdc = tbi * 16 + dc
                                if gi == 0 and hk == 0 and (gdc - 2) in ev_mark:
                                    ctx.wait_val(t, "a", ev_mark[gdc - 2])
                                for k in range(2):
                                    i = b2 + k
                                    n = gi * 4 + hk * 2 + k
                                    for seg in range(2):
                                        ins = t.matmul(psY[dc % 2][:, seg * 512:(seg + 1) * 512], idt[:],
                                                       hct[i % 4][:, seg * 512:(seg + 1) * 512],
                                                       start=(n == 0), stop=(n == 15))
                                ctx.inc(ins, "p")
                                y_mark[b2 + 1] = ctx.mark("p")
                            ctx.block.tensor(pe_grp)
                    gdc = tbi * 16 + dc
                    ymark_last = y_mark[q * 128 + (dc % 8) * 16 + 12 + 3]
                    def mk_yev(dc=dc, s0=s0, gdc=gdc, ymark_last=ymark_last):
                        def y_ev(sc):
                            ctx.wait_val(sc, "p", ymark_last)
                            if (gdc - 2) in dmao_mark:
                                ctx.wait_val(sc, "o", dmao_mark[gdc - 2])
                            ctx.inc(sc.copy(out=ysb[dc % 2][:], in_=psY[dc % 2][:]), "a")
                            ev_mark[gdc] = ctx.mark("a")
                        ctx.block.scalar(y_ev)
                        def y_dma(g):
                            ctx.wait_val(g, "a", ev_mark[gdc])
                            ctx.inc(g.dma_start(out=ys_d[dc * 128:(dc + 1) * 128, s0:s0 + TB], in_=ysb[dc % 2][:]), "o", 16)
                            dmao_mark[gdc] = ctx.mark("o")
                        ctx.block.gpsimd(y_dma)
                    pending_yev.append(mk_yev)
                for fn_ in pending_yev:
                    fn_()
                pending_yev = []
                half_amark[q] = ctx.mark("a")
                half_gmark[q] = ctx.mark("g")
                half_vmark[q] = ctx.mark("v")
        def fin(g):
            ctx.wait(g, "o")
            g.nop()
        ctx.block.gpsimd(fin)


def emit_p4(ctx, dirn):
    nc = ctx.nc
    ys_d, xcs_d, zd = ctx.io["ys_" + dirn], ctx.io["xcs_" + dirn], ctx.io["z"]
    with ExitStack() as es:
        def sb(name, shape, dt):
            return es.enter_context(nc.sbuf_tensor("p4%s_%s" % (dirn, name), shape, dt))
        ys = sb("ys", [128, 16, TT], BF16)
        xcs = sb("xcs", [128, 16, TT], F32)
        zt = sb("zt", [128, 16, TT], BF16)
        ygb = [sb("ygb%d" % i, [128, 16, TT], BF16) for i in range(2)]
        t1 = sb("t1", [128, TT], F32)
        t2 = sb("t2", [128, TT], F32)
        Dp = sb("Dp", [128, 16], F32)
        opw = sb("opw", [128, 16, 8 * 128], BF16)
        stg = sb("stg", [128, 1024], F32)
        ev = sb("ev", [128, 4, TT], F32)
        mft = sb("mft", [128, 8, TT], F32)
        psK = [es.enter_context(nc.psum_tensor("p4%s_ps%d" % (dirn, i), [128, TT], F32)) for i in range(4)]
        def dma_par(sync):
            ctx.inc(sync.dma_start(out=Dp[:], in_=ctx.io["D"]), "d", 16)
        ctx.block.sync(dma_par)
        load_weights_bf16(ctx, opw.ap(), ctx.io["out_proj_w"], DI, 1024, stg, "opw")
        opw_v = opw.ap().rearrange("p k (f j) -> p k f j", j=128)
        evk, mmk, dmk = {}, {}, {}
        gate_mark, mm_last = {}, {}
        sem_ev = "a" if dirn == "f" else "v"
        for it in range(NTILE):
            t0 = it * TT
            tbp = it % 2
            def dma_in(sync, t0=t0, it=it):
                if (it - 1) in gate_mark:
                    ctx.wait_val(sync, "v", gate_mark[it - 1][0])
                    ctx.wait_val(sync, "a", gate_mark[it - 1][1])
                if dirn == "b" and (it * 8 - 1) in evk:
                    ctx.wait_val(sync, sem_ev, evk[it * 8 - 1])
                ctx.inc(sync.dma_start(out=ys[:], in_=ys_d.rearrange("(c p) t -> p c t", p=128)[:, :, t0:t0 + TT]), "d", 16)
                ctx.inc(sync.dma_start(out=xcs[:], in_=xcs_d.rearrange("(c p) t -> p c t", p=128)[:, :, t0:t0 + TT]), "d", 16)
                ctx.inc(sync.dma_start(out=zt[:], in_=zd.rearrange("(c p) t -> p c t", p=128)[:, :, t0:t0 + TT]), "d", 16)
                if dirn == "b":
                    ctx.inc(sync.dma_start(out=mft[:], in_=ctx.io["mf"].rearrange("(c p) t -> p c t", p=128)[:, :, t0:t0 + TT]), "d", 16)
            ctx.block.sync(dma_in)
            ddm2 = ctx.mark("d")
            for c in range(16):
                def gate_a(sc, c=c, it=it, ddm2=ddm2):
                    ctx.wait_val(sc, "d", ddm2)
                    ctx.wait(sc, "v")
                    ctx.inc(sc.activation(out=t2[:], in_=zt[:, c, :], func=AF.Silu), "a")
                ctx.block.scalar(gate_a)
                def gate_v(v, c=c, it=it, tbp=tbp, ddm2=ddm2):
                    ctx.wait_val(v, "d", ddm2)
                    ctx.wait(v, "a")
                    if c == 0 and (it - 2) in mm_last:
                        ctx.wait_val(v, "p", mm_last[it - 2])
                    v.scalar_tensor_tensor(t1[:], xcs[:, c, :], Dp[:, c:c + 1], ys[:, c, :], OP.mult, OP.add)
                    ctx.inc(v.tensor_tensor(ygb[tbp][:, c, :], t1[:], t2[:], OP.mult), "v")
                ctx.block.vector(gate_v)
            gate_mark[it] = (ctx.mark("v"), ctx.mark("a"))
            ddm = ctx.mark("d")
            for fo in range(8):
                j = fo % 4
                gfo = it * 8 + fo
                def mm(t, fo=fo, j=j, gfo=gfo, it=it, tbp=tbp):
                    ctx.wait_val(t, "v", gate_mark[it][0])
                    if (gfo - 4) in evk:
                        ctx.wait_val(t, sem_ev, evk[gfo - 4])
                    for ki in range(16):
                        ins = t.matmul(psK[j][:], opw_v[:, ki, fo, :], ygb[tbp][:, ki, :],
                                       start=(ki == 0), stop=(ki == 15))
                    ctx.inc(ins, "p")
                    mmk[gfo] = ctx.mark("p")
                    if fo == 7:
                        mm_last[it] = ctx.mark("p")
                ctx.block.tensor(mm)
                if dirn == "f":
                    def evac(sc, fo=fo, j=j, gfo=gfo):
                        ctx.wait_val(sc, "p", mmk[gfo])
                        if (gfo - 4) in dmk:
                            ctx.wait_val(sc, "o", dmk[gfo - 4])
                        ctx.inc(sc.copy(out=ev[:, fo % 4, :], in_=psK[j][:]), "a")
                        evk[gfo] = ctx.mark("a")
                    ctx.block.scalar(evac)
                    def dma_out(g, fo=fo, t0=t0, gfo=gfo):
                        ctx.wait_val(g, "a", evk[gfo])
                        ctx.inc(g.dma_start(out=ctx.io["mf"][fo * 128:(fo + 1) * 128, t0:t0 + TT],
                                            in_=ev[:, fo % 4, :]), "o", 16)
                        dmk[gfo] = ctx.mark("o")
                    ctx.block.gpsimd(dma_out)
                else:
                    def evac(v, fo=fo, j=j, gfo=gfo, ddm=ddm):
                        ctx.wait_val(v, "p", mmk[gfo])
                        ctx.wait_val(v, "d", ddm)
                        if (gfo - 4) in dmk:
                            ctx.wait_val(v, "o", dmk[gfo - 4])
                        ctx.inc(v.tensor_tensor(ev[:, fo % 4, :], psK[j][:], mft[:, fo, :], OP.add), "v")
                        evk[gfo] = ctx.mark("v")
                    ctx.block.vector(evac)
                    def dma_out(g, fo=fo, t0=t0, gfo=gfo):
                        ctx.wait_val(g, "v", evk[gfo])
                        ctx.inc(g.dma_start(out=ctx.io["mo"][fo * 128:(fo + 1) * 128, t0:t0 + TT],
                                            in_=ev[:, fo % 4, :]), "o", 16)
                        dmk[gfo] = ctx.mark("o")
                    ctx.block.gpsimd(dma_out)
        def fin(g):
            ctx.wait(g, "o")
            g.nop()
        ctx.block.gpsimd(fin)


def emit_p45a(ctx):
    nc = ctx.nc
    with ExitStack() as es:
        def sb(name, shape, dt):
            return es.enter_context(nc.sbuf_tensor("p45a_%s" % name, shape, dt))
        def psum(name, shape):
            return es.enter_context(nc.psum_tensor("p45a_%s" % name, shape, F32))
        mo = [sb("mo%d" % i, [128, 8, TT], F32) for i in range(2)]
        sq = sb("sq", [128, 8, TT], F32)
        mbf = sb("mbf", [128, 8, TT], BF16)
        hb = [sb("hb%d" % i, [128, 8, TT], BF16) for i in range(2)]
        st = sb("st", [1, 7 * TT], F32)
        tmp = sb("tmp", [128, TT], F32)
        ones_bf = sb("onesb", [128, 1], BF16)
        ones_f32 = sb("onesf", [128, 1], F32)
        ones1 = sb("ones1", [1, 128], F32)
        cns = sb("cns", [128, 4], F32)
        lng = sb("lng", [128, 8], F32)
        lnb = sb("lnb", [128, 8], F32)
        fb1 = sb("fb1", [128, 32], F32)
        w1 = sb("w1", [128, 8, 32 * 128], BF16)
        stg = sb("stg", [128, 4096], F32)
        ev = sb("ev", [128, 4, TT], BF16)
        psA = psum("psA", [1, TT]); psB = psum("psB", [1, TT])
        psR = psum("psR", [128, TT]); psM = psum("psM", [128, TT])
        psK = [psum("ps%d" % i, [128, TT]) for i in range(4)]
        def dma_par(sync):
            ctx.inc(sync.dma_start(out=lng[:], in_=ctx.io["ln_g"]), "d", 16)
            ctx.inc(sync.dma_start(out=lnb[:], in_=ctx.io["ln_b"]), "d", 16)
            ctx.inc(sync.dma_start(out=fb1[:], in_=ctx.io["ff_b1"]), "d", 16)
        ctx.block.sync(dma_par)
        def init(v):
            v.memset(ones_bf[:], 1.0)
            v.memset(ones_f32[:], 1.0)
            v.memset(cns[:, 0:1], EPS_LN)
            ctx.inc(v.memset(ones1[:], 1.0), "v")
        ctx.block.vector(init)
        load_weights_bf16(ctx, w1.ap(), ctx.io["ff_w1"], DM, 4096, stg, "w1")
        w1_v = w1.ap().rearrange("p k (f j) -> p k f j", j=128)
        gains = [lng[:, c:c + 1] for c in range(8)]
        biases = [lnb[:, c:c + 1] for c in range(8)]
        tile_vmark, tile_amark, evk, dmk, mmk = {}, {}, {}, {}, {}
        for it in range(NTILE):
            t0 = it * TT
            tb = it % 2
            def dma_in(sync, t0=t0, it=it, tb=tb):
                if it >= 2:
                    ctx.wait_val(sync, "v", tile_vmark[it - 2])
                    ctx.wait_val(sync, "a", tile_amark[it - 2])
                ctx.inc(sync.dma_start(out=mo[tb][:], in_=ctx.io["mo"].rearrange("(c p) t -> p c t", p=128)[:, :, t0:t0 + TT]), "d", 16)
            ctx.block.sync(dma_in)
            rstd, m2 = emit_ln_stats(ctx, mo[tb].ap(), 8, TT, mbf.ap(), sq.ap(), psA, psB, st.ap(),
                                     ones_bf, ones_f32, cns[0:1, 0:1])
            emit_ln_normalize(ctx, mo[tb].ap(), 8, TT, hb[tb].ap(), rstd, m2, psR, psM, ones1, tmp.ap(),
                              gain=gains, bias_=biases)
            tile_vmark[it] = ctx.mark("v")
            tile_amark[it] = ctx.mark("a")
            nvm = tile_vmark[it]
            for fo in range(32):
                j = fo % 4
                gfo = it * 32 + fo
                def mm(t, fo=fo, j=j, gfo=gfo, nvm=nvm, tb=tb):
                    ctx.wait_val(t, "v", nvm)
                    if (gfo - 4) in evk:
                        ctx.wait_val(t, "a", evk[gfo - 4])
                    for ki in range(8):
                        ins = t.matmul(psK[j][:], w1_v[:, ki, fo, :], hb[tb][:, ki, :],
                                       start=(ki == 0), stop=(ki == 7))
                    ctx.inc(ins, "p")
                    mmk[gfo] = ctx.mark("p")
                ctx.block.tensor(mm)
                def evac(sc, fo=fo, j=j, gfo=gfo):
                    ctx.wait_val(sc, "p", mmk[gfo])
                    if (gfo - 4) in dmk:
                        ctx.wait_val(sc, "o", dmk[gfo - 4])
                    ctx.inc(sc.activation(out=ev[:, fo % 4, :], in_=psK[j][:], func=AF.Gelu,
                                          bias=fb1[:, fo:fo + 1]), "a")
                    evk[gfo] = ctx.mark("a")
                ctx.block.scalar(evac)
                def dma_out(g, fo=fo, t0=t0, gfo=gfo):
                    ctx.wait_val(g, "a", evk[gfo])
                    ctx.inc(g.dma_start(out=ctx.io["g1"][fo * 128:(fo + 1) * 128, t0:t0 + TT],
                                        in_=ev[:, fo % 4, :]), "o", 16)
                    dmk[gfo] = ctx.mark("o")
                ctx.block.gpsimd(dma_out)
        def fin(g):
            ctx.wait(g, "o")
            g.nop()
        ctx.block.gpsimd(fin)


def emit_p45b(ctx):
    nc = ctx.nc
    with ExitStack() as es:
        def sb(name, shape, dt):
            return es.enter_context(nc.sbuf_tensor("p45b_%s" % name, shape, dt))
        def psum(name, shape):
            return es.enter_context(nc.psum_tensor("p45b_%s" % name, shape, F32))
        g1 = sb("g1", [128, 32, TT], BF16)
        xt = sb("xt", [128, 8, TT], F32)
        f = sb("f", [128, 8, TT], F32)
        fbf = sb("fbf", [128, 8, TT], BF16)
        sq = sb("sq", [128, 8, TT], F32)
        fn = sb("fn", [128, 8, TT], F32)
        st = sb("st", [1, 7 * TT], F32)
        tmp = sb("tmp", [128, TT], F32)
        ones_bf = sb("onesb", [128, 1], BF16)
        ones_f32 = sb("onesf", [128, 1], F32)
        ones1 = sb("ones1", [1, 128], F32)
        cns = sb("cns", [128, 4], F32)
        fb2 = sb("fb2", [128, 8], F32)
        w2 = sb("w2", [128, 32, 8 * 128], BF16)
        stg = sb("stg", [128, 1024], F32)
        psA = psum("psA", [1, TT]); psB = psum("psB", [1, TT])
        psR = psum("psR", [128, TT]); psM = psum("psM", [128, TT])
        psK = [psum("ps%d" % i, [128, TT]) for i in range(4)]
        def dma_par(sync):
            ctx.inc(sync.dma_start(out=fb2[:], in_=ctx.io["ff_b2"]), "d", 16)
        ctx.block.sync(dma_par)
        def init(v):
            v.memset(ones_bf[:], 1.0)
            v.memset(ones_f32[:], 1.0)
            v.memset(cns[:, 0:1], EPS_LN)
            ctx.inc(v.memset(ones1[:], 1.0), "v")
        ctx.block.vector(init)
        load_weights_bf16(ctx, w2.ap(), ctx.io["ff_w2"], H4, 1024, stg, "w2")
        w2_v = w2.ap().rearrange("p k (f j) -> p k f j", j=128)
        evk, mmk, mm_last, res_mark = {}, {}, {}, {}
        for it in range(NTILE):
            t0 = it * TT
            def dma_in(sync, t0=t0, it=it):
                if (it - 1) in mm_last:
                    ctx.wait_val(sync, "p", mm_last[it - 1])
                if (it - 1) in res_mark:
                    ctx.wait_val(sync, "v", res_mark[it - 1])
                ctx.inc(sync.dma_start(out=g1[:], in_=ctx.io["g1"].rearrange("(c p) t -> p c t", p=128)[:, :, t0:t0 + TT]), "d", 16)
                ctx.inc(sync.dma_start(out=xt[:], in_=ctx.io["x"].rearrange("(c p) t -> p c t", p=128)[:, :, t0:t0 + TT]), "d", 16)
            ctx.block.sync(dma_in)
            dmm = ctx.mark("d")
            avm = ctx.mark("a")
            for fo in range(8):
                j = fo % 4
                gfo = it * 8 + fo
                def mm(t, fo=fo, j=j, gfo=gfo, dmm=dmm, avm=avm, it=it):
                    ctx.wait_val(t, "d", dmm)
                    ctx.wait_val(t, "a", avm if (gfo - 4) not in evk else max(avm, evk[gfo - 4]))
                    for ki in range(32):
                        ins = t.matmul(psK[j][:], w2_v[:, ki, fo, :], g1[:, ki, :],
                                       start=(ki == 0), stop=(ki == 31))
                    ctx.inc(ins, "p")
                    mmk[gfo] = ctx.mark("p")
                    if fo == 7:
                        mm_last[it] = ctx.mark("p")
                ctx.block.tensor(mm)
                def evac(sc, fo=fo, j=j, gfo=gfo):
                    ctx.wait_val(sc, "p", mmk[gfo])
                    ctx.wait(sc, "v", "o")
                    ctx.inc(sc.activation(out=f[:, fo, :], in_=psK[j][:], func=AF.Identity,
                                          bias=fb2[:, fo:fo + 1]), "a")
                    evk[gfo] = ctx.mark("a")
                ctx.block.scalar(evac)
            rstd, m2 = emit_ln_stats(ctx, f.ap(), 8, TT, fbf.ap(), sq.ap(), psA, psB, st.ap(),
                                     ones_bf, ones_f32, cns[0:1, 0:1])
            emit_ln_normalize(ctx, f.ap(), 8, TT, fn.ap(), rstd, m2, psR, psM, ones1, tmp.ap())
            def resid(v, it=it):
                ctx.wait(v, "d", "o")
                ctx.inc(v.tensor_tensor(sq[:], fn[:], xt[:], OP.add), "v")
                res_mark[it] = ctx.mark("v")
            ctx.block.vector(resid)
            def dma_out(g, t0=t0, it=it):
                ctx.wait_val(g, "v", res_mark[it])
                ctx.inc(g.dma_start(out=ctx.io["x1"].rearrange("(c p) t -> p c t", p=128)[:, :, t0:t0 + TT], in_=sq[:]), "o", 16)
            ctx.block.gpsimd(dma_out)
        def fin(g):
            ctx.wait(g, "o")
            g.nop()
        ctx.block.gpsimd(fin)


def emit_p5(ctx):
    nc = ctx.nc
    with ExitStack() as es:
        def sb(name, shape, dt):
            return es.enter_context(nc.sbuf_tensor("p5_%s" % name, shape, dt))
        def psum(name, shape):
            return es.enter_context(nc.psum_tensor("p5_%s" % name, shape, F32))
        xa = sb("xa", [128, 8, TT], F32)
        sq = sb("sq", [128, 8, TT], F32)
        xbf = sb("xbf", [128, 8, TT], BF16)
        u = sb("u", [128, 8, TT], F32)
        upr = sb("upr", [128, 8, TT], F32)
        ucar = sb("ucar", [128, 8, 1], F32)
        xrb = sb("xrb", [128, 8, TT], BF16)
        xkb = sb("xkb", [128, 8, TT], BF16)
        rb = sb("rb", [128, 8, TT], F32)
        kab = sb("kab", [128, 8, TT], BF16)
        ob = sb("ob", [128, 8, TT], F32)
        obf = sb("obf", [128, 8, TT], BF16)
        st = sb("st", [1, 7 * TT], F32)
        tmp = sb("tmp", [128, TT], F32)
        t2 = sb("t2", [128, TT], F32)
        ones_bf = sb("onesb", [128, 1], BF16)
        ones_f32 = sb("onesf", [128, 1], F32)
        ones1 = sb("ones1", [1, 128], F32)
        cns = sb("cns", [128, 4], F32)
        mur = sb("mur", [128, 8], F32)
        muk = sb("muk", [128, 8], F32)
        wr = sb("wr", [128, 8, 8 * 128], BF16)
        wk = sb("wk", [128, 8, 8 * 128], BF16)
        wv = sb("wv", [128, 8, 8 * 128], BF16)
        stg = sb("stg", [128, 1024], F32)
        psA = psum("psA", [1, TT]); psB = psum("psB", [1, TT])
        psR = psum("psR", [128, TT]); psM = psum("psM", [128, TT])
        psK = [psum("ps%d" % i, [128, TT]) for i in range(4)]
        def dma_par(sync):
            ctx.inc(sync.dma_start(out=mur[:], in_=ctx.io["L0_ffn_mu_r"]), "d", 16)
            ctx.inc(sync.dma_start(out=muk[:], in_=ctx.io["L0_ffn_mu_k"]), "d", 16)
        ctx.block.sync(dma_par)
        def init(v):
            v.memset(ones_bf[:], 1.0)
            v.memset(ones_f32[:], 1.0)
            v.memset(cns[:, 0:1], EPS_LN)
            v.memset(ucar[:], 0.0)
            ctx.inc(v.memset(ones1[:], 1.0), "v")
        ctx.block.vector(init)
        load_weights_bf16(ctx, wr.ap(), ctx.io["WrT"], DM, 1024, stg, "wr")
        load_weights_bf16(ctx, wk.ap(), ctx.io["WkT"], DM, 1024, stg, "wk")
        load_weights_bf16(ctx, wv.ap(), ctx.io["WvT"], DM, 1024, stg, "wv")
        wr_v = wr.ap().rearrange("p k (f j) -> p k f j", j=128)
        wk_v = wk.ap().rearrange("p k (f j) -> p k f j", j=128)
        wv_v = wv.ap().rearrange("p k (f j) -> p k f j", j=128)
        for it in range(NTILE):
            t0 = it * TT
            def dma_in(sync, t0=t0):
                ctx.wait(sync, "v", "a", "p")
                ctx.inc(sync.dma_start(out=xa[:], in_=ctx.io["x1"].rearrange("(c p) t -> p c t", p=128)[:, :, t0:t0 + TT]), "d", 16)
            ctx.block.sync(dma_in)
            rstd, m2 = emit_ln_stats(ctx, xa.ap(), 8, TT, xbf.ap(), sq.ap(), psA, psB, st.ap(),
                                     ones_bf, ones_f32, cns[0:1, 0:1])
            emit_ln_normalize(ctx, xa.ap(), 8, TT, u.ap(), rstd, m2, psR, psM, ones1, tmp.ap())
            def shift_v(v):
                v.tensor_copy(upr[:, :, 1:TT], u[:, :, 0:TT - 1])
                ctx.inc(v.tensor_copy(upr[:, :, 0:1], ucar[:]), "v")
                ctx.wait(v, "v")
                ctx.inc(v.tensor_copy(ucar[:], u[:, :, TT - 1:TT]), "v")
                for c in range(8):
                    v.tensor_tensor(t2[:], u[:, c, :], upr[:, c, :], OP.subtract)
                    v.scalar_tensor_tensor(xrb[:, c, :], t2[:], mur[:, c:c + 1], upr[:, c, :], OP.mult, OP.add)
                    ins = v.scalar_tensor_tensor(xkb[:, c, :], t2[:], muk[:, c:c + 1], upr[:, c, :], OP.mult, OP.add)
                ctx.inc(ins, "v")
            ctx.block.vector(shift_v)
            xvm = ctx.mark("v")
            mmk, evk = {}, {}
            for fo in range(24):
                j = fo % 4
                grp = fo // 8
                fi = fo % 8
                def mm3(t, fo=fo, j=j, grp=grp, fi=fi, xvm=xvm):
                    ctx.wait_val(t, "v", xvm)
                    if (fo - 4) in evk:
                        s_, v_ = evk[fo - 4]
                        ctx.wait_val(t, s_, v_)
                    if grp == 2:
                        ctx.wait_val(t, "a", evk[15][1])
                    wsel = (wr_v, wk_v, wv_v)[grp]
                    xsel = (xrb, xkb, kab)[grp]
                    for ki in range(8):
                        ins = t.matmul(psK[j][:], wsel[:, ki, fi, :], xsel[:, ki, :],
                                       start=(ki == 0), stop=(ki == 7))
                    ctx.inc(ins, "p")
                    mmk[fo] = ctx.mark("p")
                ctx.block.tensor(mm3)
                if grp == 0:
                    def ev_r(sc, fo=fo, j=j, fi=fi):
                        ctx.wait_val(sc, "p", mmk[fo])
                        ctx.wait(sc, "v", "o")
                        ctx.inc(sc.activation(out=rb[:, fi, :], in_=psK[j][:], func=AF.Sigmoid), "a")
                        evk[fo] = ("a", ctx.mark("a"))
                    ctx.block.scalar(ev_r)
                elif grp == 1:
                    def ev_k(sc, fo=fo, j=j, fi=fi):
                        ctx.wait_val(sc, "p", mmk[fo])
                        sc.activation(out=t2[:], in_=psK[j][:], func=AF.Relu)
                        ctx.inc(sc.activation(out=kab[:, fi, :], in_=t2[:], func=AF.Square), "a")
                        evk[fo] = ("a", ctx.mark("a"))
                    ctx.block.scalar(ev_k)
                else:
                    def ev_v(v, fo=fo, j=j, fi=fi):
                        ctx.wait_val(v, "p", mmk[fo])
                        ctx.wait_val(v, "a", evk[fi][1])
                        ctx.inc(v.tensor_tensor(ob[:, fi, :], rb[:, fi, :], psK[j][:], OP.mult), "v")
                        evk[fo] = ("v", ctx.mark("v"))
                    ctx.block.vector(ev_v)
            rstd, m2 = emit_ln_stats(ctx, ob.ap(), 8, TT, obf.ap(), sq.ap(), psA, psB, st.ap(),
                                     ones_bf, ones_f32, cns[0:1, 0:1])
            emit_ln_normalize(ctx, ob.ap(), 8, TT, rb.ap(), rstd, m2, psR, psM, ones1, tmp.ap())
            def resid(v):
                ctx.wait(v, "o")
                ctx.inc(v.tensor_tensor(sq[:], rb[:], xa[:], OP.add), "v")
            ctx.block.vector(resid)
            def dma_out(g, t0=t0):
                ctx.wait(g, "v")
                ctx.inc(g.dma_start(out=ctx.io["x2"].rearrange("(c p) t -> p c t", p=128)[:, :, t0:t0 + TT], in_=sq[:]), "o", 16)
            ctx.block.gpsimd(dma_out)
        def fin(g):
            ctx.wait(g, "o")
            g.nop()
        ctx.block.gpsimd(fin)


def emit_p6(ctx):
    nc = ctx.nc
    with ExitStack() as es:
        def sb(name, shape, dt):
            return es.enter_context(nc.sbuf_tensor("p6_%s" % name, shape, dt))
        def psum(name, shape):
            return es.enter_context(nc.psum_tensor("p6_%s" % name, shape, F32))
        xa = sb("xa", [128, 8, TT], F32)
        xbf = sb("xbf", [128, 8, TT], BF16)
        vb = sb("vb", [128, 8, TT], BF16)
        gtf = sb("gtf", [128, 8, TT], F32)
        gti = sb("gti", [128, 8, TT], F32)
        gth = sb("gth", [128, 8, TT], F32)
        tmpB = sb("tmpB", [128, 8, TT], F32)
        st = sb("st", [1, 7 * TT], F32)
        tmp = sb("tmp", [128, TT], F32)
        msk = sb("msk", [128, 8, TT], U8)
        lft = sb("lft", [128, 8, TT], F32)
        lvt = sb("lvt", [128, 8, TT], F32)
        sq = lft
        ones_bf = sb("onesb", [128, 1], BF16)
        ones_f32 = sb("onesf", [128, 1], F32)
        ones1 = sb("ones1", [1, 128], F32)
        cns = sb("cns", [128, 4], F32)
        wt = sb("wt", [128, 8, 24 * 128], BF16)
        stg = sb("stg", [128, 1024], F32)
        psA = psum("psA", [1, TT]); psB = psum("psB", [1, TT])
        psR = psum("psR", [128, TT]); psM = psum("psM", [128, TT])
        psK = [psum("ps%d" % i, [128, TT]) for i in range(4)]
        def init(v):
            v.memset(ones_bf[:], 1.0)
            v.memset(ones_f32[:], 1.0)
            v.memset(cns[:, 0:1], EPS_LN)
            v.memset(cns[:, 1:2], 1.0)
            v.memset(cns[:, 2:3], 0.5 + EPS_L1)
            ctx.inc(v.memset(ones1[:], 1.0), "v")
        ctx.block.vector(init)
        load_weights_bf16(ctx, wt.ap(), ctx.io["L1WT"], DM, 3072, stg, "wt", mcap=1024)
        wt_v = wt.ap().rearrange("p k (f j) -> p k f j", j=128)
        one_b = cns[:, 1:2]
        half_b = cns[:, 2:3]

        def sp_chain(src_ap, out_ap, sign, s1, s2):
            def sa(sc):
                ctx.wait(sc, "v")
                sc.activation(out=s1[:], in_=src_ap, func=AF.Abs)
                sc.activation(out=s2[:], in_=s1[:], func=AF.Exp, scale=-1.0)
                sc.activation(out=s1[:], in_=s2[:], func=AF.Ln, bias=one_b)
                ctx.inc(sc.activation(out=s2[:], in_=src_ap, func=AF.Relu, scale=float(sign)), "a")
            ctx.block.scalar(sa)
            def sv(v):
                ctx.wait(v, "a")
                ctx.inc(v.tensor_tensor(out_ap, s1[:], s2[:], OP.add), "v")
            ctx.block.vector(sv)

        for it in range(NTILE):
            t0 = it * TT
            def dma_in(sync, t0=t0):
                ctx.wait(sync, "v", "a", "p")
                ctx.inc(sync.dma_start(out=xa[:], in_=ctx.io["x2"].rearrange("(c p) t -> p c t", p=128)[:, :, t0:t0 + TT]), "d", 16)
            ctx.block.sync(dma_in)
            rstd, m2 = emit_ln_stats(ctx, xa.ap(), 8, TT, xbf.ap(), sq.ap(), psA, psB, st.ap(),
                                     ones_bf, ones_f32, cns[0:1, 0:1])
            emit_ln_normalize(ctx, xa.ap(), 8, TT, vb.ap(), rstd, m2, psR, psM, ones1, tmp.ap())
            mmk, evk = {}, {}
            for fo in range(24):
                j = fo % 4
                def mm(t, fo=fo, j=j):
                    ctx.wait(t, "v")
                    if fo - 4 in evk:
                        ctx.wait_val(t, "a", evk[fo - 4])
                    for ki in range(8):
                        ins = t.matmul(psK[j][:], wt_v[:, ki, fo, :], vb[:, ki, :],
                                       start=(ki == 0), stop=(ki == 7))
                    ctx.inc(ins, "p")
                    mmk[fo] = ctx.mark("p")
                ctx.block.tensor(mm)
                def evac(sc, fo=fo, j=j):
                    ctx.wait_val(sc, "p", mmk[fo])
                    ctx.wait(sc, "v")
                    dst = (gtf, gti, gth)[fo // 8]
                    ctx.inc(sc.copy(out=dst[:, fo % 8, :], in_=psK[j][:]), "a")
                    evk[fo] = ctx.mark("a")
                ctx.block.scalar(evac)
            gvm = ctx.mark("v")
            def sp_fi(sc, gvm=gvm):
                ctx.wait_val(sc, "v", gvm)
                sc.activation(out=xa[:], in_=gtf[:], func=AF.Abs)
                sc.activation(out=tmpB[:], in_=xa[:], func=AF.Exp, scale=-1.0)
                sc.activation(out=xa[:], in_=tmpB[:], func=AF.Ln, bias=one_b)
                sc.activation(out=tmpB[:], in_=gtf[:], func=AF.Relu, scale=-1.0)
                sc.activation(out=lvt[:], in_=gti[:], func=AF.Abs)
                sc.activation(out=lft[:], in_=lvt[:], func=AF.Exp, scale=-1.0)
                sc.activation(out=lvt[:], in_=lft[:], func=AF.Ln, bias=one_b)
                ctx.inc(sc.activation(out=lft[:], in_=gti[:], func=AF.Relu, scale=-1.0), "a")
            ctx.block.scalar(sp_fi)
            def mk_diff(v):
                ctx.wait(v, "a")
                v.tensor_tensor(gtf[:], xa[:], tmpB[:], OP.add)
                v.tensor_tensor(gti[:], lvt[:], lft[:], OP.add)
                ctx.inc(v.tensor_tensor(xa[:], gtf[:], gti[:], OP.subtract), "v")
            ctx.block.vector(mk_diff)
            dm_ = ctx.mark("v")
            def sp_d(sc, dm_=dm_):
                ctx.wait_val(sc, "v", dm_)
                sc.activation(out=gtf[:], in_=xa[:], func=AF.Abs)
                sc.activation(out=tmpB[:], in_=gtf[:], func=AF.Exp, scale=-1.0)
                sc.activation(out=gtf[:], in_=tmpB[:], func=AF.Ln, bias=one_b)
                sc.activation(out=tmpB[:], in_=xa[:], func=AF.Relu)
                ctx.inc(sc.activation(out=gti[:], in_=xa[:], func=AF.Relu, scale=-1.0), "a")
            ctx.block.scalar(sp_d)
            def mk_lfli(v, t0=t0):
                ctx.wait(v, "a", "o")
                v.tensor_tensor(tmpB[:], gtf[:], tmpB[:], OP.add)
                v.tensor_tensor(gti[:], gtf[:], gti[:], OP.add)
                v.tensor_scalar(lft[:], tmpB[:], -1.0, EPS_L1, OP.mult, OP.add)
                ctx.inc(v.tensor_scalar(xa[:], gti[:], -1.0, None, OP.mult), "v")
            ctx.block.vector(mk_lfli)
            lm_ = ctx.mark("v")
            def sp_h(sc, lm_=lm_):
                ctx.wait_val(sc, "v", lm_)
                sc.activation(out=gtf[:], in_=gth[:], func=AF.Abs)
                sc.activation(out=tmpB[:], in_=gtf[:], func=AF.Exp, scale=-1.0)
                sc.activation(out=gtf[:], in_=tmpB[:], func=AF.Ln, bias=one_b)
                sc.activation(out=tmpB[:], in_=gth[:], func=AF.Relu, scale=-1.0)
                sc.activation(out=gti[:], in_=gth[:], func=AF.Relu)
                ctx.inc(sc.activation(out=lvt[:], in_=gti[:], func=AF.Ln, bias=half_b), "a")
            ctx.block.scalar(sp_h)
            def mksel(v, t0=t0):
                ctx.wait(v, "a")
                v.tensor_tensor(gti[:], gtf[:], tmpB[:], OP.add)
                ctx.inc(v.tensor_scalar(tmpB[:], gti[:], -1.0, EPS_L1, OP.mult, OP.add), "v")
                ctx.inc(v.tensor_scalar(msk[:], gth[:], 0.0, None, OP.is_ge), "v")
                ctx.wait(v, "v")
                v.select(gth[:], msk[:], lvt[:], tmpB[:], add_drain=True)
                ctx.inc(v.tensor_tensor(lvt[:], xa[:], gth[:], OP.add), "v")
            ctx.block.vector(mksel)
            def dma_out(g, t0=t0):
                ctx.wait(g, "v")
                ctx.inc(g.dma_start(out=ctx.io["lf"].rearrange("(c p) t -> p c t", p=128)[:, :, t0:t0 + TT], in_=lft[:]), "o", 16)
                ctx.inc(g.dma_start(out=ctx.io["lv"].rearrange("(c p) t -> p c t", p=128)[:, :, t0:t0 + TT], in_=lvt[:]), "o", 16)
            ctx.block.gpsimd(dma_out)
        def fin(g):
            ctx.wait(g, "o")
            g.nop()
        ctx.block.gpsimd(fin)


def emit_p7(ctx):
    nc = ctx.nc
    with ExitStack() as es:
        def sb(name, shape, dt):
            return es.enter_context(nc.sbuf_tensor("p7_%s" % name, shape, dt))
        A = sb("A", [128, SEQ], F32)
        B = sb("B", [128, SEQ], F32)
        C = sb("C", [128, SEQ], F32)
        D = sb("D", [128, SEQ], F32)
        E = sb("E", [128, SEQ], F32)
        ones = sb("ones", [128, SEQ], F32)
        zc = sb("zc", [128, 1], F32)
        nc_ = sb("ncol", [128, 1], F32)
        def init(v):
            v.memset(ones[:], 1.0)
            v.memset(zc[:], 0.0)
            ctx.inc(v.memset(nc_[:], -1e30), "v")
        ctx.block.vector(init)
        for fc in range(8):
            def dma_in(sync, fc=fc):
                ctx.wait(sync, "v", "a", "o")
                ctx.inc(sync.dma_start(out=A[:], in_=ctx.io["lf"][fc * 128:(fc + 1) * 128, :]), "d", 16)
                ctx.inc(sync.dma_start(out=B[:], in_=ctx.io["lv"][fc * 128:(fc + 1) * 128, :]), "d", 16)
            ctx.block.sync(dma_in)
            def scans(v, fc=fc):
                ctx.wait(v, "d")
                ctx.inc(v.tensor_tensor_scan(C[:], ones[:], A[:], zc[:, 0:1], OP.mult, OP.add), "v")
                ctx.wait(v, "v")
                ctx.inc(v.tensor_tensor(D[:], B[:], C[:], OP.subtract), "v")
                ctx.inc(v.tensor_tensor_scan(E[:], D[:], D[:], nc_[:, 0:1], OP.max, OP.max), "v")
                ctx.wait(v, "v")
                v.tensor_copy(B[:, 0:1], nc_[:])
                ctx.inc(v.tensor_tensor(B[:, 1:SEQ], E[:, 0:SEQ - 1], E[:, 1:SEQ], OP.subtract), "v")
                ctx.inc(v.tensor_tensor(A[:], D[:], E[:], OP.subtract), "v")
            ctx.block.vector(scans)
            def exps(sc, fc=fc):
                ctx.wait(sc, "v")
                sc.activation(out=D[:], in_=B[:], func=AF.Exp)
                ctx.inc(sc.activation(out=B[:, 0:SEQ], in_=A[:], func=AF.Exp), "a")
            ctx.block.scalar(exps)
            def scan2(v, fc=fc):
                ctx.wait(v, "a")
                ctx.inc(v.tensor_tensor_scan(A[:], D[:], B[:], zc[:, 0:1], OP.mult, OP.add), "v")
                ctx.wait(v, "v")
                ctx.inc(v.tensor_tensor(B[:], C[:], E[:], OP.add), "v")
            ctx.block.vector(scan2)
            def fin_a(sc, fc=fc):
                ctx.wait(sc, "v")
                ctx.inc(sc.activation(out=D[:], in_=A[:], func=AF.Ln), "a")
            ctx.block.scalar(fin_a)
            def fin_v(v, fc=fc):
                ctx.wait(v, "a")
                ctx.inc(v.tensor_tensor(A[:], B[:], D[:], OP.add), "v")
            ctx.block.vector(fin_v)
            def fin_a2(sc, fc=fc):
                ctx.wait(sc, "v")
                ctx.inc(sc.activation(out=E[:], in_=A[:], func=AF.Exp), "a")
            ctx.block.scalar(fin_a2)
            def dma_out(g, fc=fc):
                ctx.wait(g, "a")
                ctx.inc(g.dma_start(out=ctx.io["ht"][fc * 128:(fc + 1) * 128, :], in_=E[:]), "o", 16)
            ctx.block.gpsimd(dma_out)
        def fin(g):
            ctx.wait(g, "o")
            g.nop()
        ctx.block.gpsimd(fin)


def emit_p8a(ctx):
    nc = ctx.nc
    with ExitStack() as es:
        def sb(name, shape, dt):
            return es.enter_context(nc.sbuf_tensor("p8a_%s" % name, shape, dt))
        def psum(name, shape):
            return es.enter_context(nc.psum_tensor("p8a_%s" % name, shape, F32))
        xa = sb("xa", [128, 8, TT], F32)
        hta = sb("hta", [128, 8, TT], F32)
        sq = sb("sq", [128, 8, TT], F32)
        hbf = sb("hbf", [128, 8, TT], BF16)
        hn = sb("hn", [128, 8, TT], F32)
        x3b = sb("x3b", [128, 8, TT], F32)
        w3t = sb("w3t", [128, 8, TT], BF16)
        st = sb("st", [1, 7 * TT], F32)
        tmp = sb("tmp", [128, TT], F32)
        ones_bf = sb("onesb", [128, 1], BF16)
        ones_f32 = sb("onesf", [128, 1], F32)
        ones1 = sb("ones1", [1, 128], F32)
        cns = sb("cns", [128, 4], F32)
        psA = psum("psA", [1, TT]); psB = psum("psB", [1, TT])
        psR = psum("psR", [128, TT]); psM = psum("psM", [128, TT])
        def init(v):
            v.memset(ones_bf[:], 1.0)
            v.memset(ones_f32[:], 1.0)
            v.memset(cns[:, 0:1], EPS_LN)
            ctx.inc(v.memset(ones1[:], 1.0), "v")
        ctx.block.vector(init)
        for it in range(NTILE):
            t0 = it * TT
            def dma_in(sync, t0=t0):
                ctx.wait(sync, "v", "a", "p")
                ctx.inc(sync.dma_start(out=xa[:], in_=ctx.io["x2"].rearrange("(c p) t -> p c t", p=128)[:, :, t0:t0 + TT]), "d", 16)
                ctx.inc(sync.dma_start(out=hta[:], in_=ctx.io["ht"].rearrange("(c p) t -> p c t", p=128)[:, :, t0:t0 + TT]), "d", 16)
            ctx.block.sync(dma_in)
            rstd, m2 = emit_ln_stats(ctx, hta.ap(), 8, TT, hbf.ap(), sq.ap(), psA, psB, st.ap(),
                                     ones_bf, ones_f32, cns[0:1, 0:1])
            emit_ln_normalize(ctx, hta.ap(), 8, TT, hn.ap(), rstd, m2, psR, psM, ones1, tmp.ap())
            def resid(v):
                ctx.wait(v, "o")
                ctx.inc(v.tensor_tensor(x3b[:], hn[:], xa[:], OP.add), "v")
            ctx.block.vector(resid)
            def dma_x3(g, t0=t0):
                ctx.wait(g, "v")
                ctx.inc(g.dma_start(out=ctx.io["x3"].rearrange("(c p) t -> p c t", p=128)[:, :, t0:t0 + TT], in_=x3b[:]), "o", 16)
            ctx.block.gpsimd(dma_x3)
            rstd, m2 = emit_ln_stats(ctx, x3b.ap(), 8, TT, hbf.ap(), sq.ap(), psA, psB, st.ap(),
                                     ones_bf, ones_f32, cns[0:1, 0:1])
            emit_ln_normalize(ctx, x3b.ap(), 8, TT, w3t.ap(), rstd, m2, psR, psM, ones1, tmp.ap())
            def dma_w3(g, t0=t0):
                ctx.wait(g, "v")
                ctx.inc(g.dma_start(out=ctx.io["w3"].rearrange("(c p) t -> p c t", p=128)[:, :, t0:t0 + TT], in_=w3t[:]), "o", 16)
            ctx.block.gpsimd(dma_w3)
        def fin(g):
            ctx.wait(g, "o")
            g.nop()
        ctx.block.gpsimd(fin)


def emit_p8b1(ctx):
    nc = ctx.nc
    with ExitStack() as es:
        def sb(name, shape, dt):
            return es.enter_context(nc.sbuf_tensor("p8b1_%s" % name, shape, dt))
        def psum(name, shape):
            return es.enter_context(nc.psum_tensor("p8b1_%s" % name, shape, F32))
        w3t = sb("w3t", [128, 8, TT], BF16)
        tg = sb("tg", [128, TT], F32)
        hht = sb("hht", [128, 24, TT], BF16)
        wg = sb("wg", [128, 8, 24 * 128], BF16)
        wu = sb("wu", [128, 8, 24 * 128], BF16)
        stg = sb("stg", [128, 1024], I32)
        psK = [psum("ps%d" % i, [128, TT]) for i in range(4)]
        load_weights_bf16(ctx, wg.ap(), ctx.io["wgT"], DM, 3072, stg, "wg", mcap=1024)
        load_weights_bf16(ctx, wu.ap(), ctx.io["wuT"], DM, 3072, stg, "wu", mcap=1024)
        wg_v = wg.ap().rearrange("p k (f j) -> p k f j", j=128)
        wu_v = wu.ap().rearrange("p k (f j) -> p k f j", j=128)
        SC = 1.0 / 127.0
        evk, mmk, muk = {}, {}, {}
        for it in range(NTILE):
            t0 = it * TT
            def dma_in(sync, t0=t0):
                ctx.wait(sync, "v", "a", "p")
                ctx.inc(sync.dma_start(out=w3t[:], in_=ctx.io["w3"].rearrange("(c p) t -> p c t", p=128)[:, :, t0:t0 + TT]), "d", 16)
            ctx.block.sync(dma_in)
            dmm = ctx.mark("d")
            for fo in range(24):
                jg, ju = (fo % 2) * 2, (fo % 2) * 2 + 1
                gfo = it * 24 + fo
                def mm(t, fo=fo, jg=jg, ju=ju, gfo=gfo, dmm=dmm):
                    ctx.wait_val(t, "d", dmm)
                    if (gfo - 2) in evk:
                        ctx.wait_val(t, "a", evk[gfo - 2])
                    if (gfo - 2) in muk:
                        ctx.wait_val(t, "v", muk[gfo - 2])
                    for ki in range(8):
                        ins = t.matmul(psK[jg][:], wg_v[:, ki, fo, :], w3t[:, ki, :],
                                       start=(ki == 0), stop=(ki == 7))
                    for ki in range(8):
                        ins = t.matmul(psK[ju][:], wu_v[:, ki, fo, :], w3t[:, ki, :],
                                       start=(ki == 0), stop=(ki == 7))
                    ctx.inc(ins, "p")
                    mmk[gfo] = ctx.mark("p")
                ctx.block.tensor(mm)
                def act_g(sc, fo=fo, jg=jg, gfo=gfo):
                    ctx.wait_val(sc, "p", mmk[gfo])
                    if (gfo - 1) in muk:
                        ctx.wait_val(sc, "v", muk[gfo - 1])
                    ctx.inc(sc.activation(out=tg[:], in_=psK[jg][:], func=AF.Silu, scale=SC), "a")
                    evk[gfo] = ctx.mark("a")
                ctx.block.scalar(act_g)
                def mul_u(v, fo=fo, ju=ju, gfo=gfo):
                    ctx.wait_val(v, "a", evk[gfo])
                    ctx.wait(v, "o")
                    ctx.inc(v.scalar_tensor_tensor(hht[:, fo, :], psK[ju][:], SC, tg[:], OP.mult, OP.mult), "v")
                    muk[gfo] = ctx.mark("v")
                ctx.block.vector(mul_u)
            def dma_out(g, t0=t0):
                ctx.wait(g, "v")
                ctx.inc(g.dma_start(out=ctx.io["hh"].rearrange("(c p) t -> p c t", p=128)[:, :, t0:t0 + TT], in_=hht[:]), "o", 16)
            ctx.block.gpsimd(dma_out)
        def fin(g):
            ctx.wait(g, "o")
            g.nop()
        ctx.block.gpsimd(fin)


def emit_p8b2(ctx):
    nc = ctx.nc
    with ExitStack() as es:
        def sb(name, shape, dt):
            return es.enter_context(nc.sbuf_tensor("p8b2_%s" % name, shape, dt))
        def psum(name, shape):
            return es.enter_context(nc.psum_tensor("p8b2_%s" % name, shape, F32))
        hht = sb("hht", [128, 24, TT], BF16)
        x3t = sb("x3t", [128, 8, TT], F32)
        f = sb("f", [128, 8, TT], F32)
        fbf = sb("fbf", [128, 8, TT], BF16)
        sq = sb("sq", [128, 8, TT], F32)
        fn = sb("fn", [128, 8, TT], F32)
        st = sb("st", [1, 7 * TT], F32)
        tmp = sb("tmp", [128, TT], F32)
        ones_bf = sb("onesb", [128, 1], BF16)
        ones_f32 = sb("onesf", [128, 1], F32)
        ones1 = sb("ones1", [1, 128], F32)
        cns = sb("cns", [128, 4], F32)
        wd = sb("wd", [128, 24, 8 * 128], BF16)
        stg = sb("stg", [128, 1024], I32)
        psA = psum("psA", [1, TT]); psB = psum("psB", [1, TT])
        psR = psum("psR", [128, TT]); psM = psum("psM", [128, TT])
        psK = [psum("ps%d" % i, [128, TT]) for i in range(4)]
        def init(v):
            v.memset(ones_bf[:], 1.0)
            v.memset(ones_f32[:], 1.0)
            v.memset(cns[:, 0:1], EPS_LN)
            ctx.inc(v.memset(ones1[:], 1.0), "v")
        ctx.block.vector(init)
        load_weights_bf16(ctx, wd.ap(), ctx.io["wdT"], H3, 1024, stg, "wd")
        wd_v = wd.ap().rearrange("p k (f j) -> p k f j", j=128)
        evk, mmk = {}, {}
        for it in range(NTILE):
            t0 = it * TT
            def dma_in(sync, t0=t0):
                ctx.wait(sync, "v", "a", "p")
                ctx.inc(sync.dma_start(out=hht[:], in_=ctx.io["hh"].rearrange("(c p) t -> p c t", p=128)[:, :, t0:t0 + TT]), "d", 16)
                ctx.inc(sync.dma_start(out=x3t[:], in_=ctx.io["x3"].rearrange("(c p) t -> p c t", p=128)[:, :, t0:t0 + TT]), "d", 16)
            ctx.block.sync(dma_in)
            dmm = ctx.mark("d")
            avm = ctx.mark("a")
            for fo in range(8):
                j = fo % 4
                gfo = it * 8 + fo
                def mm(t, fo=fo, j=j, gfo=gfo, dmm=dmm, avm=avm):
                    ctx.wait_val(t, "d", dmm)
                    ctx.wait_val(t, "a", avm if (gfo - 4) not in evk else max(avm, evk[gfo - 4]))
                    for ki in range(24):
                        ins = t.matmul(psK[j][:], wd_v[:, ki, fo, :], hht[:, ki, :],
                                       start=(ki == 0), stop=(ki == 23))
                    ctx.inc(ins, "p")
                    mmk[gfo] = ctx.mark("p")
                ctx.block.tensor(mm)
                def evac(sc, fo=fo, j=j, gfo=gfo):
                    ctx.wait_val(sc, "p", mmk[gfo])
                    ctx.wait(sc, "v", "o")
                    ctx.inc(sc.copy(out=f[:, fo, :], in_=psK[j][:]), "a")
                    evk[gfo] = ctx.mark("a")
                ctx.block.scalar(evac)
            rstd, m2 = emit_ln_stats(ctx, f.ap(), 8, TT, fbf.ap(), sq.ap(), psA, psB, st.ap(),
                                     ones_bf, ones_f32, cns[0:1, 0:1])
            emit_ln_normalize(ctx, f.ap(), 8, TT, fn.ap(), rstd, m2, psR, psM, ones1, tmp.ap())
            def resid(v):
                ctx.wait(v, "d", "o")
                ctx.inc(v.tensor_tensor(sq[:], fn[:], x3t[:], OP.add), "v")
            ctx.block.vector(resid)
            def dma_out(g, t0=t0):
                ctx.wait(g, "v")
                ctx.inc(g.dma_start(out=ctx.io["y"].rearrange("(c p) t -> p c t", p=128)[:, :, t0:t0 + TT], in_=sq[:]), "o", 16)
            ctx.block.gpsimd(dma_out)
        def fin(g):
            ctx.wait(g, "o")
            g.nop()
        ctx.block.gpsimd(fin)


PHASE_EMIT = {
    "p1": emit_p1,
    "p2f": lambda ctx: emit_p2(ctx, "f"), "p2b": lambda ctx: emit_p2(ctx, "b"),
    "p3f": lambda ctx: emit_p3(ctx, "f"), "p3b": lambda ctx: emit_p3(ctx, "b"),
    "p4f": lambda ctx: emit_p4(ctx, "f"), "p4b": lambda ctx: emit_p4(ctx, "b"),
    "p45a": emit_p45a, "p45b": emit_p45b,
    "p5": emit_p5, "p6": emit_p6, "p7": emit_p7,
    "p8a": emit_p8a, "p8b1": emit_p8b1, "p8b2": emit_p8b2,
}


def build_phase_nc(phase, core_input_names, fused=False):
    nc = bass.Bass("TRN2", target_bir_lowering=False)
    io = {}
    phases = phase if isinstance(phase, list) else [phase]
    wts, sins, souts = set(), set(), set()
    for ph in phases:
        w, si, so = PHASE_IO[ph]
        wts |= set(w); sins |= set(si) - souts; souts |= set(so)
    for w in sorted(wts):
        shape, dt = SHARED_SPECS[w]
        shape = (shape[0], 1) if len(shape) == 1 else shape
        io[w] = nc.dram_tensor(w, list(shape), dt, kind="ExternalInput").ap()
    if fused:
        io["x"] = nc.dram_tensor("x", [DM, SEQ], F32, kind="ExternalInput").ap()
        io["y"] = nc.dram_tensor("y", [DM, SEQ], F32, kind="ExternalOutput").ap()
        for s in sorted((sins | souts) - {"x", "y"}):
            shape, dt = SCRATCH_SPECS[s]
            io[s] = nc.dram_tensor(s, list(shape), dt, kind="Internal").ap()
    else:
        for s in sorted(sins):
            if s == "x":
                io["x"] = nc.dram_tensor("x", [DM, SEQ], F32, kind="ExternalInput").ap()
            else:
                shape, dt = SCRATCH_SPECS[s]
                io[s] = nc.dram_tensor(s, list(shape), dt, kind="ExternalInput").ap()
        for s in sorted(souts):
            if s == "y":
                io["y"] = nc.dram_tensor("y", [DM, SEQ], F32, kind="ExternalOutput").ap()
            else:
                shape, dt = SCRATCH_SPECS[s]
                io[s] = nc.dram_tensor(s, list(shape), dt, kind="ExternalOutput").ap()
    with ExitStack() as stack:
        sems = {k: stack.enter_context(nc.semaphore("s_" + k)) for k in SEM_NAMES}
        block = stack.enter_context(nc.Block())
        ctx = Ctx(nc, block, sems, io)
        for i, ph in enumerate(phases):
            if i > 0:
                ctx.barrier()
            PHASE_EMIT[ph](ctx)
    return nc, sorted(wts), sorted(sins), sorted(souts)


PHASES = ["p1", "p2f", "p2b", "p3f", "p3b", "p4f", "p4b", "p45a", "p45b", "p5", "p6", "p7", "p8a", "p8b1", "p8b2"]


def run_phases(x_cores, shared):
    ncore = len(x_cores)
    state = [{"x": x_cores[i]} for i in range(ncore)]
    for ph in PHASES:
        nc, wts, sins, souts = build_phase_nc(ph, None)
        in_maps = []
        for i in range(ncore):
            m = {}
            for w in wts:
                a = shared[w]
                m[w] = a.reshape(-1, 1) if a.ndim == 1 else a
            for s in sins:
                m[s] = state[i][s]
            in_maps.append(m)
        res = run_bass_kernel_spmd(nc, in_maps, core_ids=list(range(ncore)))
        for i in range(ncore):
            for s in souts:
                state[i][s] = np.asarray(res.results[i][s])
        later = set()
        for ph2 in PHASES[PHASES.index(ph) + 1:]:
            later |= set(PHASE_IO[ph2][1])
        for i in range(ncore):
            for s in [k for k in state[i] if k not in later and k != "y"]:
                del state[i][s]
    return state


def run_fused(x_cores, shared):
    nc, wts, sins, souts = build_phase_nc(PHASES, None, fused=True)
    in_maps = []
    for i in range(len(x_cores)):
        m = {}
        for w in wts:
            a = shared[w]
            m[w] = np.ascontiguousarray(a.reshape(-1, 1)) if a.ndim == 1 else a
        m["x"] = x_cores[i]
        in_maps.append(m)
    res = run_bass_kernel_spmd(nc, in_maps, core_ids=list(range(len(x_cores))))
    return [np.asarray(r["y"]) for r in res.results]


def kernel(**inputs):
    shared = _prep_shared(inputs)
    x = inputs["L0_attn_x"]
    x_cores = [np.ascontiguousarray(x[b].T) for b in range(8)]
    if FUSED:
        ys = run_fused(x_cores, shared)
    else:
        state = run_phases(x_cores, shared)
        ys = [state[b]["y"] for b in range(8)]
    out = np.stack([y.T for y in ys], axis=0)
    return np.ascontiguousarray(out)


# version 07 (final)
# speedup vs baseline: 1.1824x; 1.0060x over previous
import numpy as np
import ml_dtypes
import concourse.bass as bass
import concourse.mybir as mybir
from concourse.bass_utils import run_bass_kernel_spmd
from concourse.alu_op_type import AluOpType as OP
from contextlib import ExitStack

F32 = mybir.dt.float32
BF16 = mybir.dt.bfloat16
U8 = mybir.dt.uint8
I32 = mybir.dt.int32
AF = mybir.ActivationFunctionType
BF = ml_dtypes.bfloat16

DM = 1024
SEQ = 8192
DI = 2048
NST = 16
DTR = 64
H4 = 4096
H3 = 3072
TT = 512
NTILE = SEQ // TT
EPS_LN = 1e-5
EPS_L1 = 1e-8

FUSED = True

def _prep_shared(inp):
    s = {}
    for k in ["L0_attn_in_proj_w", "L0_attn_conv_w", "L0_attn_conv_b",
              "L0_attn_x_proj_w", "L0_attn_dt_proj_w", "L0_attn_dt_proj_b",
              "L0_attn_A_log", "L0_attn_D", "L0_attn_out_proj_w",
              "L0_attn_ln_g", "L0_attn_ln_b", "L0_attn_ff_w1", "L0_attn_ff_b1",
              "L0_attn_ff_w2", "L0_attn_ff_b2", "L0_ffn_mu_r", "L0_ffn_mu_k"]:
        s[k.split("L0_attn_")[-1] if k.startswith("L0_attn_") else k] = np.ascontiguousarray(inp[k])
    s["WrT"] = np.ascontiguousarray(inp["L0_ffn_W_r"].T)
    s["WkT"] = np.ascontiguousarray(inp["L0_ffn_W_k"].T)
    s["WvT"] = np.ascontiguousarray(inp["L0_ffn_W_v"].T)
    s["L1WT"] = np.ascontiguousarray(inp["L1_attn_W"].T)
    s["wgT"] = np.ascontiguousarray(inp["L1_ffn_w_gate"].T)
    s["wuT"] = np.ascontiguousarray(inp["L1_ffn_w_up"].T)
    s["wdT"] = np.ascontiguousarray(inp["L1_ffn_w_down"].T)
    for k in ["conv_b", "dt_proj_b", "D", "ln_b", "ff_b2", "ln_g", "ff_b1",
              "L0_ffn_mu_r", "L0_ffn_mu_k"]:
        s[k] = np.ascontiguousarray(s[k].reshape(-1, 128).T)
    s["conv_w"] = np.ascontiguousarray(s["conv_w"].reshape(16, 128, 4).transpose(1, 0, 2))
    s["A_log"] = np.ascontiguousarray(s["A_log"].reshape(16, 128, 16).transpose(1, 0, 2))
    s["oh16"] = np.repeat(np.eye(16, dtype=np.float32), 128, axis=1).astype(BF)
    s["id128"] = np.eye(128, dtype=np.float32).astype(BF)
    return s

SHARED_SPECS = {
    "in_proj_w": ((DM, 2 * DI), F32), "conv_w": ((128, 16, 4), F32), "conv_b": ((128, 16), F32),
    "x_proj_w": ((DI, 96), F32), "dt_proj_w": ((DTR, DI), F32), "dt_proj_b": ((128, 16), F32),
    "A_log": ((128, 16, 16), F32), "D": ((128, 16), F32), "out_proj_w": ((DI, DM), F32),
    "ln_g": ((128, 8), F32), "ln_b": ((128, 8), F32), "ff_w1": ((DM, H4), F32), "ff_b1": ((128, 32), F32),
    "ff_w2": ((H4, DM), F32), "ff_b2": ((128, 8), F32), "L0_ffn_mu_r": ((128, 8), F32),
    "L0_ffn_mu_k": ((128, 8), F32), "WrT": ((DM, DM), F32), "WkT": ((DM, DM), F32),
    "WvT": ((DM, DM), F32), "L1WT": ((DM, H3), F32), "wgT": ((DM, H3), I32),
    "wuT": ((DM, H3), I32), "wdT": ((H3, DM), I32),
    "oh16": ((16, 2048), BF16), "id128": ((128, 128), BF16),
}

SCRATCH_SPECS = {
    "xc": ((DI, SEQ), F32), "z": ((DI, SEQ), BF16),
    "xcs_f": ((DI, SEQ), F32), "xcs_b": ((DI, SEQ), F32),
    "delta_f": ((DI, SEQ), F32), "delta_b": ((DI, SEQ), F32),
    "u_f": ((DI, SEQ), BF16), "u_b": ((DI, SEQ), BF16),
    "Bt_f": ((16, SEQ), BF16), "Bt_b": ((16, SEQ), BF16),
    "Ct_f": ((16, SEQ), BF16), "Ct_b": ((16, SEQ), BF16),
    "ys_f": ((DI, SEQ), BF16), "ys_b": ((DI, SEQ), BF16),
    "mf": ((DM, SEQ), F32), "x1": ((DM, SEQ), F32), "x2": ((DM, SEQ), F32),
    "lf": ((DM, SEQ), F32), "lv": ((DM, SEQ), F32), "ht": ((DM, SEQ), F32),
    "mo": ((DM, SEQ), F32), "g1": ((H4, SEQ), BF16), "w3": ((DM, SEQ), BF16),
    "x3": ((DM, SEQ), F32), "hh": ((H3, SEQ), BF16),
}

PHASE_IO = {
    "p1":  (["in_proj_w"], ["x"], ["xc", "z"]),
    "p2f": (["conv_w", "conv_b", "x_proj_w", "dt_proj_w", "dt_proj_b"], ["xc"],
            ["xcs_f", "delta_f", "u_f", "Bt_f", "Ct_f"]),
    "p2b": (["conv_w", "conv_b", "x_proj_w", "dt_proj_w", "dt_proj_b"], ["xc"],
            ["xcs_b", "delta_b", "u_b", "Bt_b", "Ct_b"]),
    "p3f": (["A_log", "oh16", "id128"], ["delta_f", "u_f", "Bt_f", "Ct_f"], ["ys_f"]),
    "p3b": (["A_log", "oh16", "id128"], ["delta_b", "u_b", "Bt_b", "Ct_b"], ["ys_b"]),
    "p4f": (["out_proj_w", "D"], ["ys_f", "xcs_f", "z"], ["mf"]),
    "p4b": (["out_proj_w", "D"], ["ys_b", "xcs_b", "z", "mf"], ["mo"]),
    "p45a": (["ff_w1", "ff_b1", "ln_g", "ln_b"], ["mo"], ["g1"]),
    "p45b": (["ff_w2", "ff_b2"], ["g1", "x"], ["x1"]),
    "p5": (["WrT", "WkT", "WvT", "L0_ffn_mu_r", "L0_ffn_mu_k"], ["x1"], ["x2"]),
    "p6": (["L1WT"], ["x2"], ["lf", "lv"]),
    "p7": ([], ["lf", "lv"], ["ht"]),
    "p8a": ([], ["x2", "ht"], ["x3", "w3"]),
    "p8b1": (["wgT", "wuT"], ["w3"], ["hh"]),
    "p8b2": (["wdT"], ["hh", "x3"], ["y"]),
}

NP_DT = {F32: np.float32, BF16: BF, I32: np.int32, U8: np.uint8}


class Ctx:
    def __init__(self, nc, block, sems, io):
        self.nc = nc
        self.block = block
        self.sems = sems
        self.cnt = {k: 0 for k in sems}
        self.io = io
        self.sb = {}

    def inc(self, instr, sem, n=1):
        instr.then_inc(self.sems[sem], n)
        self.cnt[sem] += n

    def wait(self, eng, *sems):
        for s in sems:
            if self.cnt[s] > 0:
                eng.wait_ge(self.sems[s], self.cnt[s])

    def wait_val(self, eng, sem, val):
        if val > 0:
            eng.wait_ge(self.sems[sem], val)

    def mark(self, sem):
        return self.cnt[sem]

    def barrier(self):
        cnts = dict(self.cnt)
        for eng_name in ["sync", "vector", "scalar", "tensor", "gpsimd"]:
            def bar(eng, cnts=cnts):
                for s, c in cnts.items():
                    if c > 0:
                        eng.wait_ge(self.sems[s], c)
            getattr(self.block, eng_name)(bar)


SEM_NAMES = ["d", "o", "v", "a", "p", "g"]


def load_weights_bf16(ctx, dst, src_ap, K, M, stg, name, mcap=4096):
    nc = ctx.nc
    nk = K // 128
    for ki in range(nk):
        for c0 in range(0, M, mcap):
            mc = min(mcap, M - c0)
            def dma(sync, ki=ki, c0=c0, mc=mc):
                ctx.wait(sync, "a")
                ctx.inc(sync.dma_start(out=stg[:, :mc], in_=src_ap[ki * 128:(ki + 1) * 128, c0:c0 + mc]), "d", 16)
            ctx.block.sync(dma)
            def cast(sc, ki=ki, c0=c0, mc=mc):
                ctx.wait(sc, "d")
                ctx.inc(sc.copy(out=dst[:, ki, c0:c0 + mc], in_=stg[:, :mc]), "a")
            ctx.block.scalar(cast)


def emit_ln_stats(ctx, xa_view, nch, T, xbf_view, sq, psA, psB, st, ones_bf, ones_f32, eps_ap):
    nc = ctx.nc
    def cast(v):
        ctx.wait(v, "d", "p", "a")
        ctx.inc(v.tensor_copy(xbf_view[:], xa_view[:]), "v")
    ctx.block.vector(cast)
    def square(sc):
        ctx.wait(sc, "d", "p", "v", "o")
        ctx.inc(sc.activation(out=sq[:], in_=xa_view[:], func=AF.Square), "a")
    ctx.block.scalar(square)
    def mms(t):
        ctx.wait(t, "v", "a")
        for c in range(nch):
            ins = t.matmul(psA[:], ones_bf[:, 0:1], xbf_view[:, c, :], start=(c == 0), stop=(c == nch - 1))
        ctx.inc(ins, "p")
        for c in range(nch):
            ins = t.matmul(psB[:], ones_f32[:, 0:1], sq[:, c, :], start=(c == 0), stop=(c == nch - 1))
        ctx.inc(ins, "p")
    ctx.block.tensor(mms)
    inv = 1.0 / (nch * 128)
    mu, ms, mu2, var, sd = (st[:, 2 * T:3 * T], st[:, 3 * T:4 * T], st[:, 4 * T:5 * T],
                            st[:, 5 * T:6 * T], st[:, 6 * T:7 * T])
    def stats_a(sc):
        ctx.wait(sc, "p")
        sc.activation(out=mu, in_=psA[:], func=AF.Copy, scale=inv)
        ctx.inc(sc.activation(out=ms, in_=psB[:], func=AF.Copy, scale=inv), "a")
    ctx.block.scalar(stats_a)
    def stats_v(v):
        ctx.wait(v, "a")
        v.tensor_tensor(mu2, mu, mu, OP.mult)
        ctx.inc(v.tensor_tensor(var, ms, mu2, OP.subtract), "v")
    ctx.block.vector(stats_v)
    rstd, m2 = st[:, 0:T], st[:, T:2 * T]
    def stats_a2(sc):
        ctx.wait(sc, "v")
        sc.activation(out=sd, in_=var, func=AF.Ln, bias=eps_ap)
        ctx.inc(sc.activation(out=rstd, in_=sd, func=AF.Exp, scale=-0.5), "a")
    ctx.block.scalar(stats_a2)
    def stats_v2(v):
        ctx.wait(v, "a")
        ctx.inc(v.tensor_tensor(m2, mu, rstd, OP.mult), "v")
    ctx.block.vector(stats_v2)
    return rstd, m2


def emit_ln_normalize(ctx, xa_view, nch, T, out_view, rstd, m2, psR, psM, ones1_f32, tmp,
                      gain=None, bias_=None):
    def bcast(t):
        ctx.wait(t, "v")
        t.matmul(psR[:], ones1_f32[:, :], rstd, start=True, stop=True)
        ins = t.matmul(psM[:], ones1_f32[:, :], m2, start=True, stop=True)
        ctx.inc(ins, "p")
    ctx.block.tensor(bcast)
    def norm(v):
        ctx.wait(v, "p")
        for c in range(nch):
            v.tensor_tensor(tmp[:], xa_view[:, c, :], psR[:], OP.mult)
            if gain is None:
                ins = v.tensor_tensor(out_view[:, c, :], tmp[:], psM[:], OP.subtract)
            else:
                v.tensor_tensor(tmp[:], tmp[:], psM[:], OP.subtract)
                ins = v.tensor_scalar(out_view[:, c, :], tmp[:], gain[c], bias_[c], OP.mult, OP.add)
        ctx.inc(ins, "v")
    ctx.block.vector(norm)


def emit_p1(ctx):
    nc = ctx.nc
    x = ctx.io["x"]
    xcd, zd = ctx.io["xc"], ctx.io["z"]
    wsrc = ctx.io["in_proj_w"]
    with ExitStack() as es:
        def sb(name, shape, dt):
            return es.enter_context(nc.sbuf_tensor(name, shape, dt))
        def psum(name, shape):
            return es.enter_context(nc.psum_tensor(name, shape, F32))
        xa = [sb("p1_xa%d" % i, [128, 8, TT], F32) for i in range(2)]
        sq = [sb("p1_sq%d" % i, [128, 8, TT], F32) for i in range(2)]
        xbf = [sb("p1_xbf%d" % i, [128, 8, TT], BF16) for i in range(2)]
        xln = [sb("p1_xln%d" % i, [128, 8, TT], BF16) for i in range(2)]
        st = sb("p1_st", [1, 7 * TT], F32)
        tmp = sb("p1_tmp", [128, TT], F32)
        ones_bf = sb("p1_onesb", [128, 1], BF16)
        ones_f32 = sb("p1_onesf", [128, 1], F32)
        ones1 = sb("p1_ones1", [1, 128], F32)
        cns = sb("p1_cns", [128, 4], F32)
        wip = sb("p1_w", [128, 8, 32 * 128], BF16)
        stg = sb("p1_stg", [128, 2048], F32)
        ev = sb("p1_ev", [128, 4, TT], F32)
        psA = psum("p1_psA", [1, TT])
        psB = psum("p1_psB", [1, TT])
        psR = psum("p1_psR", [128, TT])
        psM = psum("p1_psM", [128, TT])
        psK = [psum("p1_ps%d" % i, [128, TT]) for i in range(4)]
        def init(v):
            v.memset(ones_bf[:], 1.0)
            v.memset(ones_f32[:], 1.0)
            v.memset(cns[:, 0:1], EPS_LN)
            ctx.inc(v.memset(ones1[:], 1.0), "v")
        ctx.block.vector(init)
        load_weights_bf16(ctx, wip.ap(), wsrc, DM, 4096, stg, "wip", mcap=2048)
        wip_v = wip.ap().rearrange("p k (f j) -> p k f j", j=128)
        tile_vmark = {}
        tile_amark = {}
        evk, dmk, mmk = {}, {}, {}
        for it in range(NTILE):
            t0 = it * TT
            tb = it % 2
            def dma_in(sync, t0=t0, it=it, tb=tb):
                if it >= 2:
                    ctx.wait_val(sync, "v", tile_vmark[it - 2])
                    ctx.wait_val(sync, "a", tile_amark[it - 2])
                ctx.inc(sync.dma_start(
                    out=xa[tb][:], in_=x.rearrange("(c p) t -> p c t", p=128)[:, :, t0:t0 + TT]), "d", 16)
            ctx.block.sync(dma_in)
            rstd, m2 = emit_ln_stats(ctx, xa[tb].ap(), 8, TT, xbf[tb].ap(), sq[tb].ap(), psA, psB, st.ap(),
                                     ones_bf, ones_f32, cns[0:1, 0:1])
            emit_ln_normalize(ctx, xa[tb].ap(), 8, TT, xln[tb].ap(), rstd, m2, psR, psM, ones1, tmp.ap())
            tile_vmark[it] = ctx.mark("v")
            tile_amark[it] = ctx.mark("a")
            nvm = tile_vmark[it]
            for fo in range(32):
                j = fo % 4
                gfo = it * 32 + fo
                def mm(t, fo=fo, j=j, gfo=gfo, nvm=nvm, tb=tb):
                    ctx.wait_val(t, "v", nvm)
                    if (gfo - 4) in evk:
                        ctx.wait_val(t, "a", evk[gfo - 4])
                    for ki in range(8):
                        ins = t.matmul(psK[j][:], wip_v[:, ki, fo, :], xln[tb][:, ki, :],
                                       start=(ki == 0), stop=(ki == 7))
                    ctx.inc(ins, "p")
                    mmk[gfo] = ctx.mark("p")
                ctx.block.tensor(mm)
                def evac(sc, fo=fo, j=j, gfo=gfo):
                    ctx.wait_val(sc, "p", mmk[gfo])
                    if (gfo - 4) in dmk:
                        ctx.wait_val(sc, "o", dmk[gfo - 4])
                    ctx.inc(sc.copy(out=ev[:, fo % 4, :], in_=psK[j][:]), "a")
                    evk[gfo] = ctx.mark("a")
                ctx.block.scalar(evac)
                def dma_out(g, fo=fo, t0=t0, gfo=gfo):
                    ctx.wait_val(g, "a", evk[gfo])
                    dst = xcd if fo < 16 else zd
                    r0 = (fo % 16) * 128
                    ctx.inc(g.dma_start(out=dst[r0:r0 + 128, t0:t0 + TT], in_=ev[:, fo % 4, :]), "o", 16)
                    dmk[gfo] = ctx.mark("o")
                ctx.block.gpsimd(dma_out)
        def fin(g):
            ctx.wait(g, "o")
            g.nop()
        ctx.block.gpsimd(fin)


def emit_p2(ctx, dirn):
    nc = ctx.nc
    xcd = ctx.io["xc"]
    xcs_d, delta_d, u_d = ctx.io["xcs_" + dirn], ctx.io["delta_" + dirn], ctx.io["u_" + dirn]
    Bt_d, Ct_d = ctx.io["Bt_" + dirn], ctx.io["Ct_" + dirn]
    with ExitStack() as es:
        def sb(name, shape, dt):
            return es.enter_context(nc.sbuf_tensor("p2%s_%s" % (dirn, name), shape, dt))
        def psum(name, shape):
            return es.enter_context(nc.psum_tensor("p2%s_%s" % (dirn, name), shape, F32))
        xe = [sb("xe%d" % i, [128, 16, TT + 3], F32) for i in range(2)]
        pb = [sb("pb%d" % i, [128, 4, TT], F32) for i in range(2)]
        xcs = sb("xcs", [128, 16, TT], F32)
        xcsb = sb("xcsb", [128, 16, TT], BF16)
        acc = sb("acc", [128, TT], F32)
        acc2 = sb("acc2", [128, TT], F32)
        accR = [sb("accR%d" % i, [128, TT], F32) for i in range(2)]
        delta = sb("delta", [128, 16, TT], F32)
        ub = sb("ub", [128, 16, TT], BF16)
        dtb = sb("dtb", [64, TT], BF16)
        bct = sb("bct", [16, 2, TT], BF16)
        cw = sb("cw", [128, 16, 4], F32)
        cb = sb("cb", [128, 16], F32)
        dpb = sb("dpb", [128, 16], F32)
        onep = sb("onep", [128, 1], F32)
        xpw = sb("xpw", [128, 16, 96], BF16)
        dtw = sb("dtw", [64, 16, 128], BF16)
        stg = sb("stg", [128, 2048], F32)
        psDT = psum("psDT", [64, TT])
        psB16 = psum("psB16", [16, TT])
        psC16 = psum("psC16", [16, TT])
        psK = [psum("ps%d" % i, [128, TT]) for i in range(4)]
        def dma_par(sync):
            ctx.inc(sync.dma_start(out=cw[:], in_=ctx.io["conv_w"]), "d", 16)
            ctx.inc(sync.dma_start(out=cb[:], in_=ctx.io["conv_b"]), "d", 16)
            ctx.inc(sync.dma_start(out=dpb[:], in_=ctx.io["dt_proj_b"]), "d", 16)
        ctx.block.sync(dma_par)
        def init_v(v):
            ctx.inc(v.memset(onep[:], 1.0), "v")
        ctx.block.vector(init_v)
        load_weights_bf16(ctx, xpw.ap(), ctx.io["x_proj_w"], DI, 96, stg, "xpw")
        def dma_dtw(sync):
            ctx.wait(sync, "a")
            ctx.inc(sync.dma_start(out=stg[0:64, :2048], in_=ctx.io["dt_proj_w"]), "d", 16)
        ctx.block.sync(dma_dtw)
        def cast_dtw(sc):
            ctx.wait(sc, "d")
            ctx.inc(sc.copy(out=dtw.ap().rearrange("p a b -> p (a b)"), in_=stg[0:64, :2048]), "a")
        ctx.block.scalar(cast_dtw)

        dmark = {}
        silu_mark = {}
        silu_cmark = {}
        conv_mark = {}
        cast_mark = {}
        mmxp_mark = {}
        evxp_mark = {}
        bc_mark = {}
        u_mark = {}
        st_mark = {}
        sp_mark = {}
        for it in range(NTILE):
            t0 = it * TT
            tb = it % 2
            if dirn == "f":
                lo, hi = t0 - 3, t0 + TT
                pad_lo = max(0, -lo)
                pad_hi = 0
                shifts = [0, 1, 2, 3]
            else:
                lo, hi = t0, t0 + TT + 3
                pad_lo = 0
                pad_hi = max(0, hi - SEQ)
                shifts = [3, 2, 1, 0]
            def dma_in(sync, it=it, tb=tb, lo=lo, hi=hi, pad_lo=pad_lo, pad_hi=pad_hi):
                if (it - 2) in sp_mark:
                    ctx.wait_val(sync, "v", sp_mark[it - 2])
                if dirn == "f":
                    ctx.inc(sync.dma_start(
                        out=xe[tb][:, :, pad_lo:TT + 3],
                        in_=xcd.rearrange("(c p) t -> p c t", p=128)[:, :, lo + pad_lo:hi]), "d", 16)
                else:
                    ctx.inc(sync.dma_start(
                        out=xe[tb][:, :, 0:TT + 3 - pad_hi],
                        in_=xcd.rearrange("(c p) t -> p c t", p=128)[:, :, lo:hi - pad_hi]), "d", 16)
                dmark[it] = ctx.mark("d")
            ctx.block.sync(dma_in)
            if pad_lo or pad_hi:
                def pad(g, it=it, tb=tb, pad_lo=pad_lo, pad_hi=pad_hi):
                    ctx.wait(g, "v", "a", "p")
                    if dirn == "f":
                        ctx.inc(g.memset(xe[tb][:, :, 0:pad_lo], 0.0), "g")
                    else:
                        ctx.inc(g.memset(xe[tb][:, :, TT + 3 - pad_hi:TT + 3], 0.0), "g")
                ctx.block.gpsimd(pad)
            for c in range(16):
                def conv(v, c=c, it=it, tb=tb, shifts=shifts):
                    ctx.wait_val(v, "d", dmark[it])
                    ctx.wait(v, "g")
                    key = (it, c - 2) if c >= 2 else (it - 1, 14 + c)
                    if key in silu_cmark:
                        ctx.wait_val(v, "a", silu_cmark[key])
                    if c == 0 and (it - 1) in u_mark:
                        ctx.wait_val(v, "v", u_mark[it - 1])
                    if c == 0 and (it - 1) in st_mark:
                        ctx.wait_val(v, "o", st_mark[it - 1])
                    s = shifts
                    for k in range(4):
                        v.tensor_scalar(pb[tb][:, k, :], xe[tb][:, c, s[k]:s[k] + TT], cw[:, c, k:k + 1], None, OP.mult)
                    v.tensor_tensor(acc[:], pb[tb][:, 0, :], pb[tb][:, 1, :], OP.add)
                    v.tensor_tensor(acc2[:], pb[tb][:, 2, :], pb[tb][:, 3, :], OP.add)
                    ctx.inc(v.tensor_tensor(accR[c % 2][:], acc[:], acc2[:], OP.add), "v")
                    conv_mark[(it, c)] = ctx.mark("v")
                ctx.block.vector(conv)
                def silu(sc, c=c, it=it, tb=tb):
                    ctx.wait_val(sc, "v", conv_mark[(it, c)])
                    ctx.inc(sc.activation(out=xcs[:, c, :], in_=accR[c % 2][:], func=AF.Silu, bias=cb[:, c:c + 1]), "a")
                    silu_cmark[(it, c)] = ctx.mark("a")
                    if c == 15:
                        silu_mark[it] = ctx.mark("a")
                ctx.block.scalar(silu)
            def cast_xcs(v, it=it):
                ctx.wait_val(v, "a", silu_mark[it])
                if (it - 1) in mmxp_mark:
                    ctx.wait_val(v, "p", mmxp_mark[it - 1])
                ctx.inc(v.tensor_copy(xcsb[:], xcs[:]), "v")
                cast_mark[it] = ctx.mark("v")
            ctx.block.vector(cast_xcs)
            def mm_xp(t, it=it):
                ctx.wait_val(t, "v", cast_mark[it])
                if (it - 1) in evxp_mark:
                    ctx.wait_val(t, "a", evxp_mark[it - 1])
                for ki in range(16):
                    ins = t.matmul(psDT[:], xpw[:, ki, 0:64], xcsb[:, ki, :], start=(ki == 0), stop=(ki == 15))
                for ki in range(16):
                    ins2 = t.matmul(psB16[:], xpw[:, ki, 64:80], xcsb[:, ki, :], start=(ki == 0), stop=(ki == 15))
                for ki in range(16):
                    ins3 = t.matmul(psC16[:], xpw[:, ki, 80:96], xcsb[:, ki, :], start=(ki == 0), stop=(ki == 15))
                ctx.inc(ins3, "p")
                mmxp_mark[it] = ctx.mark("p")
            ctx.block.tensor(mm_xp)
            def evac_xp(sc, it=it):
                ctx.wait_val(sc, "p", mmxp_mark[it])
                if (it - 1) in bc_mark:
                    ctx.wait_val(sc, "o", bc_mark[it - 1])
                sc.copy(out=dtb[:], in_=psDT[:])
                sc.copy(out=bct[:, 0, :], in_=psB16[:])
                ctx.inc(sc.copy(out=bct[:, 1, :], in_=psC16[:]), "a")
                evxp_mark[it] = ctx.mark("a")
            ctx.block.scalar(evac_xp)
            def dma_bc(g, t0=t0, it=it):
                ctx.wait_val(g, "a", evxp_mark[it])
                ctx.inc(g.dma_start(out=Bt_d[:, t0:t0 + TT], in_=bct[:, 0, :]), "o", 16)
                ctx.inc(g.dma_start(out=Ct_d[:, t0:t0 + TT], in_=bct[:, 1, :]), "o", 16)
                bc_mark[it] = ctx.mark("o")
            ctx.block.gpsimd(dma_bc)
            mmk2, evk2 = {}, {}
            for mo in range(16):
                j = mo % 4
                def mm_dt(t, mo=mo, j=j, it=it):
                    ctx.wait_val(t, "a", evxp_mark[it] if mo < 4 else max(evxp_mark[it], evk2[mo - 4]))
                    if mo == 0 and (it - 1) in sp_mark:
                        ctx.wait_val(t, "v", sp_mark[it - 1])
                    ctx.inc(t.matmul(psK[j][:], dtw[:, mo, :], dtb[:], start=True, stop=True), "p")
                    mmk2[mo] = ctx.mark("p")
                ctx.block.tensor(mm_dt)
                def dt_ev(sc, mo=mo, j=j, it=it):
                    ctx.wait_val(sc, "p", mmk2[mo])
                    if mo == 0 and (it - 1) in st_mark:
                        ctx.wait_val(sc, "o", st_mark[it - 1])
                    if mo == 0 and (it - 1) in u_mark:
                        ctx.wait_val(sc, "v", u_mark[it - 1])
                    ctx.inc(sc.activation(out=delta[:, mo, :], in_=psK[j][:], func=AF.Identity,
                                          bias=dpb[:, mo:mo + 1]), "a")
                    evk2[mo] = ctx.mark("a")
                ctx.block.scalar(dt_ev)
            for hh in range(2):
                if hh == 0:
                    A_, B_ = xe[tb][:, 0:8, 0:TT], xe[tb][:, 8:16, 0:TT]
                else:
                    A_, B_ = xe[tb][:, 8:16, 0:TT], xe[tb][:, 0:8, 0:TT]
                dlt = delta[:, hh * 8:hh * 8 + 8, :]
                def soft_a(sc, A_=A_, B_=B_, dlt=dlt, it=it, hh=hh):
                    ctx.wait_val(sc, "a", evk2[15])
                    ctx.wait_val(sc, "v", conv_mark[(it, 15)])
                    sc.activation(out=A_, in_=dlt, func=AF.Abs)
                    sc.activation(out=B_, in_=A_, func=AF.Exp, scale=-1.0)
                    sc.activation(out=A_, in_=B_, func=AF.Ln, bias=onep[:, 0:1])
                    ctx.inc(sc.activation(out=B_, in_=dlt, func=AF.Relu), "a")
                ctx.block.scalar(soft_a)
                def soft_v(v, A_=A_, B_=B_, dlt=dlt, it=it, hh=hh):
                    ctx.wait(v, "a")
                    ctx.inc(v.tensor_tensor(dlt, A_, B_, OP.add), "v")
                    if hh == 1:
                        sp_mark[it] = ctx.mark("v")
                ctx.block.vector(soft_v)
            def mk_u(v, it=it):
                ctx.wait_val(v, "v", sp_mark[it])
                ctx.inc(v.tensor_tensor(ub[:], delta[:], xcs[:], OP.mult), "v")
                u_mark[it] = ctx.mark("v")
            ctx.block.vector(mk_u)
            def dma_out(sc, t0=t0, it=it):
                ctx.wait_val(sc, "v", u_mark[it])
                ctx.inc(sc.dma_start(out=xcs_d.rearrange("(c p) t -> p c t", p=128)[:, :, t0:t0 + TT], in_=xcs[:]), "o", 16)
                ctx.inc(sc.dma_start(out=delta_d.rearrange("(c p) t -> p c t", p=128)[:, :, t0:t0 + TT], in_=delta[:]), "o", 16)
                ctx.inc(sc.dma_start(out=u_d.rearrange("(c p) t -> p c t", p=128)[:, :, t0:t0 + TT], in_=ub[:]), "o", 16)
                st_mark[it] = ctx.mark("o")
            ctx.block.scalar(dma_out)
        def fin(g):
            ctx.wait(g, "o")
            g.nop()
        ctx.block.gpsimd(fin)


TB = 1024
NTB = SEQ // TB

def emit_p3(ctx, dirn):
    nc = ctx.nc
    delta_d, u_d = ctx.io["delta_" + dirn], ctx.io["u_" + dirn]
    Bt_d, Ct_d, ys_d = ctx.io["Bt_" + dirn], ctx.io["Ct_" + dirn], ctx.io["ys_" + dirn]
    rev = (dirn == "b")
    with ExitStack() as es:
        def sb(name, shape, dt):
            return es.enter_context(nc.sbuf_tensor("p3%s_%s" % (dirn, name), shape, dt))
        def psum(name, shape):
            return es.enter_context(nc.psum_tensor("p3%s_%s" % (dirn, name), shape, F32))
        dall = [sb("dall%d" % i, [128, 8, TB], F32) for i in range(2)]
        uall = [sb("uall%d" % i, [128, 8, TB], BF16) for i in range(2)]
        Bbc = sb("Bbc", [128, 16, TB], BF16)
        Cbc = sb("Cbc", [128, 16, TB], BF16)
        at = [sb("at%d" % i, [128, TB], BF16) for i in range(8)]
        bt0 = [sb("bt0%d" % i, [128, TB], BF16) for i in range(2)]
        ht = [sb("ht%d" % i, [128, TB], BF16) for i in range(2)]
        hct = [sb("hct%d" % i, [128, TB], BF16) for i in range(4)]
        ysb = [sb("ysb%d" % i, [128, TB], BF16) for i in range(2)]
        carry = sb("carry", [128, 256], F32)
        zcol = sb("zcol", [128, 1], F32)
        asc = sb("asc", [128, 16, 16], F32)
        ascr = sb("ascr", [128, 16, 16], F32)
        bc16 = sb("bc16", [16, 2, TB], BF16)
        oh = sb("oh", [16, 2048], BF16)
        idt = sb("idt", [128, 128], BF16)
        psBC = [psum("psBC%d" % i, [128, TB]) for i in range(2)]
        psY = [psum("psY%d" % i, [128, TB]) for i in range(2)]
        def dma_par(sync):
            ctx.inc(sync.dma_start(out=ascr[:], in_=ctx.io["A_log"]), "d", 16)
            ctx.inc(sync.dma_start(out=oh[:], in_=ctx.io["oh16"]), "d", 16)
            ctx.inc(sync.dma_start(out=idt[:], in_=ctx.io["id128"]), "d", 16)
        ctx.block.sync(dma_par)
        ascE = at[0].ap()[:, 0:256].rearrange("p (a b) -> p a b", b=16)
        def init_a(sc):
            ctx.wait(sc, "d")
            ctx.inc(sc.activation(out=ascE, in_=ascr[:], func=AF.Exp), "a")
        ctx.block.scalar(init_a)
        def init_v(v):
            ctx.wait(v, "a")
            v.tensor_scalar(asc[:], ascE, -1.0, None, OP.mult)
            ctx.inc(v.memset(zcol[:], 0.0), "v")
        ctx.block.vector(init_v)
        init_vmark = ctx.mark("v")

        tbs = list(range(NTB))
        if rev:
            tbs = tbs[::-1]
        pending_yev = []
        carry_q = []
        half_amark = {}
        half_gmark = {}
        half_vmark = {}
        bc_pmark = {}
        bc_amark = 0
        ev_mark = {}
        dmao_mark = {}
        exp_mark = {}
        b_mark = {}
        bsem = {}
        scan_mark = {}
        y_mark = {}
        bcev_am = {}
        for tbi, tb in enumerate(tbs):
            s0 = tb * TB
            def dma_bc(sync, s0=s0, tbi=tbi):
                if tbi > 0:
                    ctx.wait_val(sync, "p", bc_pmark[tbi - 1])
                ctx.inc(sync.dma_start(out=bc16[:, 0, :], in_=Bt_d[:, s0:s0 + TB]), "d", 16)
                ctx.inc(sync.dma_start(out=bc16[:, 1, :], in_=Ct_d[:, s0:s0 + TB]), "d", 16)
            ctx.block.sync(dma_bc)
            dmark = ctx.mark("d")
            for n in range(16):
                def bc_mm(t, n=n, tbi=tbi):
                    ctx.wait_val(t, "d", dmark)
                    if tbi > 0:
                        i_last = (tbi * 2 - 1) * 128 + 7 * 16 + n
                        if i_last in scan_mark:
                            ctx.wait_val(t, "v", scan_mark[i_last])
                    ctx.wait_val(t, "a", bcev_am.get(n - 1, bc_amark))
                    for seg in range(2):
                        t.matmul(psBC[0][:, seg * 512:(seg + 1) * 512], oh[:, n * 128:(n + 1) * 128],
                                 bc16[:, 0, seg * 512:(seg + 1) * 512], start=True, stop=True)
                        ins = t.matmul(psBC[1][:, seg * 512:(seg + 1) * 512], oh[:, n * 128:(n + 1) * 128],
                                       bc16[:, 1, seg * 512:(seg + 1) * 512], start=True, stop=True)
                    ctx.inc(ins, "p")
                ctx.block.tensor(bc_mm)
                pm = ctx.mark("p")
                def bc_ev(sc, n=n, pm=pm):
                    ctx.wait_val(sc, "p", pm)
                    sc.copy(out=Bbc[:, n, :], in_=psBC[0][:])
                    ctx.inc(sc.copy(out=Cbc[:, n, :], in_=psBC[1][:]), "a")
                ctx.block.scalar(bc_ev)
                bcev_am[n] = ctx.mark("a")
            bc_pmark[tbi] = ctx.mark("p")
            bc_amark = ctx.mark("a")
            stage2_a = ctx.mark("a")

            for half in range(2):
                q = tbi * 2 + half
                hb = q % 2
                def dma_du(sync, s0=s0, half=half, q=q, hb=hb):
                    if q == 0:
                        ctx.wait_val(sync, "v", init_vmark)
                    if q >= 2:
                        ctx.wait_val(sync, "a", half_amark[q - 2])
                        ctx.wait_val(sync, "g", half_gmark[q - 2])
                        ctx.wait_val(sync, "v", half_vmark[q - 2])
                    c0 = half * 8
                    ctx.inc(sync.dma_start(out=dall[hb][:], in_=delta_d.rearrange("(c p) t -> p c t", p=128)[:, c0:c0 + 8, s0:s0 + TB]), "d", 16)
                    ctx.inc(sync.dma_start(out=uall[hb][:], in_=u_d.rearrange("(c p) t -> p c t", p=128)[:, c0:c0 + 8, s0:s0 + TB]), "d", 16)
                ctx.block.sync(dma_du)
                dumark = ctx.mark("d")
                for dci in range(8):
                    dc = half * 8 + dci
                    for gi in range(4):
                        base = q * 128 + dci * 16 + gi * 4
                        if gi == 1 and pending_yev:
                            for fn_ in pending_yev:
                                fn_()
                            pending_yev = []
                        if gi % 2 == 0:
                            def exp_grp(sc, dci=dci, dc=dc, gi=gi, base=base, hb=hb, dumark=dumark):
                                ctx.wait_val(sc, "d", dumark)
                                for k in range(8):
                                    i = base + k
                                    n = gi * 4 + k
                                    if (i - 8) in scan_mark:
                                        ctx.wait_val(sc, "v", scan_mark[i - 8])
                                    ctx.inc(sc.activation(out=at[i % 8][:], in_=dall[hb][:, dci, :], func=AF.Exp,
                                                          scale=asc[:, dc, n:n + 1]), "a")
                                    exp_mark[i] = ctx.mark("a")
                            ctx.block.scalar(exp_grp)
                        def dve_grp(v, dci=dci, dc=dc, gi=gi, base=base, hb=hb, tbi=tbi, dumark=dumark):
                            ctx.wait_val(v, "d", dumark)
                            ctx.wait_val(v, "a", max(bcev_am[gi * 4 + 3], exp_mark[base + 3]))
                            v.tensor_tensor(bt0[0][:], uall[hb][:, dci, :], Bbc[:, gi * 4, :], OP.mult)
                            for k in range(4):
                                i = base + k
                                n = gi * 4 + k
                                slot = dc * 16 + n
                                init = zcol[:, 0:1] if tbi == 0 else carry[:, slot:slot + 1]
                                a_in = at[i % 8][:, ::-1] if rev else at[i % 8][:]
                                b_in = bt0[k % 2][:, ::-1] if rev else bt0[k % 2][:]
                                if (i - 4) in y_mark:
                                    ctx.wait_val(v, "p", y_mark[i - 4])
                                ctx.inc(v.tensor_tensor_scan(ht[i % 2][:], a_in, b_in, init, OP.mult, OP.add), "v")
                                if k < 3:
                                    v.tensor_tensor(bt0[(k + 1) % 2][:], uall[hb][:, dci, :], Bbc[:, n + 1, :], OP.mult)
                                ctx.wait(v, "v")
                                v.tensor_copy(carry[:, slot:slot + 1], ht[i % 2][:, TB - 1:TB])
                                h_in = ht[i % 2][:, ::-1] if rev else ht[i % 2][:]
                                ctx.inc(v.tensor_tensor(hct[i % 4][:], h_in, Cbc[:, n, :], OP.mult), "v")
                                scan_mark[i] = ctx.mark("v")
                        ctx.block.vector(dve_grp)
                        for hk in range(2):
                            def pe_grp(t, dci=dci, dc=dc, gi=gi, base=base, tbi=tbi, hk=hk):
                                b2 = base + hk * 2
                                ctx.wait_val(t, "v", scan_mark[b2 + 1])
                                gdc = tbi * 16 + dc
                                if gi == 0 and hk == 0 and (gdc - 2) in ev_mark:
                                    ctx.wait_val(t, "a", ev_mark[gdc - 2])
                                for k in range(2):
                                    i = b2 + k
                                    n = gi * 4 + hk * 2 + k
                                    for seg in range(2):
                                        ins = t.matmul(psY[dc % 2][:, seg * 512:(seg + 1) * 512], idt[:],
                                                       hct[i % 4][:, seg * 512:(seg + 1) * 512],
                                                       start=(n == 0), stop=(n == 15))
                                ctx.inc(ins, "p")
                                y_mark[b2 + 1] = ctx.mark("p")
                            ctx.block.tensor(pe_grp)
                    gdc = tbi * 16 + dc
                    ymark_last = y_mark[q * 128 + (dc % 8) * 16 + 12 + 3]
                    def mk_yev(dc=dc, s0=s0, gdc=gdc, ymark_last=ymark_last):
                        def y_ev(sc):
                            ctx.wait_val(sc, "p", ymark_last)
                            if (gdc - 2) in dmao_mark:
                                ctx.wait_val(sc, "o", dmao_mark[gdc - 2])
                            ctx.inc(sc.copy(out=ysb[dc % 2][:], in_=psY[dc % 2][:]), "a")
                            ev_mark[gdc] = ctx.mark("a")
                        ctx.block.scalar(y_ev)
                        def y_dma(g):
                            ctx.wait_val(g, "a", ev_mark[gdc])
                            ctx.inc(g.dma_start(out=ys_d[dc * 128:(dc + 1) * 128, s0:s0 + TB], in_=ysb[dc % 2][:]), "o", 16)
                            dmao_mark[gdc] = ctx.mark("o")
                        ctx.block.gpsimd(y_dma)
                    pending_yev.append(mk_yev)
                for fn_ in pending_yev:
                    fn_()
                pending_yev = []
                half_amark[q] = ctx.mark("a")
                half_gmark[q] = ctx.mark("g")
                half_vmark[q] = ctx.mark("v")
        def fin(g):
            ctx.wait(g, "o")
            g.nop()
        ctx.block.gpsimd(fin)


def emit_p4(ctx, dirn):
    nc = ctx.nc
    ys_d, xcs_d, zd = ctx.io["ys_" + dirn], ctx.io["xcs_" + dirn], ctx.io["z"]
    with ExitStack() as es:
        def sb(name, shape, dt):
            return es.enter_context(nc.sbuf_tensor("p4%s_%s" % (dirn, name), shape, dt))
        ys = sb("ys", [128, 16, TT], BF16)
        xcs = sb("xcs", [128, 16, TT], F32)
        zt = sb("zt", [128, 16, TT], BF16)
        ygb = [sb("ygb%d" % i, [128, 16, TT], BF16) for i in range(2)]
        mft2 = [sb("mft2_%d" % i, [128, 8, TT], F32) for i in range(2)]
        t1 = sb("t1", [128, TT], F32)
        t2 = sb("t2", [128, TT], F32)
        Dp = sb("Dp", [128, 16], F32)
        opw = sb("opw", [128, 16, 8 * 128], BF16)
        stg = sb("stg", [128, 1024], F32)
        ev = sb("ev", [128, 4, TT], F32)
        psK = [es.enter_context(nc.psum_tensor("p4%s_ps%d" % (dirn, i), [128, TT], F32)) for i in range(4)]
        def dma_par(sync):
            ctx.inc(sync.dma_start(out=Dp[:], in_=ctx.io["D"]), "d", 16)
        ctx.block.sync(dma_par)
        load_weights_bf16(ctx, opw.ap(), ctx.io["out_proj_w"], DI, 1024, stg, "opw")
        opw_v = opw.ap().rearrange("p k (f j) -> p k f j", j=128)
        evk, mmk, dmk = {}, {}, {}
        gate_mark, mm_last = {}, {}
        sem_ev = "a" if dirn == "f" else "v"
        for it in range(NTILE):
            t0 = it * TT
            tbp = it % 2
            def dma_in(sync, t0=t0, it=it):
                if (it - 1) in gate_mark:
                    ctx.wait_val(sync, "v", gate_mark[it - 1][0])
                    ctx.wait_val(sync, "a", gate_mark[it - 1][1])
                if dirn == "b" and ((it - 1) * 8 - 1) in evk:
                    ctx.wait_val(sync, sem_ev, evk[(it - 1) * 8 - 1])
                ctx.inc(sync.dma_start(out=ys[:], in_=ys_d.rearrange("(c p) t -> p c t", p=128)[:, :, t0:t0 + TT]), "d", 16)
                ctx.inc(sync.dma_start(out=xcs[:], in_=xcs_d.rearrange("(c p) t -> p c t", p=128)[:, :, t0:t0 + TT]), "d", 16)
                ctx.inc(sync.dma_start(out=zt[:], in_=zd.rearrange("(c p) t -> p c t", p=128)[:, :, t0:t0 + TT]), "d", 16)
                if dirn == "b":
                    ctx.inc(sync.dma_start(out=mft2[it % 2][:], in_=ctx.io["mf"].rearrange("(c p) t -> p c t", p=128)[:, :, t0:t0 + TT]), "d", 16)
            ctx.block.sync(dma_in)
            ddm2 = ctx.mark("d")
            for c in range(16):
                def gate_a(sc, c=c, it=it, ddm2=ddm2):
                    ctx.wait_val(sc, "d", ddm2)
                    ctx.wait(sc, "v")
                    ctx.inc(sc.activation(out=t2[:], in_=zt[:, c, :], func=AF.Silu), "a")
                ctx.block.scalar(gate_a)
                def gate_v(v, c=c, it=it, tbp=tbp, ddm2=ddm2):
                    ctx.wait_val(v, "d", ddm2)
                    ctx.wait(v, "a")
                    if c == 0 and (it - 2) in mm_last:
                        ctx.wait_val(v, "p", mm_last[it - 2])
                    v.scalar_tensor_tensor(t1[:], xcs[:, c, :], Dp[:, c:c + 1], ys[:, c, :], OP.mult, OP.add)
                    ctx.inc(v.tensor_tensor(ygb[tbp][:, c, :], t1[:], t2[:], OP.mult), "v")
                ctx.block.vector(gate_v)
            gate_mark[it] = (ctx.mark("v"), ctx.mark("a"))
            ddm = ctx.mark("d")
            for fo in range(8):
                j = fo % 4
                gfo = it * 8 + fo
                def mm(t, fo=fo, j=j, gfo=gfo, it=it, tbp=tbp):
                    ctx.wait_val(t, "v", gate_mark[it][0])
                    if (gfo - 4) in evk:
                        ctx.wait_val(t, sem_ev, evk[gfo - 4])
                    for ki in range(16):
                        ins = t.matmul(psK[j][:], opw_v[:, ki, fo, :], ygb[tbp][:, ki, :],
                                       start=(ki == 0), stop=(ki == 15))
                    ctx.inc(ins, "p")
                    mmk[gfo] = ctx.mark("p")
                    if fo == 7:
                        mm_last[it] = ctx.mark("p")
                ctx.block.tensor(mm)
                if dirn == "f":
                    def evac(sc, fo=fo, j=j, gfo=gfo):
                        ctx.wait_val(sc, "p", mmk[gfo])
                        if (gfo - 4) in dmk:
                            ctx.wait_val(sc, "o", dmk[gfo - 4])
                        ctx.inc(sc.copy(out=ev[:, fo % 4, :], in_=psK[j][:]), "a")
                        evk[gfo] = ctx.mark("a")
                    ctx.block.scalar(evac)
                    def dma_out(g, fo=fo, t0=t0, gfo=gfo):
                        ctx.wait_val(g, "a", evk[gfo])
                        ctx.inc(g.dma_start(out=ctx.io["mf"][fo * 128:(fo + 1) * 128, t0:t0 + TT],
                                            in_=ev[:, fo % 4, :]), "o", 16)
                        dmk[gfo] = ctx.mark("o")
                    ctx.block.gpsimd(dma_out)
                else:
                    def evac(v, fo=fo, j=j, gfo=gfo, ddm=ddm, it=it):
                        ctx.wait_val(v, "p", mmk[gfo])
                        ctx.wait_val(v, "d", ddm)
                        if (gfo - 4) in dmk:
                            ctx.wait_val(v, "o", dmk[gfo - 4])
                        ctx.inc(v.tensor_tensor(ev[:, fo % 4, :], psK[j][:], mft2[it % 2][:, fo, :], OP.add), "v")
                        evk[gfo] = ctx.mark("v")
                    ctx.block.vector(evac)
                    def dma_out(g, fo=fo, t0=t0, gfo=gfo):
                        ctx.wait_val(g, "v", evk[gfo])
                        ctx.inc(g.dma_start(out=ctx.io["mo"][fo * 128:(fo + 1) * 128, t0:t0 + TT],
                                            in_=ev[:, fo % 4, :]), "o", 16)
                        dmk[gfo] = ctx.mark("o")
                    ctx.block.gpsimd(dma_out)
        def fin(g):
            ctx.wait(g, "o")
            g.nop()
        ctx.block.gpsimd(fin)


def emit_p45a(ctx):
    nc = ctx.nc
    with ExitStack() as es:
        def sb(name, shape, dt):
            return es.enter_context(nc.sbuf_tensor("p45a_%s" % name, shape, dt))
        def psum(name, shape):
            return es.enter_context(nc.psum_tensor("p45a_%s" % name, shape, F32))
        mo = [sb("mo%d" % i, [128, 8, TT], F32) for i in range(2)]
        sq = sb("sq", [128, 8, TT], F32)
        mbf = sb("mbf", [128, 8, TT], BF16)
        hb = [sb("hb%d" % i, [128, 8, TT], BF16) for i in range(2)]
        st = sb("st", [1, 7 * TT], F32)
        tmp = sb("tmp", [128, TT], F32)
        ones_bf = sb("onesb", [128, 1], BF16)
        ones_f32 = sb("onesf", [128, 1], F32)
        ones1 = sb("ones1", [1, 128], F32)
        cns = sb("cns", [128, 4], F32)
        lng = sb("lng", [128, 8], F32)
        lnb = sb("lnb", [128, 8], F32)
        fb1 = sb("fb1", [128, 32], F32)
        w1 = sb("w1", [128, 8, 32 * 128], BF16)
        stg = sb("stg", [128, 4096], F32)
        ev = sb("ev", [128, 4, TT], BF16)
        psA = psum("psA", [1, TT]); psB = psum("psB", [1, TT])
        psR = psum("psR", [128, TT]); psM = psum("psM", [128, TT])
        psK = [psum("ps%d" % i, [128, TT]) for i in range(4)]
        def dma_par(sync):
            ctx.inc(sync.dma_start(out=lng[:], in_=ctx.io["ln_g"]), "d", 16)
            ctx.inc(sync.dma_start(out=lnb[:], in_=ctx.io["ln_b"]), "d", 16)
            ctx.inc(sync.dma_start(out=fb1[:], in_=ctx.io["ff_b1"]), "d", 16)
        ctx.block.sync(dma_par)
        def init(v):
            v.memset(ones_bf[:], 1.0)
            v.memset(ones_f32[:], 1.0)
            v.memset(cns[:, 0:1], EPS_LN)
            ctx.inc(v.memset(ones1[:], 1.0), "v")
        ctx.block.vector(init)
        load_weights_bf16(ctx, w1.ap(), ctx.io["ff_w1"], DM, 4096, stg, "w1")
        w1_v = w1.ap().rearrange("p k (f j) -> p k f j", j=128)
        gains = [lng[:, c:c + 1] for c in range(8)]
        biases = [lnb[:, c:c + 1] for c in range(8)]
        tile_vmark, tile_amark, evk, dmk, mmk = {}, {}, {}, {}, {}
        for it in range(NTILE):
            t0 = it * TT
            tb = it % 2
            def dma_in(sync, t0=t0, it=it, tb=tb):
                if it >= 2:
                    ctx.wait_val(sync, "v", tile_vmark[it - 2])
                    ctx.wait_val(sync, "a", tile_amark[it - 2])
                ctx.inc(sync.dma_start(out=mo[tb][:], in_=ctx.io["mo"].rearrange("(c p) t -> p c t", p=128)[:, :, t0:t0 + TT]), "d", 16)
            ctx.block.sync(dma_in)
            rstd, m2 = emit_ln_stats(ctx, mo[tb].ap(), 8, TT, mbf.ap(), sq.ap(), psA, psB, st.ap(),
                                     ones_bf, ones_f32, cns[0:1, 0:1])
            emit_ln_normalize(ctx, mo[tb].ap(), 8, TT, hb[tb].ap(), rstd, m2, psR, psM, ones1, tmp.ap(),
                              gain=gains, bias_=biases)
            tile_vmark[it] = ctx.mark("v")
            tile_amark[it] = ctx.mark("a")
            nvm = tile_vmark[it]
            for fo in range(32):
                j = fo % 4
                gfo = it * 32 + fo
                def mm(t, fo=fo, j=j, gfo=gfo, nvm=nvm, tb=tb):
                    ctx.wait_val(t, "v", nvm)
                    if (gfo - 4) in evk:
                        ctx.wait_val(t, "a", evk[gfo - 4])
                    for ki in range(8):
                        ins = t.matmul(psK[j][:], w1_v[:, ki, fo, :], hb[tb][:, ki, :],
                                       start=(ki == 0), stop=(ki == 7))
                    ctx.inc(ins, "p")
                    mmk[gfo] = ctx.mark("p")
                ctx.block.tensor(mm)
                def evac(sc, fo=fo, j=j, gfo=gfo):
                    ctx.wait_val(sc, "p", mmk[gfo])
                    if (gfo - 4) in dmk:
                        ctx.wait_val(sc, "o", dmk[gfo - 4])
                    ctx.inc(sc.activation(out=ev[:, fo % 4, :], in_=psK[j][:], func=AF.Gelu,
                                          bias=fb1[:, fo:fo + 1]), "a")
                    evk[gfo] = ctx.mark("a")
                ctx.block.scalar(evac)
                def dma_out(g, fo=fo, t0=t0, gfo=gfo):
                    ctx.wait_val(g, "a", evk[gfo])
                    ctx.inc(g.dma_start(out=ctx.io["g1"][fo * 128:(fo + 1) * 128, t0:t0 + TT],
                                        in_=ev[:, fo % 4, :]), "o", 16)
                    dmk[gfo] = ctx.mark("o")
                ctx.block.gpsimd(dma_out)
        def fin(g):
            ctx.wait(g, "o")
            g.nop()
        ctx.block.gpsimd(fin)


def emit_p45b(ctx):
    nc = ctx.nc
    with ExitStack() as es:
        def sb(name, shape, dt):
            return es.enter_context(nc.sbuf_tensor("p45b_%s" % name, shape, dt))
        def psum(name, shape):
            return es.enter_context(nc.psum_tensor("p45b_%s" % name, shape, F32))
        g1 = sb("g1", [128, 32, TT], BF16)
        xt = sb("xt", [128, 8, TT], F32)
        f = sb("f", [128, 8, TT], F32)
        fbf = sb("fbf", [128, 8, TT], BF16)
        sq = sb("sq", [128, 8, TT], F32)
        fn = sb("fn", [128, 8, TT], F32)
        st = sb("st", [1, 7 * TT], F32)
        tmp = sb("tmp", [128, TT], F32)
        ones_bf = sb("onesb", [128, 1], BF16)
        ones_f32 = sb("onesf", [128, 1], F32)
        ones1 = sb("ones1", [1, 128], F32)
        cns = sb("cns", [128, 4], F32)
        fb2 = sb("fb2", [128, 8], F32)
        w2 = sb("w2", [128, 32, 8 * 128], BF16)
        stg = sb("stg", [128, 1024], F32)
        psA = psum("psA", [1, TT]); psB = psum("psB", [1, TT])
        psR = psum("psR", [128, TT]); psM = psum("psM", [128, TT])
        psK = [psum("ps%d" % i, [128, TT]) for i in range(4)]
        def dma_par(sync):
            ctx.inc(sync.dma_start(out=fb2[:], in_=ctx.io["ff_b2"]), "d", 16)
        ctx.block.sync(dma_par)
        def init(v):
            v.memset(ones_bf[:], 1.0)
            v.memset(ones_f32[:], 1.0)
            v.memset(cns[:, 0:1], EPS_LN)
            ctx.inc(v.memset(ones1[:], 1.0), "v")
        ctx.block.vector(init)
        load_weights_bf16(ctx, w2.ap(), ctx.io["ff_w2"], H4, 1024, stg, "w2")
        w2_v = w2.ap().rearrange("p k (f j) -> p k f j", j=128)
        evk, mmk, mm_last, res_mark = {}, {}, {}, {}
        for it in range(NTILE):
            t0 = it * TT
            def dma_in(sync, t0=t0, it=it):
                if (it - 1) in mm_last:
                    ctx.wait_val(sync, "p", mm_last[it - 1])
                if (it - 1) in res_mark:
                    ctx.wait_val(sync, "v", res_mark[it - 1])
                ctx.inc(sync.dma_start(out=g1[:], in_=ctx.io["g1"].rearrange("(c p) t -> p c t", p=128)[:, :, t0:t0 + TT]), "d", 16)
                ctx.inc(sync.dma_start(out=xt[:], in_=ctx.io["x"].rearrange("(c p) t -> p c t", p=128)[:, :, t0:t0 + TT]), "d", 16)
            ctx.block.sync(dma_in)
            dmm = ctx.mark("d")
            avm = ctx.mark("a")
            for fo in range(8):
                j = fo % 4
                gfo = it * 8 + fo
                def mm(t, fo=fo, j=j, gfo=gfo, dmm=dmm, avm=avm, it=it):
                    ctx.wait_val(t, "d", dmm)
                    ctx.wait_val(t, "a", avm if (gfo - 4) not in evk else max(avm, evk[gfo - 4]))
                    for ki in range(32):
                        ins = t.matmul(psK[j][:], w2_v[:, ki, fo, :], g1[:, ki, :],
                                       start=(ki == 0), stop=(ki == 31))
                    ctx.inc(ins, "p")
                    mmk[gfo] = ctx.mark("p")
                    if fo == 7:
                        mm_last[it] = ctx.mark("p")
                ctx.block.tensor(mm)
                def evac(sc, fo=fo, j=j, gfo=gfo):
                    ctx.wait_val(sc, "p", mmk[gfo])
                    ctx.wait(sc, "v", "o")
                    ctx.inc(sc.activation(out=f[:, fo, :], in_=psK[j][:], func=AF.Identity,
                                          bias=fb2[:, fo:fo + 1]), "a")
                    evk[gfo] = ctx.mark("a")
                ctx.block.scalar(evac)
            rstd, m2 = emit_ln_stats(ctx, f.ap(), 8, TT, fbf.ap(), sq.ap(), psA, psB, st.ap(),
                                     ones_bf, ones_f32, cns[0:1, 0:1])
            emit_ln_normalize(ctx, f.ap(), 8, TT, fn.ap(), rstd, m2, psR, psM, ones1, tmp.ap())
            def resid(v, it=it):
                ctx.wait(v, "d", "o")
                ctx.inc(v.tensor_tensor(sq[:], fn[:], xt[:], OP.add), "v")
                res_mark[it] = ctx.mark("v")
            ctx.block.vector(resid)
            def dma_out(g, t0=t0, it=it):
                ctx.wait_val(g, "v", res_mark[it])
                ctx.inc(g.dma_start(out=ctx.io["x1"].rearrange("(c p) t -> p c t", p=128)[:, :, t0:t0 + TT], in_=sq[:]), "o", 16)
            ctx.block.gpsimd(dma_out)
        def fin(g):
            ctx.wait(g, "o")
            g.nop()
        ctx.block.gpsimd(fin)


def emit_p5(ctx):
    nc = ctx.nc
    with ExitStack() as es:
        def sb(name, shape, dt):
            return es.enter_context(nc.sbuf_tensor("p5_%s" % name, shape, dt))
        def psum(name, shape):
            return es.enter_context(nc.psum_tensor("p5_%s" % name, shape, F32))
        xa = sb("xa", [128, 8, TT], F32)
        sq = sb("sq", [128, 8, TT], F32)
        xbf = sb("xbf", [128, 8, TT], BF16)
        u = sb("u", [128, 8, TT], F32)
        upr = sb("upr", [128, 8, TT], F32)
        ucar = sb("ucar", [128, 8, 1], F32)
        xrb = sb("xrb", [128, 8, TT], BF16)
        xkb = sb("xkb", [128, 8, TT], BF16)
        rb = sb("rb", [128, 8, TT], F32)
        kab = sb("kab", [128, 8, TT], BF16)
        ob = sb("ob", [128, 8, TT], F32)
        obf = sb("obf", [128, 8, TT], BF16)
        st = sb("st", [1, 7 * TT], F32)
        tmp = sb("tmp", [128, TT], F32)
        t2 = sb("t2", [128, TT], F32)
        ones_bf = sb("onesb", [128, 1], BF16)
        ones_f32 = sb("onesf", [128, 1], F32)
        ones1 = sb("ones1", [1, 128], F32)
        cns = sb("cns", [128, 4], F32)
        mur = sb("mur", [128, 8], F32)
        muk = sb("muk", [128, 8], F32)
        wr = sb("wr", [128, 8, 8 * 128], BF16)
        wk = sb("wk", [128, 8, 8 * 128], BF16)
        wv = sb("wv", [128, 8, 8 * 128], BF16)
        stg = sb("stg", [128, 1024], F32)
        psA = psum("psA", [1, TT]); psB = psum("psB", [1, TT])
        psR = psum("psR", [128, TT]); psM = psum("psM", [128, TT])
        psK = [psum("ps%d" % i, [128, TT]) for i in range(4)]
        def dma_par(sync):
            ctx.inc(sync.dma_start(out=mur[:], in_=ctx.io["L0_ffn_mu_r"]), "d", 16)
            ctx.inc(sync.dma_start(out=muk[:], in_=ctx.io["L0_ffn_mu_k"]), "d", 16)
        ctx.block.sync(dma_par)
        def init(v):
            v.memset(ones_bf[:], 1.0)
            v.memset(ones_f32[:], 1.0)
            v.memset(cns[:, 0:1], EPS_LN)
            v.memset(ucar[:], 0.0)
            ctx.inc(v.memset(ones1[:], 1.0), "v")
        ctx.block.vector(init)
        load_weights_bf16(ctx, wr.ap(), ctx.io["WrT"], DM, 1024, stg, "wr")
        load_weights_bf16(ctx, wk.ap(), ctx.io["WkT"], DM, 1024, stg, "wk")
        load_weights_bf16(ctx, wv.ap(), ctx.io["WvT"], DM, 1024, stg, "wv")
        wr_v = wr.ap().rearrange("p k (f j) -> p k f j", j=128)
        wk_v = wk.ap().rearrange("p k (f j) -> p k f j", j=128)
        wv_v = wv.ap().rearrange("p k (f j) -> p k f j", j=128)
        for it in range(NTILE):
            t0 = it * TT
            def dma_in(sync, t0=t0):
                ctx.wait(sync, "v", "a", "p")
                ctx.inc(sync.dma_start(out=xa[:], in_=ctx.io["x1"].rearrange("(c p) t -> p c t", p=128)[:, :, t0:t0 + TT]), "d", 16)
            ctx.block.sync(dma_in)
            rstd, m2 = emit_ln_stats(ctx, xa.ap(), 8, TT, xbf.ap(), sq.ap(), psA, psB, st.ap(),
                                     ones_bf, ones_f32, cns[0:1, 0:1])
            emit_ln_normalize(ctx, xa.ap(), 8, TT, u.ap(), rstd, m2, psR, psM, ones1, tmp.ap())
            def shift_v(v):
                v.tensor_copy(upr[:, :, 1:TT], u[:, :, 0:TT - 1])
                ctx.inc(v.tensor_copy(upr[:, :, 0:1], ucar[:]), "v")
                ctx.wait(v, "v")
                ctx.inc(v.tensor_copy(ucar[:], u[:, :, TT - 1:TT]), "v")
                for c in range(8):
                    v.tensor_tensor(t2[:], u[:, c, :], upr[:, c, :], OP.subtract)
                    v.scalar_tensor_tensor(xrb[:, c, :], t2[:], mur[:, c:c + 1], upr[:, c, :], OP.mult, OP.add)
                    ins = v.scalar_tensor_tensor(xkb[:, c, :], t2[:], muk[:, c:c + 1], upr[:, c, :], OP.mult, OP.add)
                ctx.inc(ins, "v")
            ctx.block.vector(shift_v)
            xvm = ctx.mark("v")
            mmk, evk = {}, {}
            for fo in range(24):
                j = fo % 4
                grp = fo // 8
                fi = fo % 8
                def mm3(t, fo=fo, j=j, grp=grp, fi=fi, xvm=xvm):
                    ctx.wait_val(t, "v", xvm)
                    if (fo - 4) in evk:
                        s_, v_ = evk[fo - 4]
                        ctx.wait_val(t, s_, v_)
                    if grp == 2:
                        ctx.wait_val(t, "a", evk[15][1])
                    wsel = (wr_v, wk_v, wv_v)[grp]
                    xsel = (xrb, xkb, kab)[grp]
                    for ki in range(8):
                        ins = t.matmul(psK[j][:], wsel[:, ki, fi, :], xsel[:, ki, :],
                                       start=(ki == 0), stop=(ki == 7))
                    ctx.inc(ins, "p")
                    mmk[fo] = ctx.mark("p")
                ctx.block.tensor(mm3)
                if grp == 0:
                    def ev_r(sc, fo=fo, j=j, fi=fi):
                        ctx.wait_val(sc, "p", mmk[fo])
                        ctx.wait(sc, "v", "o")
                        ctx.inc(sc.activation(out=rb[:, fi, :], in_=psK[j][:], func=AF.Sigmoid), "a")
                        evk[fo] = ("a", ctx.mark("a"))
                    ctx.block.scalar(ev_r)
                elif grp == 1:
                    def ev_k(sc, fo=fo, j=j, fi=fi):
                        ctx.wait_val(sc, "p", mmk[fo])
                        sc.activation(out=t2[:], in_=psK[j][:], func=AF.Relu)
                        ctx.inc(sc.activation(out=kab[:, fi, :], in_=t2[:], func=AF.Square), "a")
                        evk[fo] = ("a", ctx.mark("a"))
                    ctx.block.scalar(ev_k)
                else:
                    def ev_v(v, fo=fo, j=j, fi=fi):
                        ctx.wait_val(v, "p", mmk[fo])
                        ctx.wait_val(v, "a", evk[fi][1])
                        ctx.inc(v.tensor_tensor(ob[:, fi, :], rb[:, fi, :], psK[j][:], OP.mult), "v")
                        evk[fo] = ("v", ctx.mark("v"))
                    ctx.block.vector(ev_v)
            rstd, m2 = emit_ln_stats(ctx, ob.ap(), 8, TT, obf.ap(), sq.ap(), psA, psB, st.ap(),
                                     ones_bf, ones_f32, cns[0:1, 0:1])
            emit_ln_normalize(ctx, ob.ap(), 8, TT, rb.ap(), rstd, m2, psR, psM, ones1, tmp.ap())
            def resid(v):
                ctx.wait(v, "o")
                ctx.inc(v.tensor_tensor(sq[:], rb[:], xa[:], OP.add), "v")
            ctx.block.vector(resid)
            def dma_out(g, t0=t0):
                ctx.wait(g, "v")
                ctx.inc(g.dma_start(out=ctx.io["x2"].rearrange("(c p) t -> p c t", p=128)[:, :, t0:t0 + TT], in_=sq[:]), "o", 16)
            ctx.block.gpsimd(dma_out)
        def fin(g):
            ctx.wait(g, "o")
            g.nop()
        ctx.block.gpsimd(fin)


def emit_p6(ctx):
    nc = ctx.nc
    with ExitStack() as es:
        def sb(name, shape, dt):
            return es.enter_context(nc.sbuf_tensor("p6_%s" % name, shape, dt))
        def psum(name, shape):
            return es.enter_context(nc.psum_tensor("p6_%s" % name, shape, F32))
        xa = sb("xa", [128, 8, TT], F32)
        xbf = sb("xbf", [128, 8, TT], BF16)
        vb = sb("vb", [128, 8, TT], BF16)
        gtf = sb("gtf", [128, 8, TT], F32)
        gti = sb("gti", [128, 8, TT], F32)
        gth = sb("gth", [128, 8, TT], F32)
        tmpB = sb("tmpB", [128, 8, TT], F32)
        st = sb("st", [1, 7 * TT], F32)
        tmp = sb("tmp", [128, TT], F32)
        msk = sb("msk", [128, 8, TT], U8)
        lft = sb("lft", [128, 8, TT], F32)
        lvt = sb("lvt", [128, 8, TT], F32)
        sq = lft
        ones_bf = sb("onesb", [128, 1], BF16)
        ones_f32 = sb("onesf", [128, 1], F32)
        ones1 = sb("ones1", [1, 128], F32)
        cns = sb("cns", [128, 4], F32)
        wt = sb("wt", [128, 8, 24 * 128], BF16)
        stg = sb("stg", [128, 1024], F32)
        psA = psum("psA", [1, TT]); psB = psum("psB", [1, TT])
        psR = psum("psR", [128, TT]); psM = psum("psM", [128, TT])
        psK = [psum("ps%d" % i, [128, TT]) for i in range(4)]
        def init(v):
            v.memset(ones_bf[:], 1.0)
            v.memset(ones_f32[:], 1.0)
            v.memset(cns[:, 0:1], EPS_LN)
            v.memset(cns[:, 1:2], 1.0)
            v.memset(cns[:, 2:3], 0.5 + EPS_L1)
            ctx.inc(v.memset(ones1[:], 1.0), "v")
        ctx.block.vector(init)
        load_weights_bf16(ctx, wt.ap(), ctx.io["L1WT"], DM, 3072, stg, "wt", mcap=1024)
        wt_v = wt.ap().rearrange("p k (f j) -> p k f j", j=128)
        one_b = cns[:, 1:2]
        half_b = cns[:, 2:3]

        def sp_chain(src_ap, out_ap, sign, s1, s2):
            def sa(sc):
                ctx.wait(sc, "v")
                sc.activation(out=s1[:], in_=src_ap, func=AF.Abs)
                sc.activation(out=s2[:], in_=s1[:], func=AF.Exp, scale=-1.0)
                sc.activation(out=s1[:], in_=s2[:], func=AF.Ln, bias=one_b)
                ctx.inc(sc.activation(out=s2[:], in_=src_ap, func=AF.Relu, scale=float(sign)), "a")
            ctx.block.scalar(sa)
            def sv(v):
                ctx.wait(v, "a")
                ctx.inc(v.tensor_tensor(out_ap, s1[:], s2[:], OP.add), "v")
            ctx.block.vector(sv)

        for it in range(NTILE):
            t0 = it * TT
            def dma_in(sync, t0=t0):
                ctx.wait(sync, "v", "a", "p")
                ctx.inc(sync.dma_start(out=xa[:], in_=ctx.io["x2"].rearrange("(c p) t -> p c t", p=128)[:, :, t0:t0 + TT]), "d", 16)
            ctx.block.sync(dma_in)
            rstd, m2 = emit_ln_stats(ctx, xa.ap(), 8, TT, xbf.ap(), sq.ap(), psA, psB, st.ap(),
                                     ones_bf, ones_f32, cns[0:1, 0:1])
            emit_ln_normalize(ctx, xa.ap(), 8, TT, vb.ap(), rstd, m2, psR, psM, ones1, tmp.ap())
            mmk, evk = {}, {}
            for fo in range(24):
                j = fo % 4
                def mm(t, fo=fo, j=j):
                    ctx.wait(t, "v")
                    if fo - 4 in evk:
                        ctx.wait_val(t, "a", evk[fo - 4])
                    for ki in range(8):
                        ins = t.matmul(psK[j][:], wt_v[:, ki, fo, :], vb[:, ki, :],
                                       start=(ki == 0), stop=(ki == 7))
                    ctx.inc(ins, "p")
                    mmk[fo] = ctx.mark("p")
                ctx.block.tensor(mm)
                def evac(sc, fo=fo, j=j):
                    ctx.wait_val(sc, "p", mmk[fo])
                    ctx.wait(sc, "v")
                    dst = (gtf, gti, gth)[fo // 8]
                    ctx.inc(sc.copy(out=dst[:, fo % 8, :], in_=psK[j][:]), "a")
                    evk[fo] = ctx.mark("a")
                ctx.block.scalar(evac)
            gvm = ctx.mark("v")
            def sp_fi(sc, gvm=gvm):
                ctx.wait_val(sc, "v", gvm)
                sc.activation(out=xa[:], in_=gtf[:], func=AF.Abs)
                sc.activation(out=tmpB[:], in_=xa[:], func=AF.Exp, scale=-1.0)
                sc.activation(out=xa[:], in_=tmpB[:], func=AF.Ln, bias=one_b)
                sc.activation(out=tmpB[:], in_=gtf[:], func=AF.Relu, scale=-1.0)
                sc.activation(out=lvt[:], in_=gti[:], func=AF.Abs)
                sc.activation(out=lft[:], in_=lvt[:], func=AF.Exp, scale=-1.0)
                sc.activation(out=lvt[:], in_=lft[:], func=AF.Ln, bias=one_b)
                ctx.inc(sc.activation(out=lft[:], in_=gti[:], func=AF.Relu, scale=-1.0), "a")
            ctx.block.scalar(sp_fi)
            def mk_diff(v):
                ctx.wait(v, "a")
                v.tensor_tensor(gtf[:], xa[:], tmpB[:], OP.add)
                v.tensor_tensor(gti[:], lvt[:], lft[:], OP.add)
                ctx.inc(v.tensor_tensor(xa[:], gtf[:], gti[:], OP.subtract), "v")
            ctx.block.vector(mk_diff)
            dm_ = ctx.mark("v")
            def sp_d(sc, dm_=dm_):
                ctx.wait_val(sc, "v", dm_)
                sc.activation(out=gtf[:], in_=xa[:], func=AF.Abs)
                sc.activation(out=tmpB[:], in_=gtf[:], func=AF.Exp, scale=-1.0)
                sc.activation(out=gtf[:], in_=tmpB[:], func=AF.Ln, bias=one_b)
                sc.activation(out=tmpB[:], in_=xa[:], func=AF.Relu)
                ctx.inc(sc.activation(out=gti[:], in_=xa[:], func=AF.Relu, scale=-1.0), "a")
            ctx.block.scalar(sp_d)
            def mk_lfli(v, t0=t0):
                ctx.wait(v, "a", "o")
                v.tensor_tensor(tmpB[:], gtf[:], tmpB[:], OP.add)
                v.tensor_tensor(gti[:], gtf[:], gti[:], OP.add)
                v.tensor_scalar(lft[:], tmpB[:], -1.0, EPS_L1, OP.mult, OP.add)
                ctx.inc(v.tensor_scalar(xa[:], gti[:], -1.0, None, OP.mult), "v")
            ctx.block.vector(mk_lfli)
            lm_ = ctx.mark("v")
            def sp_h(sc, lm_=lm_):
                ctx.wait_val(sc, "v", lm_)
                sc.activation(out=gtf[:], in_=gth[:], func=AF.Abs)
                sc.activation(out=tmpB[:], in_=gtf[:], func=AF.Exp, scale=-1.0)
                sc.activation(out=gtf[:], in_=tmpB[:], func=AF.Ln, bias=one_b)
                sc.activation(out=tmpB[:], in_=gth[:], func=AF.Relu, scale=-1.0)
                sc.activation(out=gti[:], in_=gth[:], func=AF.Relu)
                ctx.inc(sc.activation(out=lvt[:], in_=gti[:], func=AF.Ln, bias=half_b), "a")
            ctx.block.scalar(sp_h)
            def mksel(v, t0=t0):
                ctx.wait(v, "a")
                v.tensor_tensor(gti[:], gtf[:], tmpB[:], OP.add)
                ctx.inc(v.tensor_scalar(tmpB[:], gti[:], -1.0, EPS_L1, OP.mult, OP.add), "v")
                ctx.inc(v.tensor_scalar(msk[:], gth[:], 0.0, None, OP.is_ge), "v")
                ctx.wait(v, "v")
                v.select(gth[:], msk[:], lvt[:], tmpB[:], add_drain=True)
                ctx.inc(v.tensor_tensor(lvt[:], xa[:], gth[:], OP.add), "v")
            ctx.block.vector(mksel)
            def dma_out(g, t0=t0):
                ctx.wait(g, "v")
                ctx.inc(g.dma_start(out=ctx.io["lf"].rearrange("(c p) t -> p c t", p=128)[:, :, t0:t0 + TT], in_=lft[:]), "o", 16)
                ctx.inc(g.dma_start(out=ctx.io["lv"].rearrange("(c p) t -> p c t", p=128)[:, :, t0:t0 + TT], in_=lvt[:]), "o", 16)
            ctx.block.gpsimd(dma_out)
        def fin(g):
            ctx.wait(g, "o")
            g.nop()
        ctx.block.gpsimd(fin)


def emit_p7(ctx):
    nc = ctx.nc
    with ExitStack() as es:
        def sb(name, shape, dt):
            return es.enter_context(nc.sbuf_tensor("p7_%s" % name, shape, dt))
        A = sb("A", [128, SEQ], F32)
        B = sb("B", [128, SEQ], F32)
        C = sb("C", [128, SEQ], F32)
        D = sb("D", [128, SEQ], F32)
        E = sb("E", [128, SEQ], F32)
        ones = sb("ones", [128, SEQ], F32)
        zc = sb("zc", [128, 1], F32)
        nc_ = sb("ncol", [128, 1], F32)
        def init(v):
            v.memset(ones[:], 1.0)
            v.memset(zc[:], 0.0)
            ctx.inc(v.memset(nc_[:], -1e30), "v")
        ctx.block.vector(init)
        for fc in range(8):
            def dma_in(sync, fc=fc):
                ctx.wait(sync, "v", "a", "o")
                ctx.inc(sync.dma_start(out=A[:], in_=ctx.io["lf"][fc * 128:(fc + 1) * 128, :]), "d", 16)
                ctx.inc(sync.dma_start(out=B[:], in_=ctx.io["lv"][fc * 128:(fc + 1) * 128, :]), "d", 16)
            ctx.block.sync(dma_in)
            def scans(v, fc=fc):
                ctx.wait(v, "d")
                ctx.inc(v.tensor_tensor_scan(C[:], ones[:], A[:], zc[:, 0:1], OP.mult, OP.add), "v")
                ctx.wait(v, "v")
                ctx.inc(v.tensor_tensor(D[:], B[:], C[:], OP.subtract), "v")
                ctx.inc(v.tensor_tensor_scan(E[:], D[:], D[:], nc_[:, 0:1], OP.max, OP.max), "v")
                ctx.wait(v, "v")
                v.tensor_copy(B[:, 0:1], nc_[:])
                ctx.inc(v.tensor_tensor(B[:, 1:SEQ], E[:, 0:SEQ - 1], E[:, 1:SEQ], OP.subtract), "v")
                ctx.inc(v.tensor_tensor(A[:], D[:], E[:], OP.subtract), "v")
            ctx.block.vector(scans)
            def exps(sc, fc=fc):
                ctx.wait(sc, "v")
                sc.activation(out=D[:], in_=B[:], func=AF.Exp)
                ctx.inc(sc.activation(out=B[:, 0:SEQ], in_=A[:], func=AF.Exp), "a")
            ctx.block.scalar(exps)
            def scan2(v, fc=fc):
                ctx.wait(v, "a")
                ctx.inc(v.tensor_tensor_scan(A[:], D[:], B[:], zc[:, 0:1], OP.mult, OP.add), "v")
                ctx.wait(v, "v")
                ctx.inc(v.tensor_tensor(B[:], C[:], E[:], OP.add), "v")
            ctx.block.vector(scan2)
            def fin_a(sc, fc=fc):
                ctx.wait(sc, "v")
                ctx.inc(sc.activation(out=D[:], in_=A[:], func=AF.Ln), "a")
            ctx.block.scalar(fin_a)
            def fin_v(v, fc=fc):
                ctx.wait(v, "a")
                ctx.inc(v.tensor_tensor(A[:], B[:], D[:], OP.add), "v")
            ctx.block.vector(fin_v)
            def fin_a2(sc, fc=fc):
                ctx.wait(sc, "v")
                ctx.inc(sc.activation(out=E[:], in_=A[:], func=AF.Exp), "a")
            ctx.block.scalar(fin_a2)
            def dma_out(g, fc=fc):
                ctx.wait(g, "a")
                ctx.inc(g.dma_start(out=ctx.io["ht"][fc * 128:(fc + 1) * 128, :], in_=E[:]), "o", 16)
            ctx.block.gpsimd(dma_out)
        def fin(g):
            ctx.wait(g, "o")
            g.nop()
        ctx.block.gpsimd(fin)


def emit_p8a(ctx):
    nc = ctx.nc
    with ExitStack() as es:
        def sb(name, shape, dt):
            return es.enter_context(nc.sbuf_tensor("p8a_%s" % name, shape, dt))
        def psum(name, shape):
            return es.enter_context(nc.psum_tensor("p8a_%s" % name, shape, F32))
        xa = sb("xa", [128, 8, TT], F32)
        hta = sb("hta", [128, 8, TT], F32)
        sq = sb("sq", [128, 8, TT], F32)
        hbf = sb("hbf", [128, 8, TT], BF16)
        hn = sb("hn", [128, 8, TT], F32)
        x3b = sb("x3b", [128, 8, TT], F32)
        w3t = sb("w3t", [128, 8, TT], BF16)
        st = sb("st", [1, 7 * TT], F32)
        tmp = sb("tmp", [128, TT], F32)
        ones_bf = sb("onesb", [128, 1], BF16)
        ones_f32 = sb("onesf", [128, 1], F32)
        ones1 = sb("ones1", [1, 128], F32)
        cns = sb("cns", [128, 4], F32)
        psA = psum("psA", [1, TT]); psB = psum("psB", [1, TT])
        psR = psum("psR", [128, TT]); psM = psum("psM", [128, TT])
        def init(v):
            v.memset(ones_bf[:], 1.0)
            v.memset(ones_f32[:], 1.0)
            v.memset(cns[:, 0:1], EPS_LN)
            ctx.inc(v.memset(ones1[:], 1.0), "v")
        ctx.block.vector(init)
        for it in range(NTILE):
            t0 = it * TT
            def dma_in(sync, t0=t0):
                ctx.wait(sync, "v", "a", "p")
                ctx.inc(sync.dma_start(out=xa[:], in_=ctx.io["x2"].rearrange("(c p) t -> p c t", p=128)[:, :, t0:t0 + TT]), "d", 16)
                ctx.inc(sync.dma_start(out=hta[:], in_=ctx.io["ht"].rearrange("(c p) t -> p c t", p=128)[:, :, t0:t0 + TT]), "d", 16)
            ctx.block.sync(dma_in)
            rstd, m2 = emit_ln_stats(ctx, hta.ap(), 8, TT, hbf.ap(), sq.ap(), psA, psB, st.ap(),
                                     ones_bf, ones_f32, cns[0:1, 0:1])
            emit_ln_normalize(ctx, hta.ap(), 8, TT, hn.ap(), rstd, m2, psR, psM, ones1, tmp.ap())
            def resid(v):
                ctx.wait(v, "o")
                ctx.inc(v.tensor_tensor(x3b[:], hn[:], xa[:], OP.add), "v")
            ctx.block.vector(resid)
            def dma_x3(g, t0=t0):
                ctx.wait(g, "v")
                ctx.inc(g.dma_start(out=ctx.io["x3"].rearrange("(c p) t -> p c t", p=128)[:, :, t0:t0 + TT], in_=x3b[:]), "o", 16)
            ctx.block.gpsimd(dma_x3)
            rstd, m2 = emit_ln_stats(ctx, x3b.ap(), 8, TT, hbf.ap(), sq.ap(), psA, psB, st.ap(),
                                     ones_bf, ones_f32, cns[0:1, 0:1])
            emit_ln_normalize(ctx, x3b.ap(), 8, TT, w3t.ap(), rstd, m2, psR, psM, ones1, tmp.ap())
            def dma_w3(g, t0=t0):
                ctx.wait(g, "v")
                ctx.inc(g.dma_start(out=ctx.io["w3"].rearrange("(c p) t -> p c t", p=128)[:, :, t0:t0 + TT], in_=w3t[:]), "o", 16)
            ctx.block.gpsimd(dma_w3)
        def fin(g):
            ctx.wait(g, "o")
            g.nop()
        ctx.block.gpsimd(fin)


def emit_p8b1(ctx):
    nc = ctx.nc
    with ExitStack() as es:
        def sb(name, shape, dt):
            return es.enter_context(nc.sbuf_tensor("p8b1_%s" % name, shape, dt))
        def psum(name, shape):
            return es.enter_context(nc.psum_tensor("p8b1_%s" % name, shape, F32))
        w3t = sb("w3t", [128, 8, TT], BF16)
        tg = sb("tg", [128, TT], F32)
        hht = sb("hht", [128, 24, TT], BF16)
        wg = sb("wg", [128, 8, 24 * 128], BF16)
        wu = sb("wu", [128, 8, 24 * 128], BF16)
        stg = sb("stg", [128, 1024], I32)
        psK = [psum("ps%d" % i, [128, TT]) for i in range(4)]
        load_weights_bf16(ctx, wg.ap(), ctx.io["wgT"], DM, 3072, stg, "wg", mcap=1024)
        load_weights_bf16(ctx, wu.ap(), ctx.io["wuT"], DM, 3072, stg, "wu", mcap=1024)
        wg_v = wg.ap().rearrange("p k (f j) -> p k f j", j=128)
        wu_v = wu.ap().rearrange("p k (f j) -> p k f j", j=128)
        SC = 1.0 / 127.0
        evk, mmk, muk = {}, {}, {}
        for it in range(NTILE):
            t0 = it * TT
            def dma_in(sync, t0=t0):
                ctx.wait(sync, "v", "a", "p")
                ctx.inc(sync.dma_start(out=w3t[:], in_=ctx.io["w3"].rearrange("(c p) t -> p c t", p=128)[:, :, t0:t0 + TT]), "d", 16)
            ctx.block.sync(dma_in)
            dmm = ctx.mark("d")
            for fo in range(24):
                jg, ju = (fo % 2) * 2, (fo % 2) * 2 + 1
                gfo = it * 24 + fo
                def mm(t, fo=fo, jg=jg, ju=ju, gfo=gfo, dmm=dmm):
                    ctx.wait_val(t, "d", dmm)
                    if (gfo - 2) in evk:
                        ctx.wait_val(t, "a", evk[gfo - 2])
                    if (gfo - 2) in muk:
                        ctx.wait_val(t, "v", muk[gfo - 2])
                    for ki in range(8):
                        ins = t.matmul(psK[jg][:], wg_v[:, ki, fo, :], w3t[:, ki, :],
                                       start=(ki == 0), stop=(ki == 7))
                    for ki in range(8):
                        ins = t.matmul(psK[ju][:], wu_v[:, ki, fo, :], w3t[:, ki, :],
                                       start=(ki == 0), stop=(ki == 7))
                    ctx.inc(ins, "p")
                    mmk[gfo] = ctx.mark("p")
                ctx.block.tensor(mm)
                def act_g(sc, fo=fo, jg=jg, gfo=gfo):
                    ctx.wait_val(sc, "p", mmk[gfo])
                    if (gfo - 1) in muk:
                        ctx.wait_val(sc, "v", muk[gfo - 1])
                    ctx.inc(sc.activation(out=tg[:], in_=psK[jg][:], func=AF.Silu, scale=SC), "a")
                    evk[gfo] = ctx.mark("a")
                ctx.block.scalar(act_g)
                def mul_u(v, fo=fo, ju=ju, gfo=gfo):
                    ctx.wait_val(v, "a", evk[gfo])
                    ctx.wait(v, "o")
                    ctx.inc(v.scalar_tensor_tensor(hht[:, fo, :], psK[ju][:], SC, tg[:], OP.mult, OP.mult), "v")
                    muk[gfo] = ctx.mark("v")
                ctx.block.vector(mul_u)
            def dma_out(g, t0=t0):
                ctx.wait(g, "v")
                ctx.inc(g.dma_start(out=ctx.io["hh"].rearrange("(c p) t -> p c t", p=128)[:, :, t0:t0 + TT], in_=hht[:]), "o", 16)
            ctx.block.gpsimd(dma_out)
        def fin(g):
            ctx.wait(g, "o")
            g.nop()
        ctx.block.gpsimd(fin)


def emit_p8b2(ctx):
    nc = ctx.nc
    with ExitStack() as es:
        def sb(name, shape, dt):
            return es.enter_context(nc.sbuf_tensor("p8b2_%s" % name, shape, dt))
        def psum(name, shape):
            return es.enter_context(nc.psum_tensor("p8b2_%s" % name, shape, F32))
        hht = [sb("hht%d" % i, [128, 24, TT], BF16) for i in range(2)]
        x3t = sb("x3t", [128, 8, TT], F32)
        f = sb("f", [128, 8, TT], F32)
        fbf = sb("fbf", [128, 8, TT], BF16)
        sq = sb("sq", [128, 8, TT], F32)
        fn = sb("fn", [128, 8, TT], F32)
        st = sb("st", [1, 7 * TT], F32)
        tmp = sb("tmp", [128, TT], F32)
        ones_bf = sb("onesb", [128, 1], BF16)
        ones_f32 = sb("onesf", [128, 1], F32)
        ones1 = sb("ones1", [1, 128], F32)
        cns = sb("cns", [128, 4], F32)
        wd = sb("wd", [128, 24, 8 * 128], BF16)
        stg = sb("stg", [128, 1024], I32)
        psA = psum("psA", [1, TT]); psB = psum("psB", [1, TT])
        psR = psum("psR", [128, TT]); psM = psum("psM", [128, TT])
        psK = [psum("ps%d" % i, [128, TT]) for i in range(4)]
        def init(v):
            v.memset(ones_bf[:], 1.0)
            v.memset(ones_f32[:], 1.0)
            v.memset(cns[:, 0:1], EPS_LN)
            ctx.inc(v.memset(ones1[:], 1.0), "v")
        ctx.block.vector(init)
        load_weights_bf16(ctx, wd.ap(), ctx.io["wdT"], H3, 1024, stg, "wd")
        wd_v = wd.ap().rearrange("p k (f j) -> p k f j", j=128)
        evk, mmk, mm_last8, res_mark8 = {}, {}, {}, {}
        for it in range(NTILE):
            t0 = it * TT
            def dma_in(sync, t0=t0, it=it):
                if (it - 2) in mm_last8:
                    ctx.wait_val(sync, "p", mm_last8[it - 2])
                if (it - 1) in res_mark8:
                    ctx.wait_val(sync, "v", res_mark8[it - 1])
                ctx.inc(sync.dma_start(out=hht[it % 2][:], in_=ctx.io["hh"].rearrange("(c p) t -> p c t", p=128)[:, :, t0:t0 + TT]), "d", 16)
                ctx.inc(sync.dma_start(out=x3t[:], in_=ctx.io["x3"].rearrange("(c p) t -> p c t", p=128)[:, :, t0:t0 + TT]), "d", 16)
            ctx.block.sync(dma_in)
            dmm = ctx.mark("d")
            avm = ctx.mark("a")
            for fo in range(8):
                j = fo % 4
                gfo = it * 8 + fo
                def mm(t, fo=fo, j=j, gfo=gfo, dmm=dmm, avm=avm, it=it):
                    ctx.wait_val(t, "d", dmm)
                    ctx.wait_val(t, "a", avm if (gfo - 4) not in evk else max(avm, evk[gfo - 4]))
                    for ki in range(24):
                        ins = t.matmul(psK[j][:], wd_v[:, ki, fo, :], hht[it % 2][:, ki, :],
                                       start=(ki == 0), stop=(ki == 23))
                    ctx.inc(ins, "p")
                    mmk[gfo] = ctx.mark("p")
                    if fo == 7:
                        mm_last8[it] = ctx.mark("p")
                ctx.block.tensor(mm)
                def evac(sc, fo=fo, j=j, gfo=gfo):
                    ctx.wait_val(sc, "p", mmk[gfo])
                    ctx.wait(sc, "v", "o")
                    ctx.inc(sc.copy(out=f[:, fo, :], in_=psK[j][:]), "a")
                    evk[gfo] = ctx.mark("a")
                ctx.block.scalar(evac)
            rstd, m2 = emit_ln_stats(ctx, f.ap(), 8, TT, fbf.ap(), sq.ap(), psA, psB, st.ap(),
                                     ones_bf, ones_f32, cns[0:1, 0:1])
            emit_ln_normalize(ctx, f.ap(), 8, TT, fn.ap(), rstd, m2, psR, psM, ones1, tmp.ap())
            def resid(v, it=it):
                ctx.wait(v, "d", "o")
                ctx.inc(v.tensor_tensor(sq[:], fn[:], x3t[:], OP.add), "v")
                res_mark8[it] = ctx.mark("v")
            ctx.block.vector(resid)
            def dma_out(g, t0=t0, it=it):
                ctx.wait_val(g, "v", res_mark8[it])
                ctx.inc(g.dma_start(out=ctx.io["y"].rearrange("(c p) t -> p c t", p=128)[:, :, t0:t0 + TT], in_=sq[:]), "o", 16)
            ctx.block.gpsimd(dma_out)
        def fin(g):
            ctx.wait(g, "o")
            g.nop()
        ctx.block.gpsimd(fin)


PHASE_EMIT = {
    "p1": emit_p1,
    "p2f": lambda ctx: emit_p2(ctx, "f"), "p2b": lambda ctx: emit_p2(ctx, "b"),
    "p3f": lambda ctx: emit_p3(ctx, "f"), "p3b": lambda ctx: emit_p3(ctx, "b"),
    "p4f": lambda ctx: emit_p4(ctx, "f"), "p4b": lambda ctx: emit_p4(ctx, "b"),
    "p45a": emit_p45a, "p45b": emit_p45b,
    "p5": emit_p5, "p6": emit_p6, "p7": emit_p7,
    "p8a": emit_p8a, "p8b1": emit_p8b1, "p8b2": emit_p8b2,
}


def build_phase_nc(phase, core_input_names, fused=False):
    nc = bass.Bass("TRN2", target_bir_lowering=False)
    io = {}
    phases = phase if isinstance(phase, list) else [phase]
    wts, sins, souts = set(), set(), set()
    for ph in phases:
        w, si, so = PHASE_IO[ph]
        wts |= set(w); sins |= set(si) - souts; souts |= set(so)
    for w in sorted(wts):
        shape, dt = SHARED_SPECS[w]
        shape = (shape[0], 1) if len(shape) == 1 else shape
        io[w] = nc.dram_tensor(w, list(shape), dt, kind="ExternalInput").ap()
    if fused:
        io["x"] = nc.dram_tensor("x", [DM, SEQ], F32, kind="ExternalInput").ap()
        io["y"] = nc.dram_tensor("y", [DM, SEQ], F32, kind="ExternalOutput").ap()
        for s in sorted((sins | souts) - {"x", "y"}):
            shape, dt = SCRATCH_SPECS[s]
            io[s] = nc.dram_tensor(s, list(shape), dt, kind="Internal").ap()
    else:
        for s in sorted(sins):
            if s == "x":
                io["x"] = nc.dram_tensor("x", [DM, SEQ], F32, kind="ExternalInput").ap()
            else:
                shape, dt = SCRATCH_SPECS[s]
                io[s] = nc.dram_tensor(s, list(shape), dt, kind="ExternalInput").ap()
        for s in sorted(souts):
            if s == "y":
                io["y"] = nc.dram_tensor("y", [DM, SEQ], F32, kind="ExternalOutput").ap()
            else:
                shape, dt = SCRATCH_SPECS[s]
                io[s] = nc.dram_tensor(s, list(shape), dt, kind="ExternalOutput").ap()
    with ExitStack() as stack:
        sems = {k: stack.enter_context(nc.semaphore("s_" + k)) for k in SEM_NAMES}
        block = stack.enter_context(nc.Block())
        ctx = Ctx(nc, block, sems, io)
        for i, ph in enumerate(phases):
            if i > 0:
                ctx.barrier()
            PHASE_EMIT[ph](ctx)
    return nc, sorted(wts), sorted(sins), sorted(souts)


PHASES = ["p1", "p2f", "p2b", "p3f", "p3b", "p4f", "p4b", "p45a", "p45b", "p5", "p6", "p7", "p8a", "p8b1", "p8b2"]


def run_phases(x_cores, shared):
    ncore = len(x_cores)
    state = [{"x": x_cores[i]} for i in range(ncore)]
    for ph in PHASES:
        nc, wts, sins, souts = build_phase_nc(ph, None)
        in_maps = []
        for i in range(ncore):
            m = {}
            for w in wts:
                a = shared[w]
                m[w] = a.reshape(-1, 1) if a.ndim == 1 else a
            for s in sins:
                m[s] = state[i][s]
            in_maps.append(m)
        res = run_bass_kernel_spmd(nc, in_maps, core_ids=list(range(ncore)))
        for i in range(ncore):
            for s in souts:
                state[i][s] = np.asarray(res.results[i][s])
        later = set()
        for ph2 in PHASES[PHASES.index(ph) + 1:]:
            later |= set(PHASE_IO[ph2][1])
        for i in range(ncore):
            for s in [k for k in state[i] if k not in later and k != "y"]:
                del state[i][s]
    return state


def run_fused(x_cores, shared):
    nc, wts, sins, souts = build_phase_nc(PHASES, None, fused=True)
    in_maps = []
    for i in range(len(x_cores)):
        m = {}
        for w in wts:
            a = shared[w]
            m[w] = np.ascontiguousarray(a.reshape(-1, 1)) if a.ndim == 1 else a
        m["x"] = x_cores[i]
        in_maps.append(m)
    res = run_bass_kernel_spmd(nc, in_maps, core_ids=list(range(len(x_cores))))
    return [np.asarray(r["y"]) for r in res.results]


def kernel(**inputs):
    shared = _prep_shared(inputs)
    x = inputs["L0_attn_x"]
    x_cores = [np.ascontiguousarray(x[b].T) for b in range(8)]
    if FUSED:
        ys = run_fused(x_cores, shared)
    else:
        state = run_phases(x_cores, shared)
        ys = [state[b]["y"] for b in range(8)]
    out = np.stack([y.T for y in ys], axis=0)
    return np.ascontiguousarray(out)
